# version 02
# baseline (speedup 1.0000x reference)
import numpy as np
import sys
sys.path.insert(0, "/opt/trn_rl_repo")
import concourse.bass as bass
import concourse.mybir as mybir
from concourse.bass_utils import run_bass_kernel_spmd
from contextlib import ExitStack

F32 = mybir.dt.float32
BF16 = mybir.dt.bfloat16
I32 = mybir.dt.int32
AF = mybir.ActivationFunctionType
ALU = mybir.AluOpType
AX = mybir.AxisListType

P = 128
DM = 2048
DMC = DM // P
DI = 4096
DIH = DI // 2
DIHC = DIH // P
T = 1024
TT = 2048
N = 16
E_EXP = 8
F_MOE = 2048
FC_MOE = F_MOE // P
R_LR = 64
NL = 2
H = 16
HH = 8
DK = 64
DV = 128
M_ABC = 64
HID = 5632
HIDC = HID // P
GOLDEN_CENTER = 1.0 / np.e
GOLDEN_LOWER = 0.5 - np.log(4.0 / 3.0)
GOLDEN_UPPER = 0.5
EPS_LN = 1e-5
EPS_RMS = 1e-5

DEBUG_COPY = []
LAST_RESULTS = None



def build_program():
    nc = bass.Bass("TRN2", target_bir_lowering=False)
    dram = {}

    def din(name, shape, dt=F32):
        dram[name] = nc.dram_tensor(name, shape, dt, kind="ExternalInput").ap()

    def dout(name, shape, dt=F32):
        dram[name] = nc.dram_tensor(name, shape, dt, kind="ExternalOutput").ap()

    def dint(name, shape, dt=F32, shared=False):
        if shared:
            dram[name] = nc.dram_tensor(name, shape, dt, addr_space="Shared").ap()
        else:
            dram[name] = nc.dram_tensor(name, shape, dt).ap()

    din("x_own", [T, DM])
    din("w_inT", [DM, 2 * DI])
    din("conv_w_loc", [P, DIHC * 4])
    din("conv_b_loc", [P, DIHC])
    din("w_xpT_loc", [DIH, 160])
    din("w_dtT_loc", [128, DIH])
    din("dtb_loc", [P, DIHC])
    din("alog_lanes", [P, DIHC * N])
    din("d_loc", [P, DIHC])
    din("w_outT", [DI, DM])
    din("moe_wg", [E_EXP * DM, F_MOE])
    din("moe_wu", [E_EXP * DM, F_MOE])
    din("moe_wd", [E_EXP * F_MOE, DM])
    din("moe_ga", [NL * DM, R_LR])
    din("moe_gb", [NL * R_LR, F_MOE])
    din("moe_ua", [NL * DM, R_LR])
    din("moe_ub", [NL * R_LR, F_MOE])
    din("moe_da", [NL * F_MOE, R_LR])
    din("moe_db", [NL * R_LR, DM])
    din("wr", [DM, E_EXP])
    din("temp", [1, 1])
    din("wq1", [DM, H * DK])
    din("wk1", [DM, H * DK])
    din("wv1", [DM, H * DV])
    din("wg1", [DM, H * DV])
    din("ws1", [DM, H * M_ABC])
    din("wo1", [H * DV, DM])
    din("gnw", [P, 1])
    din("qg_T", [DM, HID], I32)
    din("qu_T", [DM, HID], I32)
    din("qd_T", [HID, DM], I32)
    din("sg_cols", [P, HIDC])
    din("su_cols", [P, HIDC])
    din("sd_cols", [P, DMC])
    din("id_f32", [P, P])
    din("id_bf16", [P, P], BF16)
    din("tri_bf16", [P, P], BF16)
    din("tri_scaled", [P, P], BF16)
    din("selrep", [P, N * P], BF16)
    din("selred", [P, N * P], BF16)
    din("sB", [32, P])
    din("sC", [32, P])
    din("ones_c_bf", [P, 1], BF16)
    din("ones_c_f32", [P, 1])
    din("ones_r_bf", [1, P], BF16)
    din("ones_r_f32", [1, P])
    din("m128_r", [1, P], BF16)
    din("role_in", [1, 1], I32)
    din("magic", [1, 40], I32)
    din("magic_cmp", [1, 8], I32)

    dout("y", [T, DM])

    dint("xc_dram", [DI, T], BF16)
    dint("sz_dram", [DI, T], BF16)
    dint("xc_scan", [DIH, TT], BF16)
    dint("sz_scan", [DIH, TT], BF16)
    dint("xt_scan", [DIH, TT], BF16)
    dint("ya_loc", [DIH, TT], BF16)
    dint("dt_dbg", [DIH, TT], BF16)
    dint("scan_dbg", [P, 3072], F32)
    dint("ya_own", [DI, T], BF16)
    dint("bc_dram", [32, TT], F32)
    dint("attn_loc", [7168, T], BF16)
    dint("attn_q", [HH * DK, TT], BF16)
    dint("attn_k", [HH * DK, TT], BF16)
    dint("attn_s", [HH * M_ABC, TT], BF16)
    dint("attn_v", [HH * DV, TT], BF16)
    dint("attn_g", [HH * DV, TT], BF16)
    dint("o_own", [H * DV, T], BF16)
    for ab in "ab":
        dint(f"sh_e1_{ab}", [DI, T], BF16, shared=True)
        dint(f"sh_e2_{ab}", [160, TT], F32, shared=True)
        dint(f"sh_e3_{ab}", [DIH, T], BF16, shared=True)
        dint(f"sh_e4_{ab}", [3584, T], BF16, shared=True)
        dint(f"sh_e5_{ab}", [HH * DV, T], BF16, shared=True)
        for k in range(5):
            dint(f"fl{k}_{ab}", [1, 8], I32, shared=True)
    for nm, outn in DEBUG_COPY:
        shp = {"xc_dram": [DI, T], "sz_dram": [DI, T], "xc_scan": [DIH, TT],
               "sz_scan": [DIH, TT], "xt_scan": [DIH, TT], "ya_loc": [DIH, TT],
               "ya_own": [DI, T], "attn_loc": [7168, T], "o_own": [H * DV, T],
               "dt_dbg": [DIH, TT], "scan_dbg": [P, 3072],
               "attn_q": [HH * DK, TT], "attn_s": [HH * M_ABC, TT]}[nm]
        dout(outn, shp, F32 if nm == "scan_dbg" else BF16)

    with ExitStack() as es:
        def sb(name, shape, dt=F32):
            return es.enter_context(nc.sbuf_tensor(name, shape, dt))

        def psum(name, shape, dt=F32):
            return es.enter_context(nc.psum_tensor(name, shape, dt))

        s_xacc = sb("s_xacc", [P, DMC * T])
        s_xn = sb("s_xn", [P, DMC * T], BF16)
        s_mid = sb("s_mid", [P, FC_MOE * T], BF16)
        s_big0 = sb("s_big0", [P, 2048])
        s_big1 = sb("s_big1", [P, 2304])
        s_big2 = sb("s_big2", [P, 2048])
        s_big3 = sb("s_big3", [P, 2048])
        s_big4 = sb("s_big4", [P, 2048])
        s_wbuf0 = sb("s_wbuf0", [P, DM], BF16)
        s_wbuf1 = sb("s_wbuf1", [P, DM], BF16)
        s_wbig = sb("s_wbig", [P, 6144], BF16)
        s_lr = sb("s_lr", [P, DM], BF16)
        s_rows = sb("s_rows", [1, 2048])
        s_wr = sb("s_wr", [P, P])
        s_hcarry = sb("s_hcarry", [P, DIHC * N])
        s_idf = sb("s_idf", [P, P])
        s_idb = sb("s_idb", [P, P], BF16)
        s_tri = sb("s_tri", [P, P], BF16)
        s_sB = sb("s_sB", [32, P])
        s_sC = sb("s_sC", [32, P])
        s_ones_c_bf = sb("s_ones_c_bf", [P, 1], BF16)
        s_ones_c_f = sb("s_ones_c_f", [P, 1])
        s_ones_r_bf = sb("s_ones_r_bf", [1, P], BF16)
        s_ones_r_f = sb("s_ones_r_f", [1, P])
        s_m128 = sb("s_m128", [1, P], BF16)
        s_role = sb("s_role", [1, 1], I32)
        s_magic = sb("s_magic", [1, 40], I32)
        s_mcmp = sb("s_mcmp", [1, 8], I32)
        s_gnw = sb("s_gnw", [P, 1])
        s_temp = sb("s_temp", [1, 1])
        s_rtemp = sb("s_rtemp", [P, 2])
        s_ceps = s_rtemp[:, 1:2]
        s_convw = sb("s_convw", [P, DIHC * 4])
        s_convb = sb("s_convb", [P, DIHC])
        s_dtb = sb("s_dtb", [P, DIHC])
        s_alanes = sb("s_alanes", [P, DIHC * N])
        s_dloc = sb("s_dloc", [P, DIHC])
        s_sgc = sb("s_sgc", [P, HIDC])
        s_suc = sb("s_suc", [P, HIDC])
        s_sdc = sb("s_sdc", [P, DMC])
        s_sdm128 = sb("s_sdm128", [P, DMC])
        s_flag = sb("s_flag", [1, 8], I32)

        s_logits = s_lr.bitcast(F32)[64:64 + E_EXP, 0:T]

        row0 = s_rows[0:1, :]
        rowb0 = s_rows.bitcast(BF16)[0:1, 2048:3072]
        rowb1 = s_rows.bitcast(BF16)[0:1, 3072:4096]
        s_crow = s_rows.bitcast(BF16)[0:1, 0:64]
        s_wfm = s_big3[0:8, 1024:2048]

        PS = [psum(f"PS{i}", [P, 512]) for i in range(8)]
        PSB = [p.bitcast(BF16) for p in PS]

        qg = es.enter_context(nc.semaphore("qg"))
        qs = es.enter_context(nc.semaphore("qs"))
        qc = es.enter_context(nc.semaphore("qc"))
        tS = es.enter_context(nc.semaphore("tS"))
        vS = es.enter_context(nc.semaphore("vS"))
        sS = es.enter_context(nc.semaphore("sS"))
        spin = es.enter_context(nc.semaphore("spin"))
        SEMS = {"qg": qg, "qs": qs, "qc": qc, "tS": tS, "vS": vS, "sS": sS}
        ENGSEM = {"tensor": "tS", "vector": "vS", "scalar": "sS"}
        C = {k: 0 for k in SEMS}
        chain = [None]

        block = es.enter_context(nc.Block())

        _SYNCED_OPS = {
            "tensor_tensor", "tensor_add", "tensor_sub", "tensor_mul", "tensor_max",
            "tensor_copy", "tensor_scalar", "tensor_single_scalar",
            "tensor_scalar_add", "tensor_scalar_sub", "tensor_scalar_mul",
            "tensor_scalar_max", "tensor_scalar_min", "tensor_reduce",
            "reduce_sum", "reduce_max", "reciprocal", "memset", "memzero",
            "tensor_tensor_scan", "scalar_tensor_tensor", "select",
            "activation", "copy", "add", "mul", "sqrt", "square", "sign",
        }

        class _SeqProxy:
            def __init__(self, E, sem, base):
                self._E = E
                self._sem = sem
                self._n = [base]

            def __getattr__(self, name):
                fnb = getattr(self._E, name)
                if name in _SYNCED_OPS:
                    def wrapped(*a, **k):
                        if self._n[0] > 0:
                            self._E.wait_ge(self._sem, self._n[0])
                        inst = fnb(*a, **k)
                        inst.then_inc(self._sem, 1)
                        self._n[0] += 1
                        return inst
                    return wrapped
                return fnb

        def step(eng, fn, ndma=None):
            prev = chain[0]
            if eng in ("sync", "gpsimd_dma", "scalar_dma"):
                qname = {"sync": "qs", "gpsimd_dma": "qg", "scalar_dma": "qc"}[eng]
                qsem = SEMS[qname]
                blkfn = {"sync": block.sync, "gpsimd_dma": block.gpsimd,
                         "scalar_dma": block.scalar}[eng]

                def body(E):
                    if prev is not None:
                        E.wait_ge(SEMS[prev[0]], prev[1])

                    def dma(dst, src):
                        E.dma_start(out=dst, in_=src).then_inc(qsem, 16)
                    fn(E, dma)
                blkfn(body)
                C[qname] += 16 * ndma
                chain[0] = (qname, C[qname])
            elif eng == "tensor":
                semn = ENGSEM[eng]

                def body(E):
                    if prev is not None:
                        E.wait_ge(SEMS[prev[0]], prev[1])
                    last = fn(E)
                    last.then_inc(SEMS[semn], 1)
                getattr(block, eng)(body)
                C[semn] += 1
                chain[0] = (semn, C[semn])
            else:
                semn = ENGSEM[eng]
                base = C[semn]
                holder = [None]

                def body(E):
                    if prev is not None:
                        E.wait_ge(SEMS[prev[0]], prev[1])
                    px = _SeqProxy(E, SEMS[semn], base)
                    fn(px)
                    holder[0] = px._n[0]
                getattr(block, eng)(body)
                assert holder[0] is not None and holder[0] > base, "empty group"
                C[semn] = holder[0]
                chain[0] = (semn, C[semn])

        xn3 = s_xn.rearrange("p (dc t) -> p dc t", t=T)
        smf = s_mid.bitcast(F32)
        smb = s_mid

        const_pairs = [
            (s_idf, "id_f32"), (s_idb, "id_bf16"), (s_tri, "tri_bf16"),
            (s_sB, "sB"), (s_sC, "sC"),
            (s_ones_c_bf, "ones_c_bf"), (s_ones_c_f, "ones_c_f32"),
            (s_ones_r_bf, "ones_r_bf"), (s_ones_r_f, "ones_r_f32"),
            (s_m128, "m128_r"), (s_role, "role_in"), (s_magic, "magic"),
            (s_mcmp, "magic_cmp"), (s_gnw, "gnw"), (s_temp, "temp"),
            (s_convw, "conv_w_loc"), (s_convb, "conv_b_loc"), (s_dtb, "dtb_loc"),
            (s_alanes, "alog_lanes"), (s_dloc, "d_loc"), (s_sgc, "sg_cols"),
            (s_suc, "su_cols"), (s_sdc, "sd_cols"), (s_wr, None),
        ]

        def ph0(E, dma):
            for dst, nm in const_pairs:
                if nm is None:
                    continue
                dma(dst[:], dram[nm])
            dma(s_wr.rearrange("p (dc e) -> p dc e", e=E_EXP)[:, 0:DMC, :],
                dram["wr"].rearrange("(dc p) e -> p dc e", p=P))
        step("gpsimd_dma", ph0, ndma=len(const_pairs))

        step("vector", lambda v: v.memset(s_ceps, EPS_LN))
        step("scalar", lambda s: s.activation(s_alanes[:], s_alanes[:], AF.Exp))
        step("vector", lambda v: v.tensor_scalar_mul(s_alanes[:], s_alanes[:], -1.0))
        step("vector", lambda v: v.tensor_scalar_mul(s_sdm128[:], s_sdc[:], -128.0))
        step("vector", lambda v: v.reciprocal(s_rtemp[0:1, 0:1], s_temp[0:1, 0:1]))
        step("tensor", lambda t: t.matmul(PS[0][:, 0:1], s_ones_r_f[:], s_rtemp[0:1, 0:1],
                                          start=True, stop=True))
        step("vector", lambda v: v.tensor_copy(s_rtemp[:, 0:1], PS[0][:, 0:1]))

        NT = T // P
        for i in range(NT):
            step("sync", lambda E, dma, i=i:
                 dma(s_big0[:, 0:DM], dram["x_own"][i * P:(i + 1) * P, :]), ndma=1)
            step("vector", lambda v: v.reduce_sum(s_big3[:, 0:1], s_big0[:, 0:DM], AX.X))
            step("scalar", lambda s: s.activation(s_big2[:, 0:DM], s_big0[:, 0:DM], AF.Square))

            def stats2(v):
                v.reduce_sum(s_big3[:, 1:2], s_big2[:, 0:DM], AX.X)
                v.tensor_scalar_mul(s_big3[:, 2:3], s_big3[:, 0:1], -1.0 / DM)
                v.tensor_scalar_mul(s_big3[:, 3:4], s_big3[:, 1:2], 1.0 / DM)
                v.tensor_mul(s_big3[:, 4:5], s_big3[:, 2:3], s_big3[:, 2:3])
                return v.tensor_sub(s_big3[:, 5:6], s_big3[:, 3:4], s_big3[:, 4:5])
            step("vector", stats2)

            def rstd(s):
                s.activation(s_big3[:, 6:7], s_big3[:, 5:6], AF.Ln, bias=s_ceps)
                return s.activation(s_big3[:, 7:8], s_big3[:, 6:7], AF.Exp, scale=-0.5)
            step("scalar", rstd)
            step("vector", lambda v: v.tensor_scalar(s_big1[:, 0:DM], s_big0[:, 0:DM],
                                                     s_big3[:, 2:3], s_big3[:, 7:8],
                                                     ALU.add, ALU.mult))
            for dc in range(DMC):
                par = dc % 2

                def tr(t, dc=dc, par=par):
                    t.matmul(PS[par][:, 0:P], s_big0[:, dc * P:(dc + 1) * P], s_idf[:],
                             is_transpose=True, start=True, stop=True)
                    return t.matmul(PS[2 + par][:, 0:P], s_big1[:, dc * P:(dc + 1) * P],
                                    s_idf[:], is_transpose=True, start=True, stop=True)
                step("tensor", tr)

                def ev(v, dc=dc, par=par, i=i):
                    v.tensor_copy(s_xacc[:, dc * T + i * P: dc * T + (i + 1) * P], PS[par][:, 0:P])
                    v.tensor_copy(xn3[:, dc, i * P:(i + 1) * P], PS[2 + par][:, 0:P])
                    return v.tensor_copy(s_big2[:, dc * P:(dc + 1) * P], PS[2 + par][:, 0:P])
                step("vector", ev)

            def router_mm(t, i=i):
                for dc in range(DMC):
                    last = t.matmul(PS[4][0:E_EXP, 0:P],
                                    s_wr.rearrange("p (dc e) -> p dc e", e=E_EXP)[:, dc, :],
                                    s_big2[:, dc * P:(dc + 1) * P],
                                    start=(dc == 0), stop=(dc == DMC - 1))
                return last
            step("tensor", router_mm)
            step("vector", lambda v, i=i: v.tensor_copy(s_logits[:, i * P:(i + 1) * P],
                                                        PS[4][0:E_EXP, 0:P]))

        for fc in range(64):
            def wload(E, dma, fc=fc):
                dma(s_wbuf0.rearrange("p (dc j) -> p dc j", j=P),
                    dram["w_inT"][:, fc * P:(fc + 1) * P].rearrange("(dc p) j -> p dc j", p=P))
            step("gpsimd_dma", wload, ndma=1)

            def mm(t, fc=fc):
                b3 = s_wbuf0.rearrange("p (dc j) -> p dc j", j=P)
                for half in range(2):
                    for dc in range(DMC):
                        last = t.matmul(PS[half][:, :], b3[:, dc, :],
                                        xn3[:, dc, half * 512:(half + 1) * 512],
                                        start=(dc == 0), stop=(dc == DMC - 1))
                return last
            step("tensor", mm)

            if fc < 32:
                def evx(v):
                    v.tensor_copy(s_big0.bitcast(BF16)[:, 0:512], PS[0][:, :])
                    return v.tensor_copy(s_big0.bitcast(BF16)[:, 512:1024], PS[1][:, :])
                step("vector", evx)
                step("sync", lambda E, dma, fc=fc:
                     dma(dram["xc_dram"][fc * P:(fc + 1) * P, :], s_big0.bitcast(BF16)[:, 0:T]),
                     ndma=1)
            else:
                def evz(s):
                    s.activation(s_big0.bitcast(BF16)[:, 0:512], PS[0][:, :], AF.Silu)
                    return s.activation(s_big0.bitcast(BF16)[:, 512:1024], PS[1][:, :], AF.Silu)
                step("scalar", evz)
                step("sync", lambda E, dma, fc=fc:
                     dma(dram["sz_dram"][(fc - 32) * P:(fc - 31) * P, :],
                         s_big0.bitcast(BF16)[:, 0:T]), ndma=1)

        def exchange(tag, k, body_fn):
            pre_a, post_a = body_fn("a")
            pre_b, post_b = body_fn("b")
            assert len(pre_a) == len(pre_b) and len(post_a) == len(post_b)
            n_total = len(pre_a) + 1 + len(post_a)
            prev = chain[0]
            cqs0 = C["qs"]

            def body(sy):
                if prev is not None:
                    sy.wait_ge(SEMS[prev[0]], prev[1])
                role = sy.value_load(s_role[0:1, 0:1])
                mreg = sy.value_load(s_mcmp[0:1, k:k + 1])
                with sy.register(f"ca_{tag}") as ca, sy.register(f"cb_{tag}") as cb:
                    sy.reg_alu(ca, role, 0, ALU.is_equal)
                    sy.reg_alu(cb, role, 1, ALU.is_equal)
                    for ab, creg, pre, post in (("a", ca, pre_a, post_a),
                                                ("b", cb, pre_b, post_b)):
                        other = "b" if ab == "a" else "a"
                        with sy.While(creg):
                            nd = 0
                            for dst, src in pre:
                                sy.dma_start(out=dst, in_=src).then_inc(qs, 16)
                                nd += 16
                            sy.wait_ge(qs, cqs0 + nd)
                            sy.dma_start(out=dram[f"fl{k}_{ab}"],
                                         in_=s_magic[0:1, 8 * k:8 * k + 8]).then_inc(qs, 16)
                            nd += 16
                            sy.wait_ge(qs, cqs0 + nd)
                            with sy.register(f"cw_{tag}_{ab}") as cond:
                                sy.sem_clear(spin)
                                sy.dma_start(out=s_flag[:], in_=dram[f"fl{k}_{other}"]).then_inc(spin, 16)
                                sy.wait_ge(spin, 16)
                                fl = sy.value_load(s_flag[0:1, 0:1])
                                sy.reg_alu(cond, fl, mreg, ALU.not_equal)
                                with sy.While(cond):
                                    sy.sem_clear(spin)
                                    sy.dma_start(out=s_flag[:], in_=dram[f"fl{k}_{other}"]).then_inc(spin, 16)
                                    sy.wait_ge(spin, 16)
                                    fl2 = sy.value_load(s_flag[0:1, 0:1])
                                    sy.reg_alu(cond, fl2, mreg, ALU.not_equal)
                            for dst, src in post:
                                sy.dma_start(out=dst, in_=src).then_inc(qs, 16)
                                nd += 16
                            sy.wait_ge(qs, cqs0 + nd)
                            sy.reg_alu(creg, creg, 0, ALU.mult)
            block.sync(body)
            C["qs"] += 16 * n_total
            chain[0] = ("qs", C["qs"])

        def e1_body(ab):
            me0 = (ab == "a")
            own_lo, own_hi = (0, T) if me0 else (T, TT)
            send = (DIH, DI) if me0 else (0, DIH)
            keep = (0, DIH) if me0 else (DIH, DI)
            other = "b" if ab == "a" else "a"
            po_lo, po_hi = (T, TT) if me0 else (0, T)
            pre = [
                (dram[f"sh_e1_{ab}"][0:DIH, :], dram["xc_dram"][send[0]:send[1], :]),
                (dram[f"sh_e1_{ab}"][DIH:DI, :], dram["sz_dram"][send[0]:send[1], :]),
                (dram["xc_scan"][:, own_lo:own_hi], dram["xc_dram"][keep[0]:keep[1], :]),
                (dram["sz_scan"][:, own_lo:own_hi], dram["sz_dram"][keep[0]:keep[1], :]),
            ]
            post = [
                (dram["xc_scan"][:, po_lo:po_hi], dram[f"sh_e1_{other}"][0:DIH, :]),
                (dram["sz_scan"][:, po_lo:po_hi], dram[f"sh_e1_{other}"][DIH:DI, :]),
            ]
            return pre, post
        exchange("e1", 0, e1_body)

        def wxp_load(E, dma):
            dma(s_wbig[:, 0:DIHC * 160].rearrange("p (dc o) -> p dc o", o=160),
                dram["w_xpT_loc"].rearrange("(dc p) o -> p dc o", p=P))
        step("gpsimd_dma", wxp_load, ndma=1)
        wxp3 = s_wbig[:, 0:DIHC * 160].rearrange("p (dc o) -> p dc o", o=160)
        xc_in = s_big2.bitcast(BF16)
        cw3 = s_convw.rearrange("p (dc j) -> p dc j", j=4)

        for dc in range(DIHC):
            step("sync", lambda E, dma, dc=dc:
                 dma(xc_in[:, 3:3 + TT], dram["xc_scan"][dc * P:(dc + 1) * P, :]), ndma=1)

            def conv(v, dc=dc):
                v.memset(xc_in[:, 0:3], 0.0)
                v.tensor_scalar_mul(s_big1[:, 0:TT], xc_in[:, 0:TT], cw3[:, dc, 0:1])
                v.scalar_tensor_tensor(s_big1[:, 0:TT], xc_in[:, 1:1 + TT], cw3[:, dc, 1:2],
                                       s_big1[:, 0:TT], ALU.mult, ALU.add)
                v.scalar_tensor_tensor(s_big1[:, 0:TT], xc_in[:, 2:2 + TT], cw3[:, dc, 2:3],
                                       s_big1[:, 0:TT], ALU.mult, ALU.add)
                return v.scalar_tensor_tensor(s_big1[:, 0:TT], xc_in[:, 3:3 + TT], cw3[:, dc, 3:4],
                                              s_big1[:, 0:TT], ALU.mult, ALU.add)
            step("vector", conv)
            step("scalar", lambda s, dc=dc:
                 s.activation(s_big3.bitcast(BF16)[:, 0:TT], s_big1[:, 0:TT],
                              AF.Silu, bias=s_convb[:, dc:dc + 1]))
            step("sync", lambda E, dma, dc=dc:
                 dma(dram["xt_scan"][dc * P:(dc + 1) * P, :], s_big3.bitcast(BF16)[:, 0:TT]),
                 ndma=1)

            def xp(t, dc=dc):
                for tg in range(4):
                    t.matmul(PS[tg][:, :], wxp3[:, dc, 0:P],
                             s_big3.bitcast(BF16)[:, tg * 512:(tg + 1) * 512],
                             start=(dc == 0), stop=(dc == DIHC - 1))
                    last = t.matmul(PS[4 + tg][0:32, :], wxp3[:, dc, P:160],
                                    s_big3.bitcast(BF16)[:, tg * 512:(tg + 1) * 512],
                                    start=(dc == 0), stop=(dc == DIHC - 1))
                return last
            step("tensor", xp)

        def xp_evac(v):
            for tg in range(4):
                v.tensor_copy(s_big0[:, tg * 512:(tg + 1) * 512], PS[tg][:, :])
                last = v.tensor_copy(s_big1[0:32, tg * 512:(tg + 1) * 512], PS[4 + tg][0:32, :])
            return last
        step("vector", xp_evac)

        def e2_body(ab):
            other = "b" if ab == "a" else "a"
            pre = [
                (dram[f"sh_e2_{ab}"][0:P, :], s_big0[:, 0:TT]),
                (dram[f"sh_e2_{ab}"][P:160, :], s_big1[0:32, 0:TT]),
            ]
            post = [
                (s_big2[:, 0:TT], dram[f"sh_e2_{other}"][0:P, :]),
                (s_big3[0:32, 0:TT], dram[f"sh_e2_{other}"][P:160, :]),
            ]
            return pre, post
        exchange("e2", 1, e2_body)

        s_dtlow = s_big4.bitcast(BF16)

        def sums(v):
            v.tensor_add(s_dtlow[:, 0:TT], s_big0[:, 0:TT], s_big2[:, 0:TT])
            return v.tensor_add(s_big3[0:32, 0:TT], s_big1[0:32, 0:TT], s_big3[0:32, 0:TT])
        step("vector", sums)
        step("sync", lambda E, dma: dma(dram["bc_dram"], s_big3[0:32, 0:TT]), ndma=1)

        def ph4_wload(E, dma):
            dma(s_wbig[:, 0:2048], dram["w_dtT_loc"])
            dma(s_wbig[:, 2048:4096], dram["selrep"])
            dma(s_wbig[:, 4096:6144], dram["selred"])
        step("gpsimd_dma", ph4_wload, ndma=3)
        s_wdt = s_wbig[:, 0:2048]
        selrep3 = s_wbig[:, 2048:4096].rearrange("p (m j) -> p m j", j=P)
        selred3 = s_wbig[:, 4096:6144].rearrange("p (m j) -> p m j", j=P)
        hc_bf = smb[:, 14336:15360]

        for th in range(2):
            tlo = th * 1024
            step("sync", lambda E, dma, th=th, tlo=tlo:
                 dma(s_big0[0:32, 0:1024], dram["bc_dram"][:, tlo:tlo + 1024]), ndma=1)

            def bc_build(t):
                for half in range(2):
                    t.matmul(PS[4 + half][:, :], s_sB[:, :],
                             s_big0[0:32, half * 512:(half + 1) * 512],
                             start=True, stop=True)
                    last = t.matmul(PS[6 + half][:, :], s_sC[:, :],
                                    s_big0[0:32, half * 512:(half + 1) * 512],
                                    start=True, stop=True)
                return last
            step("tensor", bc_build)

            def bc_ev(v):
                for half in range(2):
                    v.tensor_copy(smf[:, 5120 + half * 512:5120 + (half + 1) * 512],
                                  PS[4 + half][:, :])
                    last = v.tensor_copy(smf[:, 6144 + half * 512:6144 + (half + 1) * 512],
                                         PS[6 + half][:, :])
                return last
            step("vector", bc_ev)

            for dc in range(DIHC):
                def tile_ld(E, dma, dc=dc, tlo=tlo):
                    dma(s_wbuf0[:, 0:1024], dram["xt_scan"][dc * P:(dc + 1) * P, tlo:tlo + 1024])
                    dma(s_wbuf0[:, 1024:2048], dram["sz_scan"][dc * P:(dc + 1) * P, tlo:tlo + 1024])
                step("sync", tile_ld, ndma=2)

                def dt_mm(t, dc=dc, tlo=tlo):
                    for half in range(2):
                        last = t.matmul(PS[half][:, :], s_wdt[:, dc * P:(dc + 1) * P],
                                        s_dtlow[:, tlo + half * 512: tlo + (half + 1) * 512],
                                        start=True, stop=True)
                    return last
                step("tensor", dt_mm)

                def softplus(s, dc=dc):
                    for half in range(2):
                        s.activation(smf[:, 3072 + half * 512:3072 + (half + 1) * 512],
                                     PS[half][:, :], AF.Identity, bias=s_dtb[:, dc:dc + 1])
                    s.activation(smf[:, 4096:5120], smf[:, 3072:4096], AF.Abs)
                    s.activation(smf[:, 4096:5120], smf[:, 4096:5120], AF.Exp, scale=-1.0)
                    s.activation(smf[:, 4096:5120], smf[:, 4096:5120], AF.Ln, bias=1.0)
                    return s.activation(smf[:, 0:1024], smf[:, 3072:4096], AF.Relu)
                step("scalar", softplus)

                def dtu(v, dc=dc):
                    v.tensor_add(s_dtlow[:, 2048:3072], smf[:, 4096:5120], smf[:, 0:1024])
                    return v.tensor_mul(s_dtlow[:, 3072:4096], s_dtlow[:, 2048:3072],
                                        s_wbuf0[:, 0:1024])
                step("vector", dtu)
                step("sync", lambda E, dma, dc=dc, tlo=tlo:
                     dma(dram["dt_dbg"][dc * P:(dc + 1) * P, tlo:tlo + 1024],
                         s_dtlow[:, 2048:3072]), ndma=1)

                for m in range(N):
                    k = dc * N + m

                    def rep(t, m=m):
                        for sub in range(2):
                            t.matmul(PS[2 + sub][:, :], selrep3[:, m, :],
                                     s_dtlow[:, 2048 + sub * 512:2048 + (sub + 1) * 512],
                                     start=True, stop=True)
                            last = t.matmul(PS[4 + sub][:, :], selrep3[:, m, :],
                                            s_dtlow[:, 3072 + sub * 512:3072 + (sub + 1) * 512],
                                            start=True, stop=True)
                        return last
                    step("tensor", rep)

                    def dA(s, k=k):
                        s.activation(smf[:, 0:512], PS[2][:, :], AF.Exp,
                                     scale=s_alanes[:, k:k + 1])
                        return s.activation(smf[:, 512:1024], PS[3][:, :], AF.Exp,
                                            scale=s_alanes[:, k:k + 1])
                    step("scalar", dA)

                    def vmk(v, k=k, th=th):
                        v.tensor_mul(smf[:, 1024:1536], PS[4][:, :], smf[:, 5120:5632])
                        v.tensor_mul(smf[:, 1536:2048], PS[5][:, :], smf[:, 5632:6144])
                        init = 0.0 if th == 0 else s_hcarry[:, k:k + 1]
                        v.tensor_tensor_scan(smf[:, 2048:3072], smf[:, 0:1024],
                                             smf[:, 1024:2048], init, ALU.mult, ALU.add)
                        if th == 0:
                            v.tensor_copy(s_hcarry[:, k:k + 1], smf[:, 3071:3072])
                        return v.tensor_mul(hc_bf[:, 0:1024], smf[:, 2048:3072],
                                            smf[:, 6144:7168])
                    step("vector", vmk)

                    if th == 0 and dc == 0 and m == 0:
                        step("sync", lambda E, dma:
                             dma(dram["scan_dbg"], smf[:, 0:3072]), ndma=1)

                    def red(t, m=m):
                        for half in range(2):
                            last = t.matmul(PS[6 + half][:, :], selred3[:, m, :],
                                            hc_bf[:, half * 512:(half + 1) * 512],
                                            start=(m == 0), stop=(m == N - 1))
                        return last
                    step("tensor", red)

                def ya(v, dc=dc):
                    for half in range(2):
                        v.scalar_tensor_tensor(smf[:, 3072 + half * 512:3072 + (half + 1) * 512],
                                               s_wbuf0[:, half * 512:(half + 1) * 512],
                                               s_dloc[:, dc:dc + 1],
                                               PS[6 + half][:, :], ALU.mult, ALU.add)
                    return v.tensor_mul(s_wbuf1[:, 0:1024], smf[:, 3072:4096],
                                        s_wbuf0[:, 1024:2048])
                step("vector", ya)
                step("sync", lambda E, dma, dc=dc, tlo=tlo:
                     dma(dram["ya_loc"][dc * P:(dc + 1) * P, tlo:tlo + 1024],
                         s_wbuf1[:, 0:1024]), ndma=1)

        def e3_body(ab):
            me0 = (ab == "a")
            own = (0, T) if me0 else (T, TT)
            myrows = (0, DIH) if me0 else (DIH, DI)
            other = "b" if ab == "a" else "a"
            pr = (DIH, DI) if me0 else (0, DIH)
            pre = [
                (dram[f"sh_e3_{ab}"], dram["ya_loc"][:, (T if me0 else 0):(TT if me0 else T)]),
                (dram["ya_own"][myrows[0]:myrows[1], :], dram["ya_loc"][:, own[0]:own[1]]),
            ]
            post = [(dram["ya_own"][pr[0]:pr[1], :], dram[f"sh_e3_{other}"])]
            return pre, post
        exchange("e3", 2, e3_body)

        for grp in range(16):
            def wld(E, dma, grp=grp):
                dma(s_wbig[:, 0:4096].rearrange("p (j d) -> p j d", d=DM),
                    dram["w_outT"][grp * 256:(grp + 1) * 256, :].rearrange("(j p) d -> p j d", p=P))
            step("gpsimd_dma", wld, ndma=1)

            def yld(E, dma, grp=grp):
                dma(smb[:, 0:2048].rearrange("p (j t) -> p j t", t=T),
                    dram["ya_own"][grp * 256:(grp + 1) * 256, :].rearrange("(j p) t -> p j t", p=P))
            step("sync", yld, ndma=1)

            for dm in range(DMC):
                def opmm(t, grp=grp, dm=dm):
                    w3 = s_wbig[:, 0:4096].rearrange("p (j d) -> p j d", d=DM)
                    y3 = smb[:, 0:2048].rearrange("p (j t) -> p j t", t=T)
                    for half in range(2):
                        for j in range(2):
                            last = t.matmul(PS[half][:, :], w3[:, j, dm * P:(dm + 1) * P],
                                            y3[:, j, half * 512:(half + 1) * 512],
                                            start=(j == 0), stop=(j == 1))
                    return last
                step("tensor", opmm)

                def opadd(v, dm=dm):
                    for half in range(2):
                        last = v.scalar_tensor_tensor(
                            s_xacc[:, dm * T + half * 512: dm * T + (half + 1) * 512],
                            PS[half][:, :], 1.0,
                            s_xacc[:, dm * T + half * 512: dm * T + (half + 1) * 512],
                            ALU.mult, ALU.add)
                    return last
                step("vector", opadd)

        for i in range(NT):
            step("tensor", lambda t, i=i:
                 t.matmul(PS[0][:, 0:E_EXP], s_logits[:, i * P:(i + 1) * P],
                          s_idf[64:64 + E_EXP, 64:64 + E_EXP],
                          is_transpose=True, start=True, stop=True))
            step("vector", lambda v, i=i:
                 v.tensor_copy(s_big0[:, i * E_EXP:(i + 1) * E_EXP], PS[0][:, 0:E_EXP]))

        step("scalar", lambda s: s.activation(s_big2[:, 0:64], s_big0[:, 0:64],
                                              AF.Sigmoid, scale=s_rtemp[:, 0:1]))

        def rv1(v):
            v.tensor_scalar_add(s_big2[:, 64:128], s_big2[:, 0:64], -GOLDEN_CENTER)
            v.tensor_scalar(s_big2[:, 128:192], s_big2[:, 0:64], GOLDEN_LOWER, None, ALU.is_ge)
            v.tensor_scalar(s_big2[:, 192:256], s_big2[:, 0:64], GOLDEN_UPPER, None, ALU.is_le)
            return v.tensor_mul(s_big2[:, 128:192], s_big2[:, 128:192], s_big2[:, 192:256])
        step("vector", rv1)

        def rsig(s):
            s.activation(s_big2[:, 256:320], s_big2[:, 64:128], AF.Abs)
            s.activation(s_big2[:, 320:384], s_big2[:, 256:320], AF.Exp, scale=-10.0)
            return s.activation(s_big2[:, 384:448], s_big2[:, 256:320], AF.Exp, scale=-1.0 / 0.3)
        step("scalar", rsig)

        def rv2(v):
            v.tensor_mul(s_big2[:, 448:512], s_big2[:, 320:384], s_big2[:, 128:192])
            for i in range(NT):
                def a(c):
                    return s_big2[:, c + i * E_EXP: c + (i + 1) * E_EXP]
                wt = a(448)
                fb = a(384)
                v.reduce_sum(s_big3[:, i:i + 1], wt, AX.X)
                v.tensor_scalar(s_big3[:, 8 + i:9 + i], s_big3[:, i:i + 1], 1e-8, None, ALU.is_lt)
                v.reduce_max(s_big3[:, 16 + i:17 + i], fb, AX.X)
                v.tensor_scalar(a(512), fb, s_big3[:, 16 + i:17 + i], None, ALU.is_ge)
                v.tensor_scalar_mul(a(576), a(512), 1e30)
                v.tensor_sub(a(576), fb, a(576))
                v.reduce_max(s_big3[:, 24 + i:25 + i], a(576), AX.X)
                v.tensor_scalar(a(640), a(576), s_big3[:, 24 + i:25 + i], None, ALU.is_ge)
                v.tensor_add(a(640), a(640), a(512))
                v.tensor_mul(a(640), a(640), fb)
                v.reduce_sum(s_big3[:, 32 + i:33 + i], a(640), AX.X)
                v.tensor_scalar_max(s_big3[:, 32 + i:33 + i], s_big3[:, 32 + i:33 + i], 1e-8)
                v.reciprocal(s_big3[:, 40 + i:41 + i], s_big3[:, 32 + i:33 + i])
                v.tensor_scalar_mul(a(640), a(640), s_big3[:, 40 + i:41 + i])
                v.tensor_sub(a(704), a(640), wt)
                v.scalar_tensor_tensor(a(704), a(704), s_big3[:, 8 + i:9 + i], wt,
                                       ALU.mult, ALU.add)
                v.reduce_sum(s_big3[:, 48 + i:49 + i], a(704), AX.X)
                v.tensor_scalar_max(s_big3[:, 48 + i:49 + i], s_big3[:, 48 + i:49 + i], 1e-8)
                v.reciprocal(s_big3[:, 56 + i:57 + i], s_big3[:, 48 + i:49 + i])
                last = v.tensor_scalar_mul(a(704), a(704), s_big3[:, 56 + i:57 + i])
            return last
        step("vector", rv2)

        for i in range(NT):
            step("tensor", lambda t, i=i:
                 t.matmul(PS[0][0:E_EXP, 0:P], s_big2[:, 704 + i * E_EXP:704 + (i + 1) * E_EXP],
                          s_idf[:, :], is_transpose=True, start=True, stop=True))
            step("vector", lambda v, i=i:
                 v.tensor_copy(s_wfm[:, i * P:(i + 1) * P], PS[0][0:E_EXP, 0:P]))

        moe_wg3 = dram["moe_wg"].rearrange("(e d) f -> e d f", e=E_EXP)
        moe_wu3 = dram["moe_wu"].rearrange("(e d) f -> e d f", e=E_EXP)
        moe_wd3 = dram["moe_wd"].rearrange("(e f) d -> e f d", e=E_EXP)
        ga3 = dram["moe_ga"].rearrange("(e d) r -> e d r", e=NL)
        gb3 = dram["moe_gb"].rearrange("(e r) f -> e r f", e=NL)
        ua3 = dram["moe_ua"].rearrange("(e d) r -> e d r", e=NL)
        ub3 = dram["moe_ub"].rearrange("(e r) f -> e r f", e=NL)
        da3 = dram["moe_da"].rearrange("(e f) r -> e f r", e=NL)
        db3 = dram["moe_db"].rearrange("(e r) d -> e r d", e=NL)
        mid3 = smb.rearrange("p (q t) -> p q t", t=T)

        for e in range(E_EXP):
            def wrep(E, dma, e=e, cqs=C["qs"]):
                k_ = [0]

                def dmaw(dst, src):
                    dma(dst, src)
                    k_[0] += 16
                    E.wait_ge(qs, cqs + k_[0])
                dmaw(s_big2[0:1, 0:1024], s_wfm[e:e + 1, :])
                n = 1
                while n < P:
                    dmaw(s_big2[n:min(2 * n, P), 0:1024], s_big2[0:min(n, P - n), 0:1024])
                    n *= 2
            step("sync", wrep, ndma=8)

            if e < NL:
                def lr_load(E, dma, e=e):
                    dma(s_wbig[0:R_LR, 0:F_MOE], gb3[e])
                    dma(s_wbig[0:R_LR, F_MOE:2 * F_MOE], ub3[e])
                    dma(s_wbig[:, 4096:5120].rearrange("p (dc r) -> p dc r", r=R_LR),
                        ga3[e].rearrange("(dc p) r -> p dc r", p=P))
                    dma(s_wbig[:, 5120:6144].rearrange("p (dc r) -> p dc r", r=R_LR),
                        ua3[e].rearrange("(dc p) r -> p dc r", p=P))
                step("gpsimd_dma", lr_load, ndma=4)
                ga_s = s_wbig[:, 4096:5120].rearrange("p (dc r) -> p dc r", r=R_LR)
                ua_s = s_wbig[:, 5120:6144].rearrange("p (dc r) -> p dc r", r=R_LR)

                for which, wsrc in (("g", ga_s), ("u", ua_s)):
                    off = 0 if which == "g" else 1024
                    for half in range(2):
                        def lrmm(t, wsrc=wsrc, half=half):
                            for dc in range(DMC):
                                last = t.matmul(PS[7][0:R_LR, :], wsrc[:, dc, :],
                                                xn3[:, dc, half * 512:(half + 1) * 512],
                                                start=(dc == 0), stop=(dc == DMC - 1))
                            return last
                        step("tensor", lrmm)
                        step("vector", lambda v, off=off, half=half:
                             v.tensor_copy(s_lr[0:R_LR, off + half * 512:off + (half + 1) * 512],
                                           PS[7][0:R_LR, :]))

            for fc in range(FC_MOE):
                def gw(E, dma, e=e, fc=fc):
                    dma(s_wbuf0.rearrange("p (dc j) -> p dc j", j=P),
                        moe_wg3[e][:, fc * P:(fc + 1) * P].rearrange("(dc p) j -> p dc j", p=P))
                    dma(s_wbuf1.rearrange("p (dc j) -> p dc j", j=P),
                        moe_wu3[e][:, fc * P:(fc + 1) * P].rearrange("(dc p) j -> p dc j", p=P))
                step("gpsimd_dma", gw, ndma=2)

                def gumm(t, e=e, fc=fc):
                    bg = s_wbuf0.rearrange("p (dc j) -> p dc j", j=P)
                    bu = s_wbuf1.rearrange("p (dc j) -> p dc j", j=P)
                    last = None
                    for half in range(2):
                        for dc in range(DMC):
                            last = t.matmul(PS[0 + half][:, :], bg[:, dc, :],
                                            xn3[:, dc, half * 512:(half + 1) * 512],
                                            start=(dc == 0),
                                            stop=(dc == DMC - 1 and e >= NL))
                        if e < NL:
                            last = t.matmul(PS[0 + half][:, :],
                                            s_wbig[0:R_LR, fc * P:(fc + 1) * P],
                                            s_lr[0:R_LR, half * 512:(half + 1) * 512],
                                            start=False, stop=True)
                        for dc in range(DMC):
                            last = t.matmul(PS[2 + half][:, :], bu[:, dc, :],
                                            xn3[:, dc, half * 512:(half + 1) * 512],
                                            start=(dc == 0),
                                            stop=(dc == DMC - 1 and e >= NL))
                        if e < NL:
                            last = t.matmul(PS[2 + half][:, :],
                                            s_wbig[0:R_LR, F_MOE + fc * P:F_MOE + (fc + 1) * P],
                                            s_lr[0:R_LR, 1024 + half * 512:1024 + (half + 1) * 512],
                                            start=False, stop=True)
                    return last
                step("tensor", gumm)

                def silu_g(s):
                    for half in range(2):
                        last = s.activation(s_big0[:, half * 512:(half + 1) * 512],
                                            PS[0 + half][:, :], AF.Silu)
                    return last
                step("scalar", silu_g)

                def midv(v, fc=fc):
                    for half in range(2):
                        v.tensor_mul(s_big1[:, half * 512:(half + 1) * 512],
                                     s_big0[:, half * 512:(half + 1) * 512], PS[2 + half][:, :])
                        last = v.tensor_mul(mid3[:, fc, half * 512:(half + 1) * 512],
                                            s_big1[:, half * 512:(half + 1) * 512],
                                            s_big2[:, half * 512:(half + 1) * 512])
                    return last
                step("vector", midv)

            if e < NL:
                def da_load(E, dma, e=e):
                    dma(s_wbig[:, 4096:5120].rearrange("p (q r) -> p q r", r=R_LR),
                        da3[e].rearrange("(q p) r -> p q r", p=P))
                    dma(s_wbig[0:R_LR, 0:DM], db3[e])
                step("gpsimd_dma", da_load, ndma=2)
                da_s = s_wbig[:, 4096:5120].rearrange("p (q r) -> p q r", r=R_LR)

                for half in range(2):
                    def damm(t, half=half):
                        for q in range(FC_MOE):
                            last = t.matmul(PS[7][0:R_LR, :], da_s[:, q, :],
                                            mid3[:, q, half * 512:(half + 1) * 512],
                                            start=(q == 0), stop=(q == FC_MOE - 1))
                        return last
                    step("tensor", damm)
                    step("vector", lambda v, half=half:
                         v.tensor_copy(s_lr[0:R_LR, half * 512:(half + 1) * 512],
                                       PS[7][0:R_LR, :]))

            for dm in range(DMC):
                def dw(E, dma, e=e, dm=dm):
                    dma(s_wbuf0.rearrange("p (q j) -> p q j", j=P),
                        moe_wd3[e][:, dm * P:(dm + 1) * P].rearrange("(q p) j -> p q j", p=P))
                step("gpsimd_dma", dw, ndma=1)

                def dmm(t, e=e, dm=dm):
                    b3 = s_wbuf0.rearrange("p (q j) -> p q j", j=P)
                    last = None
                    for half in range(2):
                        for q in range(FC_MOE):
                            last = t.matmul(PS[4 + half][:, :], b3[:, q, :],
                                            mid3[:, q, half * 512:(half + 1) * 512],
                                            start=(q == 0),
                                            stop=(q == FC_MOE - 1 and e >= NL))
                        if e < NL:
                            last = t.matmul(PS[4 + half][:, :],
                                            s_wbig[0:R_LR, dm * P:(dm + 1) * P],
                                            s_lr[0:R_LR, half * 512:(half + 1) * 512],
                                            start=False, stop=True)
                    return last
                step("tensor", dmm)

                def dadd(v, dm=dm):
                    for half in range(2):
                        last = v.scalar_tensor_tensor(
                            s_xacc[:, dm * T + half * 512: dm * T + (half + 1) * 512],
                            PS[4 + half][:, :], 1.0,
                            s_xacc[:, dm * T + half * 512: dm * T + (half + 1) * 512],
                            ALU.mult, ALU.add)
                    return last
                step("vector", dadd)

        for dc in range(DMC):
            step("scalar", lambda s, dc=dc:
                 s.activation(s_big0[:, 0:1024], s_xacc[:, dc * T:(dc + 1) * T], AF.Square))

            def st6(t, dc=dc):
                t.matmul(PS[6][0:1, :], s_ones_c_f[:, :], s_xacc[:, dc * T: dc * T + 512],
                         start=(dc == 0), stop=(dc == DMC - 1))
                t.matmul(PS[5][0:1, :], s_ones_c_f[:, :], s_xacc[:, dc * T + 512: dc * T + 1024],
                         start=(dc == 0), stop=(dc == DMC - 1))
                t.matmul(PS[7][0:1, :], s_ones_c_f[:, :], s_big0[:, 0:512],
                         start=(dc == 0), stop=(dc == DMC - 1))
                return t.matmul(PS[4][0:1, :], s_ones_c_f[:, :], s_big0[:, 512:1024],
                                start=(dc == 0), stop=(dc == DMC - 1))
            step("tensor", st6)

        def rows6(v):
            v.tensor_copy(row0[0:1, 0:512], PS[6][0:1, :])
            v.tensor_copy(row0[0:1, 512:1024], PS[5][0:1, :])
            v.tensor_copy(row0[0:1, 1024:1536], PS[7][0:1, :])
            v.tensor_copy(row0[0:1, 1536:2048], PS[4][0:1, :])
            v.tensor_scalar_mul(row0[0:1, 0:1024], row0[0:1, 0:1024], -1.0 / DM)
            v.tensor_scalar_mul(row0[0:1, 1024:2048], row0[0:1, 1024:2048], 1.0 / DM)
            v.tensor_mul(s_big1[0:1, 1024:2048], row0[0:1, 0:1024], row0[0:1, 0:1024])
            return v.tensor_sub(row0[0:1, 1024:2048], row0[0:1, 1024:2048],
                                s_big1[0:1, 1024:2048])
        step("vector", rows6)

        def rstd6(s):
            s.activation(row0[0:1, 1024:2048], row0[0:1, 1024:2048], AF.Ln,
                         bias=s_rtemp[0:1, 1:2])
            return s.activation(row0[0:1, 1024:2048], row0[0:1, 1024:2048],
                                AF.Exp, scale=-0.5)
        step("scalar", rstd6)

        def rep6(t):
            t.matmul(PS[6][:, :], s_ones_r_f[:, :], row0[0:1, 0:512], start=True, stop=True)
            t.matmul(PS[5][:, :], s_ones_r_f[:, :], row0[0:1, 512:1024], start=True, stop=True)
            t.matmul(PS[7][:, :], s_ones_r_f[:, :], row0[0:1, 1024:1536], start=True, stop=True)
            return t.matmul(PS[4][:, :], s_ones_r_f[:, :], row0[0:1, 1536:2048],
                            start=True, stop=True)
        step("tensor", rep6)

        def xn1f(v):
            v.tensor_copy(s_big1[:, 0:512], PS[6][:, :])
            v.tensor_copy(s_big1[:, 512:1024], PS[5][:, :])
            v.tensor_copy(s_big2[:, 0:512], PS[7][:, :])
            v.tensor_copy(s_big2[:, 512:1024], PS[4][:, :])
            for dc in range(DMC):
                v.tensor_add(s_big0[:, 0:1024], s_xacc[:, dc * T:(dc + 1) * T], s_big1[:, 0:1024])
                last = v.tensor_mul(xn3[:, dc, :], s_big0[:, 0:1024], s_big2[:, 0:1024])
            return last
        step("vector", xn1f)

        l1w = [("wq1", 0, 8), ("wk1", 1024, 8), ("ws1", 2048, 8),
               ("wv1", 3072, 16), ("wg1", 5120, 16)]
        for wname, rbase, nfc in l1w:
            for fc in range(nfc):
                step("gpsimd_dma", lambda E, dma, wname=wname, fc=fc:
                     dma(s_wbuf0.rearrange("p (dc j) -> p dc j", j=P),
                         dram[wname][:, fc * P:(fc + 1) * P].rearrange("(dc p) j -> p dc j", p=P)),
                     ndma=1)

                def pmm(t, fc=fc):
                    b3 = s_wbuf0.rearrange("p (dc j) -> p dc j", j=P)
                    for half in range(2):
                        for dc in range(DMC):
                            last = t.matmul(PS[half][:, :], b3[:, dc, :],
                                            xn3[:, dc, half * 512:(half + 1) * 512],
                                            start=(dc == 0), stop=(dc == DMC - 1))
                    return last
                step("tensor", pmm)

                if wname == "ws1":
                    def pev(v):
                        for half in range(2):
                            last = v.tensor_scalar(
                                s_big0.bitcast(BF16)[:, half * 512:(half + 1) * 512],
                                PS[half][:, :], 32.0, -32.0, ALU.min, ALU.max)
                        return last
                else:
                    def pev(v):
                        for half in range(2):
                            last = v.tensor_copy(
                                s_big0.bitcast(BF16)[:, half * 512:(half + 1) * 512],
                                PS[half][:, :])
                        return last
                step("vector", pev)
                step("sync", lambda E, dma, rbase=rbase, fc=fc:
                     dma(dram["attn_loc"][rbase + fc * P: rbase + (fc + 1) * P, :],
                         s_big0.bitcast(BF16)[:, 0:T]), ndma=1)

        segs = [("attn_q", 0, 512, 0), ("attn_k", 1024, 512, 512),
                ("attn_s", 2048, 512, 1024), ("attn_v", 3072, 1024, 1536),
                ("attn_g", 5120, 1024, 2560)]

        def e4_body(ab):
            me0 = (ab == "a")
            own_lo, own_hi = (0, T) if me0 else (T, TT)
            other = "b" if ab == "a" else "a"
            po_lo, po_hi = (T, TT) if me0 else (0, T)
            pre, post = [], []
            for can, lbase, sz, shb in segs:
                mine_lo = lbase + (0 if me0 else sz)
                send_lo = lbase + (sz if me0 else 0)
                pre.append((dram[can][:, own_lo:own_hi],
                            dram["attn_loc"][mine_lo:mine_lo + sz, :]))
                pre.append((dram[f"sh_e4_{ab}"][shb:shb + sz, :],
                            dram["attn_loc"][send_lo:send_lo + sz, :]))
                post.append((dram[can][:, po_lo:po_hi],
                             dram[f"sh_e4_{other}"][shb:shb + sz, :]))
            return pre, post
        exchange("e4", 3, e4_body)

        b0b = s_big0.bitcast(BF16)
        b1b = s_big1.bitcast(BF16)
        b2b = s_big2.bitcast(BF16)
        b3b = s_big3.bitcast(BF16)
        p2v = s_big4.bitcast(BF16)[:, 2048:4096]
        o3 = smb.rearrange("p (hh t) -> p hh t", t=TT)

        for h in range(HH):
            def hld(E, dma, h=h):
                dma(b0b[0:DK, 0:TT], dram["attn_q"][h * DK:(h + 1) * DK, :])
                dma(b0b[0:DK, 2048:4096], dram["attn_k"][h * DK:(h + 1) * DK, :])
                dma(b1b[0:M_ABC, 2048:4096], dram["attn_s"][h * M_ABC:(h + 1) * M_ABC, :])
                dma(b3b[:, 2048:4096], dram["attn_v"][h * DV:(h + 1) * DV, :])
                dma(s_wbuf0[:, 0:TT], dram["attn_g"][h * DV:(h + 1) * DV, :])
            step("sync", hld, ndma=5)

            step("scalar", lambda s:
                 s.activation(b1b[0:M_ABC, 0:TT], b1b[0:M_ABC, 2048:4096], AF.Exp))
            step("scalar", lambda s:
                 s.activation(s_wbuf1[:, 0:TT], s_wbuf0[:, 0:TT], AF.Silu))

            ktm = b2b[:, 0:1024].rearrange("p (c x) -> p c x", x=DK)
            ektm = b2b[:, 1024:2048].rearrange("p (c x) -> p c x", x=DK)
            vtm = b3b[:, 0:2048].rearrange("p (c x) -> p c x", x=DV)
            sinv = s_big4[:, 0:1024].rearrange("p (c x) -> p c x", x=DK)
            rfm = b2b[:, 2048:4096]
            hk_sb = b1b[0:DK, 4096:4160]
            hv_sb = b1b[0:M_ABC, 4160:4288]
            rtm_sb = b1b[:, 4288:4352]
            s1m = b1b[:, 4352:4480]

            for cc in range(16):
                def trp(t, cc=cc):
                    t.matmul(PSB[0][:, 0:DK], b0b[0:DK, 2048 + cc * P:2048 + (cc + 1) * P],
                             s_idb[0:DK, 0:DK], is_transpose=True, start=True, stop=True)
                    t.matmul(PSB[1][:, 0:DK], b1b[0:M_ABC, cc * P:(cc + 1) * P],
                             s_idb[0:M_ABC, 0:M_ABC], is_transpose=True, start=True, stop=True)
                    return t.matmul(PSB[2][:, 0:DV], b3b[:, 2048 + cc * P:2048 + (cc + 1) * P],
                                    s_idb[:, :], is_transpose=True, start=True, stop=True)
                step("tensor", trp)

                def trev(v, cc=cc):
                    v.tensor_copy(ktm[:, cc, :], PSB[0][:, 0:DK])
                    v.tensor_copy(ektm[:, cc, :], PSB[1][:, 0:DK])
                    return v.tensor_copy(vtm[:, cc, :], PSB[2][:, 0:DV])
                step("vector", trev)

            for cc in range(16):
                def p1a(t, cc=cc):
                    t.matmul(PS[6][:, 0:DK], s_tri[:, :], ektm[:, cc, :],
                             start=True, stop=(cc == 0))
                    if cc > 0:
                        t.matmul(PS[6][:, 0:DK], s_ones_r_bf[:, :], s_crow[0:1, :],
                                 start=False, stop=True)
                    t.matmul(PS[7][0:1, 0:DK], s_ones_c_bf[:, :], ektm[:, cc, :],
                             start=(cc == 0), stop=(cc == 15))
                    return t.matmul(PS[5][:, 0:P], b0b[0:DK, 2048 + cc * P:2048 + (cc + 1) * P],
                                    b0b[0:DK, cc * P:(cc + 1) * P], start=True, stop=True)
                step("tensor", p1a)

                def p1b(v, cc=cc):
                    v.reciprocal(sinv[:, cc, :], PS[6][:, 0:DK])
                    v.tensor_copy(s_crow[0:1, :], PS[7][0:1, 0:DK])
                    return v.scalar_tensor_tensor(s1m, PS[5][:, 0:P], 0.125,
                                                  s_tri[:, :], ALU.mult, ALU.mult)
                step("vector", p1b)

                def p1c(t, cc=cc):
                    t.matmul(PS[4][:, 0:DK], s1m, ektm[:, cc, :],
                             start=True, stop=(cc == 0))
                    if cc > 0:
                        t.matmul(PS[4][:, 0:DK], b0b[0:DK, cc * P:(cc + 1) * P],
                                 hk_sb, start=False, stop=True)
                    return t.matmul(PS[3][0:DK, 0:DK], ktm[:, cc, :], ektm[:, cc, :],
                                    start=(cc == 0), stop=(cc == 15))
                step("tensor", p1c)

                step("scalar", lambda s:
                     s.activation(hk_sb, PS[3][0:DK, 0:DK], AF.Identity, scale=0.125))

                step("vector", lambda v, cc=cc:
                     v.tensor_mul(s_big4[:, 1024:1088], PS[4][:, 0:DK], sinv[:, cc, :]))
                step("scalar", lambda s:
                     s.activation(s_big4[:, 1088:1152], s_big4[:, 1024:1088], AF.Exp))

                def smax2(v, cc=cc):
                    v.reduce_sum(s_big4[:, 1152:1153], s_big4[:, 1088:1152], AX.X)
                    v.reciprocal(s_big4[:, 1153:1154], s_big4[:, 1152:1153])
                    v.tensor_mul(s_big4[:, 1216:1280], s_big4[:, 1088:1152], sinv[:, cc, :])
                    return v.tensor_scalar_mul(rtm_sb, s_big4[:, 1216:1280],
                                               s_big4[:, 1153:1154])
                step("vector", smax2)

                step("tensor", lambda t:
                     t.matmul(PSB[1][0:DK, 0:P], rtm_sb[:, 0:DK], s_idb[:, :],
                              is_transpose=True, start=True, stop=True))
                step("vector", lambda v, cc=cc:
                     v.tensor_copy(rfm[0:DK, cc * P:(cc + 1) * P], PSB[1][0:DK, 0:P]))

            for j in range(16):
                ncols = TT - j * P
                nmm = (ncols + 511) // 512

                for qbase in range(0, nmm, 3):
                    qgrp = list(range(qbase, min(qbase + 3, nmm)))

                    def p2a(t, j=j, qgrp=qgrp):
                        for idx, q in enumerate(qgrp):
                            cl = j * P + q * 512
                            cw = min(512, TT - cl)
                            last = t.matmul(PS[4 + idx][:, 0:cw],
                                            b1b[0:M_ABC, j * P:(j + 1) * P],
                                            rfm[0:M_ABC, cl:cl + cw],
                                            start=True, stop=True)
                        return last
                    step("tensor", p2a)

                    def p2b(v, j=j, qgrp=qgrp):
                        for idx, q in enumerate(qgrp):
                            cl = j * P + q * 512
                            cw = min(512, TT - cl)
                            if q == 0:
                                last = v.tensor_mul(p2v[:, 0:P],
                                                    PS[4 + idx][:, 0:P], s_tri[:, :])
                                if cw > P:
                                    last = v.tensor_copy(p2v[:, P:cw],
                                                         PS[4 + idx][:, P:cw])
                            else:
                                last = v.tensor_copy(p2v[:, cl - j * P:cl - j * P + cw],
                                                     PS[4 + idx][:, 0:cw])
                        return last
                    step("vector", p2b)

                def p2c(t, j=j):
                    last = None
                    if j > 0:
                        bank, off = divmod(j * P, 512)
                        last = t.matmul(PS[bank][:, off:off + P], hv_sb,
                                        rfm[0:M_ABC, j * P:(j + 1) * P], start=False, stop=False)
                    for q in range(4):
                        blo, bhi = q * 512, (q + 1) * 512
                        lo = max(j * P, blo)
                        if lo >= bhi:
                            continue
                        last = t.matmul(PS[q][:, lo - blo:bhi - blo], vtm[:, j, :],
                                        p2v[:, lo - j * P:bhi - j * P],
                                        start=(j == 0), stop=(j == min(4 * q + 3, 15)))
                    last = t.matmul(PS[7][0:M_ABC, 0:DV], ektm[:, j, :], vtm[:, j, :],
                                    start=(j == 0), stop=(j == 15))
                    return last
                step("tensor", p2c)

                step("scalar", lambda s:
                     s.activation(hv_sb, PS[7][0:M_ABC, 0:DV], AF.Identity))

            def rms_s(s):
                for q in range(4):
                    last = s.activation(b0b[:, q * 512:(q + 1) * 512], PS[q][:, :], AF.Square)
                return last
            step("scalar", rms_s)

            for q in range(4):
                step("tensor", lambda t, q=q:
                     t.matmul(PS[6][0:1, 0:512], s_ones_c_bf[:, :],
                              b0b[:, q * 512:(q + 1) * 512], start=True, stop=True))
                step("vector", lambda v, q=q:
                     v.tensor_copy(row0[0:1, q * 512:(q + 1) * 512], PS[6][0:1, 0:512]))

            step("vector", lambda v:
                 v.tensor_scalar_mul(row0, row0, 1.0 / DV))

            def rmsact(s):
                s.activation(row0, row0, AF.Ln, bias=s_rtemp[0:1, 1:2])
                return s.activation(row0, row0, AF.Exp, scale=-0.5)
            step("scalar", rmsact)

            for q in range(4):
                step("tensor", lambda t, q=q:
                     t.matmul(PS[5][:, :], s_ones_r_f[:, :], row0[0:1, q * 512:(q + 1) * 512],
                              start=True, stop=True))

                def gate(v, h=h, q=q):
                    v.tensor_copy(s_big4[:, 1024:1536], PS[5][:, :])
                    v.tensor_mul(s_big4[:, 1536:2048], PS[q][:, :], s_big4[:, 1024:1536])
                    v.tensor_scalar_mul(s_big4[:, 1536:2048], s_big4[:, 1536:2048], s_gnw[:, 0:1])
                    return v.tensor_mul(o3[:, h, q * 512:(q + 1) * 512], s_big4[:, 1536:2048],
                                        s_wbuf1[:, q * 512:(q + 1) * 512])
                step("vector", gate)

        def e5_body(ab):
            me0 = (ab == "a")
            own_lo = 0 if me0 else T
            myrows = 0 if me0 else 1024
            other = "b" if ab == "a" else "a"
            prr = 1024 if me0 else 0
            pre = []
            for hh_ in range(HH):
                pre.append((dram["o_own"][myrows + hh_ * DV: myrows + (hh_ + 1) * DV, :],
                            o3[:, hh_, own_lo:own_lo + T]))
                pre.append((dram[f"sh_e5_{ab}"][hh_ * DV:(hh_ + 1) * DV, :],
                            o3[:, hh_, (T if me0 else 0):(TT if me0 else T)]))
            post = [(dram["o_own"][prr:prr + 1024, :], dram[f"sh_e5_{other}"])]
            return pre, post
        exchange("e5", 4, e5_body)

        def oo_ld(E, dma):
            dma(smb.rearrange("p (j t) -> p j t", t=T)[:, 0:16, :],
                dram["o_own"].rearrange("(j p) t -> p j t", p=P))
        step("sync", oo_ld, ndma=1)

        for grp in range(8):
            def wld7(E, dma, grp=grp):
                dma(s_wbig[:, 0:4096].rearrange("p (j d) -> p j d", d=DM),
                    dram["wo1"][grp * 256:(grp + 1) * 256, :].rearrange("(j p) d -> p j d", p=P))
            step("gpsimd_dma", wld7, ndma=1)

            for dm in range(DMC):
                def womm(t, grp=grp, dm=dm):
                    w3 = s_wbig[:, 0:4096].rearrange("p (j d) -> p j d", d=DM)
                    om = smb.rearrange("p (j t) -> p j t", t=T)
                    for half in range(2):
                        for j in range(2):
                            last = t.matmul(PS[half][:, :], w3[:, j, dm * P:(dm + 1) * P],
                                            om[:, grp * 2 + j, half * 512:(half + 1) * 512],
                                            start=(j == 0), stop=(j == 1))
                    return last
                step("tensor", womm)

                def woadd(v, dm=dm):
                    for half in range(2):
                        last = v.scalar_tensor_tensor(
                            s_xacc[:, dm * T + half * 512: dm * T + (half + 1) * 512],
                            PS[half][:, :], 1.0,
                            s_xacc[:, dm * T + half * 512: dm * T + (half + 1) * 512],
                            ALU.mult, ALU.add)
                    return last
                step("vector", woadd)

        def rs_t(t):
            for dc in range(DMC):
                t.matmul(PS[6][0:1, :], s_ones_c_bf[:, :], xn3[:, dc, 0:512],
                         start=(dc == 0), stop=(dc == DMC - 1))
                last = t.matmul(PS[5][0:1, :], s_ones_c_bf[:, :], xn3[:, dc, 512:1024],
                                start=(dc == 0), stop=(dc == DMC - 1))
            return last
        step("tensor", rs_t)

        def rs_v(v):
            v.tensor_copy(row0[0:1, 0:512], PS[6][0:1, :])
            v.tensor_copy(row0[0:1, 512:1024], PS[5][0:1, :])
            v.tensor_copy(rowb0, row0[0:1, 0:1024])
            v.tensor_sub(s_big1[0:1, 0:1024], row0[0:1, 0:1024], rowb0)
            return v.tensor_copy(rowb1, s_big1[0:1, 0:1024])
        step("vector", rs_v)

        mid11 = smb.rearrange("p (q t) -> p q t", t=T)
        for sup in range(4):
            for hc_i in range(11):
                hc = sup * 11 + hc_i

                def qwl(E, dma, hc=hc):
                    dma(s_wbuf0.rearrange("p (dc j) -> p dc j", j=P),
                        dram["qg_T"][:, hc * P:(hc + 1) * P].rearrange("(dc p) j -> p dc j", p=P))
                    dma(s_wbuf1.rearrange("p (dc j) -> p dc j", j=P),
                        dram["qu_T"][:, hc * P:(hc + 1) * P].rearrange("(dc p) j -> p dc j", p=P))
                step("gpsimd_dma", qwl, ndma=2)

                def qmm(t, hc=hc):
                    bg = s_wbuf0.rearrange("p (dc j) -> p dc j", j=P)
                    bu = s_wbuf1.rearrange("p (dc j) -> p dc j", j=P)
                    for half in range(2):
                        for dc in range(DMC):
                            t.matmul(PS[0 + half][:, :], bg[:, dc, :],
                                     xn3[:, dc, half * 512:(half + 1) * 512],
                                     start=(dc == 0), stop=False)
                        t.matmul(PS[0 + half][:, :], s_m128[:, :],
                                 rowb0[0:1, half * 512:(half + 1) * 512], start=False, stop=False)
                        t.matmul(PS[0 + half][:, :], s_m128[:, :],
                                 rowb1[0:1, half * 512:(half + 1) * 512], start=False, stop=True)
                        for dc in range(DMC):
                            t.matmul(PS[2 + half][:, :], bu[:, dc, :],
                                     xn3[:, dc, half * 512:(half + 1) * 512],
                                     start=(dc == 0), stop=False)
                        t.matmul(PS[2 + half][:, :], s_m128[:, :],
                                 rowb0[0:1, half * 512:(half + 1) * 512], start=False, stop=False)
                        last = t.matmul(PS[2 + half][:, :], s_m128[:, :],
                                        rowb1[0:1, half * 512:(half + 1) * 512],
                                        start=False, stop=True)
                    return last
                step("tensor", qmm)

                def qsil(s, hc=hc):
                    for half in range(2):
                        last = s.activation(s_big0[:, half * 512:(half + 1) * 512],
                                            PS[0 + half][:, :], AF.Silu,
                                            scale=s_sgc[:, hc:hc + 1])
                    return last
                step("scalar", qsil)

                def qmid(v, hc=hc, hc_i=hc_i):
                    for half in range(2):
                        last = v.scalar_tensor_tensor(
                            mid11[:, hc_i, half * 512:(half + 1) * 512],
                            PS[2 + half][:, :], s_suc[:, hc:hc + 1],
                            s_big0[:, half * 512:(half + 1) * 512], ALU.mult, ALU.mult)
                    return last
                step("vector", qmid)

            def rs2t(t, sup=sup):
                for hc_i in range(11):
                    t.matmul(PS[7][0:1, :], s_ones_c_bf[:, :], mid11[:, hc_i, 0:512],
                             start=(sup == 0 and hc_i == 0), stop=(sup == 3 and hc_i == 10))
                    last = t.matmul(PS[4][0:1, :], s_ones_c_bf[:, :], mid11[:, hc_i, 512:1024],
                                    start=(sup == 0 and hc_i == 0), stop=(sup == 3 and hc_i == 10))
                return last
            step("tensor", rs2t)

            for dm in range(DMC):
                def qdw(E, dma, sup=sup, dm=dm):
                    dma(s_wbuf0[:, 0:11 * P].rearrange("p (q j) -> p q j", j=P),
                        dram["qd_T"][sup * 11 * P:(sup + 1) * 11 * P, dm * P:(dm + 1) * P]
                        .rearrange("(q p) j -> p q j", p=P))
                step("gpsimd_dma", qdw, ndma=1)

                def qdmm(t, dm=dm):
                    b3 = s_wbuf0[:, 0:11 * P].rearrange("p (q j) -> p q j", j=P)
                    for half in range(2):
                        for q in range(11):
                            last = t.matmul(PS[2 + half][:, :], b3[:, q, :],
                                            mid11[:, q, half * 512:(half + 1) * 512],
                                            start=(q == 0), stop=(q == 10))
                    return last
                step("tensor", qdmm)

                def qdadd(v, dm=dm):
                    for half in range(2):
                        last = v.scalar_tensor_tensor(
                            s_xacc[:, dm * T + half * 512: dm * T + (half + 1) * 512],
                            PS[2 + half][:, :], s_sdc[:, dm:dm + 1],
                            s_xacc[:, dm * T + half * 512: dm * T + (half + 1) * 512],
                            ALU.mult, ALU.add)
                    return last
                step("vector", qdadd)

        def rs2row(v):
            v.tensor_copy(row0[0:1, 0:512], PS[7][0:1, :])
            return v.tensor_copy(row0[0:1, 512:1024], PS[4][0:1, :])
        step("vector", rs2row)

        def rs2rep(t):
            t.matmul(PS[6][:, :], s_ones_r_f[:, :], row0[0:1, 0:512], start=True, stop=True)
            return t.matmul(PS[5][:, :], s_ones_r_f[:, :], row0[0:1, 512:1024],
                            start=True, stop=True)
        step("tensor", rs2rep)

        def rs2corr(v):
            for dm in range(DMC):
                for half in range(2):
                    bank = PS[6] if half == 0 else PS[5]
                    last = v.scalar_tensor_tensor(
                        s_xacc[:, dm * T + half * 512: dm * T + (half + 1) * 512],
                        bank[:, :], s_sdm128[:, dm:dm + 1],
                        s_xacc[:, dm * T + half * 512: dm * T + (half + 1) * 512],
                        ALU.mult, ALU.add)
            return last
        step("vector", rs2corr)

        for i in range(NT):
            for dc in range(DMC):
                step("tensor", lambda t, i=i, dc=dc:
                     t.matmul(PS[dc % 2][:, 0:P], s_xacc[:, dc * T + i * P: dc * T + (i + 1) * P],
                              s_idf[:], is_transpose=True, start=True, stop=True))
                step("vector", lambda v, i=i, dc=dc:
                     v.tensor_copy(s_big0[:, dc * P:(dc + 1) * P], PS[dc % 2][:, 0:P]))
            step("scalar_dma", lambda E, dma, i=i:
                 dma(dram["y"][i * P:(i + 1) * P, :], s_big0[:, 0:DM]), ndma=1)

        for nm, outn in DEBUG_COPY:
            step("sync", lambda E, dma, nm=nm, outn=outn:
                 dma(dram[outn], dram[nm]), ndma=1)

        cfin = dict(C)

        def fin(sy):
            if chain[0] is not None:
                sy.wait_ge(SEMS[chain[0][0]], chain[0][1])
            for nm in ("qc", "qs", "qg"):
                if cfin[nm] > 0:
                    sy.wait_ge(SEMS[nm], cfin[nm])
        block.sync(fin)

    return nc



def _prep_consts():
    c = {}
    c["id_f32"] = np.eye(P, dtype=np.float32)
    c["id_bf16"] = np.eye(P, dtype=np.float32)
    tri = (np.arange(P)[:, None] <= np.arange(P)[None, :]).astype(np.float32)
    c["tri_bf16"] = tri
    c["tri_scaled"] = tri * 0.125
    selrep = np.zeros((P, N * P), np.float32)
    selred = np.zeros((P, N * P), np.float32)
    for m in range(N):
        for j in range(P):
            selrep[8 * m + (j % 8), m * P + j] = 1.0
        for p_ in range(P):
            selred[p_, m * P + 8 * m + (p_ % 8)] = 1.0
    c["selrep"] = selrep
    c["selred"] = selred
    sB = np.zeros((32, P), np.float32)
    sC = np.zeros((32, P), np.float32)
    for j in range(P):
        sB[j // 8, j] = 1.0
        sC[16 + j // 8, j] = 1.0
    c["sB"] = sB
    c["sC"] = sC
    c["ones_c_bf"] = np.ones((P, 1), np.float32)
    c["ones_c_f32"] = np.ones((P, 1), np.float32)
    c["ones_r_bf"] = np.ones((1, P), np.float32)
    c["ones_r_f32"] = np.ones((1, P), np.float32)
    c["m128_r"] = np.full((1, P), -128.0, np.float32)
    return c


_BF16_NAMES = {"id_bf16", "tri_bf16", "tri_scaled", "selrep", "selred",
               "ones_c_bf", "ones_r_bf", "m128_r"}


def kernel(**inputs):
    import ml_dtypes
    inp = {k: np.asarray(v) for k, v in inputs.items()}

    x = np.asarray(inp["L0_attn_hidden_states"], np.float32)
    w_inT = np.ascontiguousarray(np.asarray(inp["L0_attn_in_proj_w"], np.float32).T)
    conv_w = np.asarray(inp["L0_attn_conv_w"], np.float32)
    conv_b = np.asarray(inp["L0_attn_conv_b"], np.float32)
    w_xpT = np.ascontiguousarray(np.asarray(inp["L0_attn_x_proj_w"], np.float32).T)
    w_dtT = np.ascontiguousarray(np.asarray(inp["L0_attn_dt_proj_w"], np.float32).T)
    dt_b = np.asarray(inp["L0_attn_dt_proj_b"], np.float32)
    A_log = np.asarray(inp["L0_attn_A_log"], np.float32)
    D_ = np.asarray(inp["L0_attn_D"], np.float32)
    w_outT = np.ascontiguousarray(np.asarray(inp["L0_attn_out_proj_w"], np.float32).T)

    consts = _prep_consts()
    rng = np.random.default_rng()
    magic_base = int(rng.integers(1 << 20, 1 << 30))
    magic = np.zeros((1, 40), np.int32)
    for k in range(5):
        magic[0, 8 * k:8 * k + 8] = magic_base + k
    magic_cmp = np.concatenate([np.arange(5, dtype=np.int32) + magic_base,
                                np.zeros(3, np.int32)])[None, :]

    def colsify(vec, nch):
        return np.ascontiguousarray(np.asarray(vec, np.float32).reshape(nch, P).T)

    nc = build_program()

    shared_f32 = {
        "w_inT": w_inT, "w_outT": w_outT,
        "moe_wg": np.asarray(inp["L0_ffn_Wg"], np.float32).reshape(E_EXP * DM, F_MOE),
        "moe_wu": np.asarray(inp["L0_ffn_Wu"], np.float32).reshape(E_EXP * DM, F_MOE),
        "moe_wd": np.asarray(inp["L0_ffn_Wd"], np.float32).reshape(E_EXP * F_MOE, DM),
        "moe_ga": np.asarray(inp["L0_ffn_Ga"], np.float32).reshape(NL * DM, R_LR),
        "moe_gb": np.asarray(inp["L0_ffn_Gb"], np.float32).reshape(NL * R_LR, F_MOE),
        "moe_ua": np.asarray(inp["L0_ffn_Ua"], np.float32).reshape(NL * DM, R_LR),
        "moe_ub": np.asarray(inp["L0_ffn_Ub"], np.float32).reshape(NL * R_LR, F_MOE),
        "moe_da": np.asarray(inp["L0_ffn_Da"], np.float32).reshape(NL * F_MOE, R_LR),
        "moe_db": np.asarray(inp["L0_ffn_Db"], np.float32).reshape(NL * R_LR, DM),
        "wr": np.asarray(inp["L0_ffn_Wr"], np.float32),
        "wq1": np.asarray(inp["L1_attn_Wq"], np.float32),
        "wk1": np.asarray(inp["L1_attn_Wk"], np.float32),
        "wv1": np.asarray(inp["L1_attn_Wv"], np.float32),
        "wg1": np.asarray(inp["L1_attn_Wg"], np.float32),
        "ws1": np.asarray(inp["L1_attn_Ws"], np.float32),
        "wo1": np.asarray(inp["L1_attn_Wo"], np.float32),
    }
    qg_T = np.ascontiguousarray(np.asarray(inp["L1_ffn_wq_gate"], np.int32).T)
    qu_T = np.ascontiguousarray(np.asarray(inp["L1_ffn_wq_up"], np.int32).T)
    qd_T = np.ascontiguousarray(np.asarray(inp["L1_ffn_wq_down"], np.int32).T)

    in_maps = []
    for c in range(8):
        b, role = c // 2, c % 2
        dlo = role * DIH
        m = dict(shared_f32)
        m["qg_T"] = qg_T
        m["qu_T"] = qu_T
        m["qd_T"] = qd_T
        m["x_own"] = np.ascontiguousarray(x[b, role * T:(role + 1) * T, :])
        cw = conv_w[dlo:dlo + DIH]
        m["conv_w_loc"] = np.ascontiguousarray(
            cw.reshape(DIHC, P, 4).transpose(1, 0, 2).reshape(P, DIHC * 4))
        m["conv_b_loc"] = colsify(conv_b[dlo:dlo + DIH], DIHC)
        m["w_xpT_loc"] = np.ascontiguousarray(w_xpT[dlo:dlo + DIH])
        m["w_dtT_loc"] = np.ascontiguousarray(w_dtT[:, dlo:dlo + DIH])
        m["dtb_loc"] = colsify(dt_b[dlo:dlo + DIH], DIHC)
        kk = np.arange(DIHC * N)
        pp = np.arange(P)
        m["alog_lanes"] = np.ascontiguousarray(
            A_log[dlo + 8 * kk[None, :] + (pp[:, None] % 8), pp[:, None] // 8]).astype(np.float32)
        m["d_loc"] = colsify(D_[dlo:dlo + DIH], DIHC)
        m["temp"] = np.asarray(inp["L0_ffn_temperature"], np.float32).reshape(1, 1)
        m["gnw"] = np.asarray(inp["L1_attn_g_norm_weight"], np.float32).reshape(P, 1)
        m["sg_cols"] = colsify(inp["L1_ffn_s_gate"], HIDC)
        m["su_cols"] = colsify(inp["L1_ffn_s_up"], HIDC)
        m["sd_cols"] = colsify(inp["L1_ffn_s_down"], DMC)
        for k_, v_ in consts.items():
            m[k_] = v_.astype(ml_dtypes.bfloat16) if k_ in _BF16_NAMES else v_
        m["role_in"] = np.array([[role]], np.int32)
        m["magic"] = magic
        m["magic_cmp"] = magic_cmp
        in_maps.append(m)

    res = run_bass_kernel_spmd(nc, in_maps, core_ids=list(range(8)))
    global LAST_RESULTS
    LAST_RESULTS = res
    out = np.zeros((4, TT, DM), np.float32)
    for c in range(8):
        b, role = c // 2, c % 2
        out[b, role * T:(role + 1) * T, :] = res.results[c]["y"]
    return out


# version 03
# speedup vs baseline: 1.6047x; 1.6047x over previous
import numpy as np
import sys
sys.path.insert(0, "/opt/trn_rl_repo")
import concourse.bass as bass
import concourse.mybir as mybir
from concourse.bass_utils import run_bass_kernel_spmd
from contextlib import ExitStack

F32 = mybir.dt.float32
BF16 = mybir.dt.bfloat16
I32 = mybir.dt.int32
AF = mybir.ActivationFunctionType
ALU = mybir.AluOpType
AX = mybir.AxisListType

P = 128
DM = 2048
DMC = DM // P
DI = 4096
DIH = DI // 2
DIHC = DIH // P
T = 1024
TT = 2048
N = 16
E_EXP = 8
F_MOE = 2048
FC_MOE = F_MOE // P
R_LR = 64
NL = 2
H = 16
HH = 8
DK = 64
DV = 128
M_ABC = 64
HID = 5632
HIDC = HID // P
GOLDEN_CENTER = 1.0 / np.e
GOLDEN_LOWER = 0.5 - np.log(4.0 / 3.0)
GOLDEN_UPPER = 0.5
EPS_LN = 1e-5
EPS_RMS = 1e-5

DEBUG_COPY = []
LAST_RESULTS = None



def build_program():
    nc = bass.Bass("TRN2", target_bir_lowering=False)
    dram = {}

    def din(name, shape, dt=F32):
        dram[name] = nc.dram_tensor(name, shape, dt, kind="ExternalInput").ap()

    def dout(name, shape, dt=F32):
        dram[name] = nc.dram_tensor(name, shape, dt, kind="ExternalOutput").ap()

    def dint(name, shape, dt=F32, shared=False):
        if shared:
            dram[name] = nc.dram_tensor(name, shape, dt, addr_space="Shared").ap()
        else:
            dram[name] = nc.dram_tensor(name, shape, dt).ap()

    din("x_own", [T, DM])
    din("w_inT", [DM, 2 * DI])
    din("conv_w_loc", [P, DIHC * 4])
    din("conv_b_loc", [P, DIHC])
    din("w_xpT_loc", [DIH, 160])
    din("w_dtT_loc", [128, DIH])
    din("dtb_loc", [P, DIHC])
    din("alog_lanes", [P, DIHC * N])
    din("d_loc", [P, DIHC])
    din("w_outT", [DI, DM])
    din("moe_wg", [E_EXP * DM, F_MOE])
    din("moe_wu", [E_EXP * DM, F_MOE])
    din("moe_wd", [E_EXP * F_MOE, DM])
    din("moe_ga", [NL * DM, R_LR])
    din("moe_gb", [NL * R_LR, F_MOE])
    din("moe_ua", [NL * DM, R_LR])
    din("moe_ub", [NL * R_LR, F_MOE])
    din("moe_da", [NL * F_MOE, R_LR])
    din("moe_db", [NL * R_LR, DM])
    din("wr", [DM, E_EXP])
    din("temp", [1, 1])
    din("wq1", [DM, H * DK])
    din("wk1", [DM, H * DK])
    din("wv1", [DM, H * DV])
    din("wg1", [DM, H * DV])
    din("ws1", [DM, H * M_ABC])
    din("wo1", [H * DV, DM])
    din("gnw", [P, 1])
    din("qg_T", [DM, HID], I32)
    din("qu_T", [DM, HID], I32)
    din("qd_T", [HID, DM], I32)
    din("sg_cols", [P, HIDC])
    din("su_cols", [P, HIDC])
    din("sd_cols", [P, DMC])
    din("id_f32", [P, P])
    din("id_bf16", [P, P], BF16)
    din("tri_bf16", [P, P], BF16)
    din("tri_scaled", [P, P], BF16)
    din("selrep", [P, N * P], BF16)
    din("selred", [P, N * P], BF16)
    din("sB", [32, P])
    din("sC", [32, P])
    din("ones_c_bf", [P, 1], BF16)
    din("ones_c_f32", [P, 1])
    din("ones_r_bf", [1, P], BF16)
    din("ones_r_f32", [1, P])
    din("m128_r", [1, P], BF16)
    din("role_in", [1, 1], I32)
    din("magic", [1, 40], I32)
    din("magic_cmp", [1, 8], I32)

    dout("y", [T, DM])

    dint("xc_dram", [DI, T], BF16)
    dint("sz_dram", [DI, T], BF16)
    dint("xc_scan", [DIH, TT], BF16)
    dint("sz_scan", [DIH, TT], BF16)
    dint("xt_scan", [DIH, TT], BF16)
    dint("ya_loc", [DIH, TT], BF16)
    dint("dt_dbg", [DIH, TT], BF16)
    dint("scan_dbg", [P, 3072], F32)
    dint("ya_own", [DI, T], BF16)
    dint("bc_dram", [32, TT], F32)
    dint("attn_loc", [7168, T], BF16)
    dint("attn_q", [HH * DK, TT], BF16)
    dint("attn_k", [HH * DK, TT], BF16)
    dint("attn_s", [HH * M_ABC, TT], BF16)
    dint("attn_v", [HH * DV, TT], BF16)
    dint("attn_g", [HH * DV, TT], BF16)
    dint("o_own", [H * DV, T], BF16)
    for ab in "ab":
        dint(f"sh_e1_{ab}", [DI, T], BF16, shared=True)
        dint(f"sh_e2_{ab}", [160, TT], F32, shared=True)
        dint(f"sh_e3_{ab}", [DIH, T], BF16, shared=True)
        dint(f"sh_e4_{ab}", [3584, T], BF16, shared=True)
        dint(f"sh_e5_{ab}", [HH * DV, T], BF16, shared=True)
        for k in range(5):
            dint(f"fl{k}_{ab}", [1, 8], I32, shared=True)
    for nm, outn in DEBUG_COPY:
        shp = {"xc_dram": [DI, T], "sz_dram": [DI, T], "xc_scan": [DIH, TT],
               "sz_scan": [DIH, TT], "xt_scan": [DIH, TT], "ya_loc": [DIH, TT],
               "ya_own": [DI, T], "attn_loc": [7168, T], "o_own": [H * DV, T],
               "dt_dbg": [DIH, TT], "scan_dbg": [P, 3072],
               "attn_q": [HH * DK, TT], "attn_s": [HH * M_ABC, TT]}[nm]
        dout(outn, shp, F32 if nm == "scan_dbg" else BF16)

    with ExitStack() as es:
        def sb(name, shape, dt=F32):
            return es.enter_context(nc.sbuf_tensor(name, shape, dt))

        def psum(name, shape, dt=F32):
            return es.enter_context(nc.psum_tensor(name, shape, dt))

        s_xacc = sb("s_xacc", [P, DMC * T])
        s_xn = sb("s_xn", [P, DMC * T], BF16)
        s_mid = sb("s_mid", [P, FC_MOE * T], BF16)
        s_big0 = sb("s_big0", [P, 2048])
        s_big1 = sb("s_big1", [P, 2304])
        s_big2 = sb("s_big2", [P, 2048])
        s_big3 = sb("s_big3", [P, 2048])
        s_big4 = sb("s_big4", [P, 2048])
        s_wbuf0 = sb("s_wbuf0", [P, DM], BF16)
        s_wbuf1 = sb("s_wbuf1", [P, DM], BF16)
        s_wbig = sb("s_wbig", [P, 6144], BF16)
        s_lr = sb("s_lr", [P, DM], BF16)
        s_rows = sb("s_rows", [1, 2048])
        s_wr = sb("s_wr", [P, P])
        s_hcarry = sb("s_hcarry", [P, DIHC * N])
        s_idf = sb("s_idf", [P, P])
        s_idb = sb("s_idb", [P, P], BF16)
        s_tri = sb("s_tri", [P, P], BF16)
        s_sB = sb("s_sB", [32, P])
        s_sC = sb("s_sC", [32, P])
        s_ones_c_bf = sb("s_ones_c_bf", [P, 1], BF16)
        s_ones_c_f = sb("s_ones_c_f", [P, 1])
        s_ones_r_bf = sb("s_ones_r_bf", [1, P], BF16)
        s_ones_r_f = sb("s_ones_r_f", [1, P])
        s_m128 = sb("s_m128", [1, P], BF16)
        s_role = sb("s_role", [1, 1], I32)
        s_magic = sb("s_magic", [1, 40], I32)
        s_mcmp = sb("s_mcmp", [1, 8], I32)
        s_gnw = sb("s_gnw", [P, 1])
        s_temp = sb("s_temp", [1, 1])
        s_rtemp = sb("s_rtemp", [P, 2])
        s_ceps = s_rtemp[:, 1:2]
        s_convw = sb("s_convw", [P, DIHC * 4])
        s_convb = sb("s_convb", [P, DIHC])
        s_dtb = sb("s_dtb", [P, DIHC])
        s_alanes = sb("s_alanes", [P, DIHC * N])
        s_dloc = sb("s_dloc", [P, DIHC])
        s_sgc = sb("s_sgc", [P, HIDC])
        s_suc = sb("s_suc", [P, HIDC])
        s_sdc = sb("s_sdc", [P, DMC])
        s_sdm128 = sb("s_sdm128", [P, DMC])
        s_flag = sb("s_flag", [1, 8], I32)

        s_logits = s_lr.bitcast(F32)[64:64 + E_EXP, 0:T]

        row0 = s_rows[0:1, :]
        rowb0 = s_rows.bitcast(BF16)[0:1, 2048:3072]
        rowb1 = s_rows.bitcast(BF16)[0:1, 3072:4096]
        s_crow = s_rows.bitcast(BF16)[0:1, 0:64]
        s_wfm = s_big3[0:8, 1024:2048]

        PS = [psum(f"PS{i}", [P, 512]) for i in range(8)]
        PSB = [p.bitcast(BF16) for p in PS]

        qg = es.enter_context(nc.semaphore("qg"))
        qs = es.enter_context(nc.semaphore("qs"))
        qc = es.enter_context(nc.semaphore("qc"))
        tS = es.enter_context(nc.semaphore("tS"))
        vS = es.enter_context(nc.semaphore("vS"))
        sS = es.enter_context(nc.semaphore("sS"))
        spin = es.enter_context(nc.semaphore("spin"))
        SEMS = {"qg": qg, "qs": qs, "qc": qc, "tS": tS, "vS": vS, "sS": sS}
        ENGSEM = {"tensor": "tS", "vector": "vS", "scalar": "sS"}
        C = {k: 0 for k in SEMS}
        chain = [None]

        block = es.enter_context(nc.Block())

        _SYNCED_OPS = {
            "tensor_tensor", "tensor_add", "tensor_sub", "tensor_mul", "tensor_max",
            "tensor_copy", "tensor_scalar", "tensor_single_scalar",
            "tensor_scalar_add", "tensor_scalar_sub", "tensor_scalar_mul",
            "tensor_scalar_max", "tensor_scalar_min", "tensor_reduce",
            "reduce_sum", "reduce_max", "reciprocal", "memset", "memzero",
            "tensor_tensor_scan", "scalar_tensor_tensor", "select",
            "activation", "copy", "add", "mul", "sqrt", "square", "sign",
        }

        class _SeqProxy:
            def __init__(self, E, sem, base):
                self._E = E
                self._sem = sem
                self._n = [base]

            def __getattr__(self, name):
                fnb = getattr(self._E, name)
                if name in _SYNCED_OPS:
                    def wrapped(*a, **k):
                        if self._n[0] > 0:
                            self._E.wait_ge(self._sem, self._n[0])
                        inst = fnb(*a, **k)
                        inst.then_inc(self._sem, 1)
                        self._n[0] += 1
                        return inst
                    return wrapped
                return fnb

        def _mkwaits(deps):
            if deps is None:
                dl = [chain[0]] if chain[0] is not None else []
            else:
                dl = [d for d in deps if d is not None]
            md = {}
            for sname, cntv in dl:
                if cntv > 0:
                    md[sname] = max(md.get(sname, 0), cntv)
            return md

        def barrier_deps():
            return [(s, C[s]) for s in SEMS if C[s] > 0]

        def step(eng, fn, ndma=None, deps=None):
            md = _mkwaits(deps)
            if eng in ("sync", "gpsimd_dma", "scalar_dma"):
                qname = {"sync": "qs", "gpsimd_dma": "qg", "scalar_dma": "qc"}[eng]
                qsem = SEMS[qname]
                blkfn = {"sync": block.sync, "gpsimd_dma": block.gpsimd,
                         "scalar_dma": block.scalar}[eng]

                def body(E):
                    for sname, cntv in md.items():
                        E.wait_ge(SEMS[sname], cntv)

                    def dma(dst, src):
                        E.dma_start(out=dst, in_=src).then_inc(qsem, 16)
                    fn(E, dma)
                blkfn(body)
                C[qname] += 16 * ndma
                chain[0] = (qname, C[qname])
            elif eng == "tensor":
                semn = ENGSEM[eng]

                def body(E):
                    for sname, cntv in md.items():
                        E.wait_ge(SEMS[sname], cntv)
                    last = fn(E)
                    last.then_inc(SEMS[semn], 1)
                getattr(block, eng)(body)
                C[semn] += 1
                chain[0] = (semn, C[semn])
            else:
                semn = ENGSEM[eng]
                base = C[semn]
                holder = [None]

                def body(E):
                    for sname, cntv in md.items():
                        E.wait_ge(SEMS[sname], cntv)
                    px = _SeqProxy(E, SEMS[semn], base)
                    fn(px)
                    holder[0] = px._n[0]
                getattr(block, eng)(body)
                assert holder[0] is not None and holder[0] > base, "empty group"
                C[semn] = holder[0]
                chain[0] = (semn, C[semn])
            return chain[0]

        _scope_state = [None]

        def phase(name):
            if _scope_state[0] is not None:
                pn, pid = _scope_state[0]
                nc.leave_named_scope(pn, pid, False)
                _scope_state[0] = None
            if name is not None:
                sid, _ = nc.enter_named_scope(name, False)
                _scope_state[0] = (name, sid)

        xn3 = s_xn.rearrange("p (dc t) -> p dc t", t=T)
        smf = s_mid.bitcast(F32)
        smb = s_mid

        phase("ph0_consts")
        const_pairs = [
            (s_idf, "id_f32"), (s_idb, "id_bf16"), (s_tri, "tri_bf16"),
            (s_sB, "sB"), (s_sC, "sC"),
            (s_ones_c_bf, "ones_c_bf"), (s_ones_c_f, "ones_c_f32"),
            (s_ones_r_bf, "ones_r_bf"), (s_ones_r_f, "ones_r_f32"),
            (s_m128, "m128_r"), (s_role, "role_in"), (s_magic, "magic"),
            (s_mcmp, "magic_cmp"), (s_gnw, "gnw"), (s_temp, "temp"),
            (s_convw, "conv_w_loc"), (s_convb, "conv_b_loc"), (s_dtb, "dtb_loc"),
            (s_alanes, "alog_lanes"), (s_dloc, "d_loc"), (s_sgc, "sg_cols"),
            (s_suc, "su_cols"), (s_sdc, "sd_cols"), (s_wr, None),
        ]

        def ph0(E, dma):
            for dst, nm in const_pairs:
                if nm is None:
                    continue
                dma(dst[:], dram[nm])
            dma(s_wr.rearrange("p (dc e) -> p dc e", e=E_EXP)[:, 0:DMC, :],
                dram["wr"].rearrange("(dc p) e -> p dc e", p=P))
        step("gpsimd_dma", ph0, ndma=len(const_pairs))

        step("vector", lambda v: v.memset(s_ceps, EPS_LN))
        step("scalar", lambda s: s.activation(s_alanes[:], s_alanes[:], AF.Exp))
        step("vector", lambda v: v.tensor_scalar_mul(s_alanes[:], s_alanes[:], -1.0))
        step("vector", lambda v: v.tensor_scalar_mul(s_sdm128[:], s_sdc[:], -128.0))
        step("vector", lambda v: v.reciprocal(s_rtemp[0:1, 0:1], s_temp[0:1, 0:1]))
        step("tensor", lambda t: t.matmul(PS[0][:, 0:1], s_ones_r_f[:], s_rtemp[0:1, 0:1],
                                          start=True, stop=True))
        step("vector", lambda v: v.tensor_copy(s_rtemp[:, 0:1], PS[0][:, 0:1]))

        phase("ph1_ln0")
        NT = T // P
        for i in range(NT):
            step("sync", lambda E, dma, i=i:
                 dma(s_big0[:, 0:DM], dram["x_own"][i * P:(i + 1) * P, :]), ndma=1)
            step("vector", lambda v: v.reduce_sum(s_big3[:, 0:1], s_big0[:, 0:DM], AX.X))
            step("scalar", lambda s: s.activation(s_big2[:, 0:DM], s_big0[:, 0:DM], AF.Square))

            def stats2(v):
                v.reduce_sum(s_big3[:, 1:2], s_big2[:, 0:DM], AX.X)
                v.tensor_scalar_mul(s_big3[:, 2:3], s_big3[:, 0:1], -1.0 / DM)
                v.tensor_scalar_mul(s_big3[:, 3:4], s_big3[:, 1:2], 1.0 / DM)
                v.tensor_mul(s_big3[:, 4:5], s_big3[:, 2:3], s_big3[:, 2:3])
                return v.tensor_sub(s_big3[:, 5:6], s_big3[:, 3:4], s_big3[:, 4:5])
            step("vector", stats2)

            def rstd(s):
                s.activation(s_big3[:, 6:7], s_big3[:, 5:6], AF.Ln, bias=s_ceps)
                return s.activation(s_big3[:, 7:8], s_big3[:, 6:7], AF.Exp, scale=-0.5)
            step("scalar", rstd)
            step("vector", lambda v: v.tensor_scalar(s_big1[:, 0:DM], s_big0[:, 0:DM],
                                                     s_big3[:, 2:3], s_big3[:, 7:8],
                                                     ALU.add, ALU.mult))
            for dc in range(DMC):
                par = dc % 2

                def tr(t, dc=dc, par=par):
                    t.matmul(PS[par][:, 0:P], s_big0[:, dc * P:(dc + 1) * P], s_idf[:],
                             is_transpose=True, start=True, stop=True)
                    return t.matmul(PS[2 + par][:, 0:P], s_big1[:, dc * P:(dc + 1) * P],
                                    s_idf[:], is_transpose=True, start=True, stop=True)
                step("tensor", tr)

                def ev(v, dc=dc, par=par, i=i):
                    v.tensor_copy(s_xacc[:, dc * T + i * P: dc * T + (i + 1) * P], PS[par][:, 0:P])
                    v.tensor_copy(xn3[:, dc, i * P:(i + 1) * P], PS[2 + par][:, 0:P])
                    return v.tensor_copy(s_big2[:, dc * P:(dc + 1) * P], PS[2 + par][:, 0:P])
                step("vector", ev)

            def router_mm(t, i=i):
                for dc in range(DMC):
                    last = t.matmul(PS[4][0:E_EXP, 0:P],
                                    s_wr.rearrange("p (dc e) -> p dc e", e=E_EXP)[:, dc, :],
                                    s_big2[:, dc * P:(dc + 1) * P],
                                    start=(dc == 0), stop=(dc == DMC - 1))
                return last
            step("tensor", router_mm)
            step("vector", lambda v, i=i: v.tensor_copy(s_logits[:, i * P:(i + 1) * P],
                                                        PS[4][0:E_EXP, 0:P]))

        phase("ph2_inproj")
        K2 = {}
        bd2 = barrier_deps()
        for fc in range(64):
            fb2 = fc % 2
            wb2 = s_wbuf0 if fb2 == 0 else s_wbuf1
            eb2 = s_big0.bitcast(BF16)[:, fb2 * 2048:fb2 * 2048 + 2048]

            def wload(E, dma, fc=fc, wb2=wb2):
                dma(wb2.rearrange("p (dc j) -> p dc j", j=P),
                    dram["w_inT"][:, fc * P:(fc + 1) * P].rearrange("(dc p) j -> p dc j", p=P))
            K2['w', fc] = step("gpsimd_dma", wload, ndma=1,
                deps=(bd2 if fc < 2 else [K2['t', fc - 2]]))

            def mm(t, fc=fc, wb2=wb2, fb2=fb2):
                b3 = wb2.rearrange("p (dc j) -> p dc j", j=P)
                for half in range(2):
                    for dc in range(DMC):
                        last = t.matmul(PS[2 * fb2 + half][:, :], b3[:, dc, :],
                                        xn3[:, dc, half * 512:(half + 1) * 512],
                                        start=(dc == 0), stop=(dc == DMC - 1))
                return last
            K2['t', fc] = step("tensor", mm,
                deps=[K2['w', fc]] + ([K2['e', fc - 2]] if fc >= 2 else bd2))

            if fc < 32:
                def evx(v, eb2=eb2, fb2=fb2):
                    v.tensor_copy(eb2[:, 0:512], PS[2 * fb2][:, :])
                    return v.tensor_copy(eb2[:, 512:1024], PS[2 * fb2 + 1][:, :])
                K2['e', fc] = step("vector", evx,
                    deps=[K2['t', fc]] + ([K2['o', fc - 2]] if fc >= 2 else []))
            else:
                def evz(s, eb2=eb2, fb2=fb2):
                    s.activation(eb2[:, 0:512], PS[2 * fb2][:, :], AF.Silu)
                    return s.activation(eb2[:, 512:1024], PS[2 * fb2 + 1][:, :], AF.Silu)
                K2['e', fc] = step("scalar", evz,
                    deps=[K2['t', fc]] + ([K2['o', fc - 2]] if fc >= 2 else
                                          [K2['o', 30 + fc - 32]] if fc < 34 else []))
            dst2 = (dram["xc_dram"][fc * P:(fc + 1) * P, :] if fc < 32 else
                    dram["sz_dram"][(fc - 32) * P:(fc - 31) * P, :])
            K2['o', fc] = step("sync", lambda E, dma, dst2=dst2, eb2=eb2:
                 dma(dst2, eb2[:, 0:T]), ndma=1, deps=[K2['e', fc]])

        def exchange(tag, k, body_fn):
            pre_a, post_a = body_fn("a")
            pre_b, post_b = body_fn("b")
            assert len(pre_a) == len(pre_b) and len(post_a) == len(post_b)
            n_total = len(pre_a) + 1 + len(post_a)
            prev = chain[0]
            cqs0 = C["qs"]

            def body(sy):
                if prev is not None:
                    sy.wait_ge(SEMS[prev[0]], prev[1])
                role = sy.value_load(s_role[0:1, 0:1])
                mreg = sy.value_load(s_mcmp[0:1, k:k + 1])
                with sy.register(f"ca_{tag}") as ca, sy.register(f"cb_{tag}") as cb:
                    sy.reg_alu(ca, role, 0, ALU.is_equal)
                    sy.reg_alu(cb, role, 1, ALU.is_equal)
                    for ab, creg, pre, post in (("a", ca, pre_a, post_a),
                                                ("b", cb, pre_b, post_b)):
                        other = "b" if ab == "a" else "a"
                        with sy.While(creg):
                            nd = 0
                            for dst, src in pre:
                                sy.dma_start(out=dst, in_=src).then_inc(qs, 16)
                                nd += 16
                            sy.wait_ge(qs, cqs0 + nd)
                            sy.dma_start(out=dram[f"fl{k}_{ab}"],
                                         in_=s_magic[0:1, 8 * k:8 * k + 8]).then_inc(qs, 16)
                            nd += 16
                            sy.wait_ge(qs, cqs0 + nd)
                            with sy.register(f"cw_{tag}_{ab}") as cond:
                                sy.sem_clear(spin)
                                sy.dma_start(out=s_flag[:], in_=dram[f"fl{k}_{other}"]).then_inc(spin, 16)
                                sy.wait_ge(spin, 16)
                                fl = sy.value_load(s_flag[0:1, 0:1])
                                sy.reg_alu(cond, fl, mreg, ALU.not_equal)
                                with sy.While(cond):
                                    sy.sem_clear(spin)
                                    sy.dma_start(out=s_flag[:], in_=dram[f"fl{k}_{other}"]).then_inc(spin, 16)
                                    sy.wait_ge(spin, 16)
                                    fl2 = sy.value_load(s_flag[0:1, 0:1])
                                    sy.reg_alu(cond, fl2, mreg, ALU.not_equal)
                            for dst, src in post:
                                sy.dma_start(out=dst, in_=src).then_inc(qs, 16)
                                nd += 16
                            sy.wait_ge(qs, cqs0 + nd)
                            sy.reg_alu(creg, creg, 0, ALU.mult)
            block.sync(body)
            C["qs"] += 16 * n_total
            chain[0] = ("qs", C["qs"])

        def e1_body(ab):
            me0 = (ab == "a")
            own_lo, own_hi = (0, T) if me0 else (T, TT)
            send = (DIH, DI) if me0 else (0, DIH)
            keep = (0, DIH) if me0 else (DIH, DI)
            other = "b" if ab == "a" else "a"
            po_lo, po_hi = (T, TT) if me0 else (0, T)
            pre = [
                (dram[f"sh_e1_{ab}"][0:DIH, :], dram["xc_dram"][send[0]:send[1], :]),
                (dram[f"sh_e1_{ab}"][DIH:DI, :], dram["sz_dram"][send[0]:send[1], :]),
                (dram["xc_scan"][:, own_lo:own_hi], dram["xc_dram"][keep[0]:keep[1], :]),
                (dram["sz_scan"][:, own_lo:own_hi], dram["sz_dram"][keep[0]:keep[1], :]),
            ]
            post = [
                (dram["xc_scan"][:, po_lo:po_hi], dram[f"sh_e1_{other}"][0:DIH, :]),
                (dram["sz_scan"][:, po_lo:po_hi], dram[f"sh_e1_{other}"][DIH:DI, :]),
            ]
            return pre, post
        phase("exch_e1")
        exchange("e1", 0, e1_body)

        phase("ph3_conv_xproj")
        def wxp_load(E, dma):
            dma(s_wbig[:, 0:DIHC * 160].rearrange("p (dc o) -> p dc o", o=160),
                dram["w_xpT_loc"].rearrange("(dc p) o -> p dc o", p=P))
        step("gpsimd_dma", wxp_load, ndma=1)
        wxp3 = s_wbig[:, 0:DIHC * 160].rearrange("p (dc o) -> p dc o", o=160)
        xc_in = s_big2.bitcast(BF16)
        cw3 = s_convw.rearrange("p (dc j) -> p dc j", j=4)

        for dc in range(DIHC):
            step("sync", lambda E, dma, dc=dc:
                 dma(xc_in[:, 3:3 + TT], dram["xc_scan"][dc * P:(dc + 1) * P, :]), ndma=1)

            def conv(v, dc=dc):
                v.memset(xc_in[:, 0:3], 0.0)
                v.tensor_scalar_mul(s_big1[:, 0:TT], xc_in[:, 0:TT], cw3[:, dc, 0:1])
                v.scalar_tensor_tensor(s_big1[:, 0:TT], xc_in[:, 1:1 + TT], cw3[:, dc, 1:2],
                                       s_big1[:, 0:TT], ALU.mult, ALU.add)
                v.scalar_tensor_tensor(s_big1[:, 0:TT], xc_in[:, 2:2 + TT], cw3[:, dc, 2:3],
                                       s_big1[:, 0:TT], ALU.mult, ALU.add)
                return v.scalar_tensor_tensor(s_big1[:, 0:TT], xc_in[:, 3:3 + TT], cw3[:, dc, 3:4],
                                              s_big1[:, 0:TT], ALU.mult, ALU.add)
            step("vector", conv)
            step("scalar", lambda s, dc=dc:
                 s.activation(s_big3.bitcast(BF16)[:, 0:TT], s_big1[:, 0:TT],
                              AF.Silu, bias=s_convb[:, dc:dc + 1]))
            step("sync", lambda E, dma, dc=dc:
                 dma(dram["xt_scan"][dc * P:(dc + 1) * P, :], s_big3.bitcast(BF16)[:, 0:TT]),
                 ndma=1)

            def xp(t, dc=dc):
                for tg in range(4):
                    t.matmul(PS[tg][:, :], wxp3[:, dc, 0:P],
                             s_big3.bitcast(BF16)[:, tg * 512:(tg + 1) * 512],
                             start=(dc == 0), stop=(dc == DIHC - 1))
                    last = t.matmul(PS[4 + tg][0:32, :], wxp3[:, dc, P:160],
                                    s_big3.bitcast(BF16)[:, tg * 512:(tg + 1) * 512],
                                    start=(dc == 0), stop=(dc == DIHC - 1))
                return last
            step("tensor", xp)

        def xp_evac(v):
            for tg in range(4):
                v.tensor_copy(s_big0[:, tg * 512:(tg + 1) * 512], PS[tg][:, :])
                last = v.tensor_copy(s_big1[0:32, tg * 512:(tg + 1) * 512], PS[4 + tg][0:32, :])
            return last
        step("vector", xp_evac)

        def e2_body(ab):
            other = "b" if ab == "a" else "a"
            pre = [
                (dram[f"sh_e2_{ab}"][0:P, :], s_big0[:, 0:TT]),
                (dram[f"sh_e2_{ab}"][P:160, :], s_big1[0:32, 0:TT]),
            ]
            post = [
                (s_big2[:, 0:TT], dram[f"sh_e2_{other}"][0:P, :]),
                (s_big3[0:32, 0:TT], dram[f"sh_e2_{other}"][P:160, :]),
            ]
            return pre, post
        phase("exch_e2")
        exchange("e2", 1, e2_body)

        s_dtlow = s_big4.bitcast(BF16)

        def sums(v):
            v.tensor_add(s_dtlow[:, 0:TT], s_big0[:, 0:TT], s_big2[:, 0:TT])
            return v.tensor_add(s_big3[0:32, 0:TT], s_big1[0:32, 0:TT], s_big3[0:32, 0:TT])
        step("vector", sums)
        step("sync", lambda E, dma: dma(dram["bc_dram"], s_big3[0:32, 0:TT]), ndma=1)

        phase("ph4_scan")
        def ph4_wload(E, dma):
            dma(s_wbig[:, 0:2048], dram["w_dtT_loc"])
            dma(s_wbig[:, 2048:4096], dram["selrep"])
            dma(s_wbig[:, 4096:6144], dram["selred"])
        step("gpsimd_dma", ph4_wload, ndma=3)
        s_wdt = s_wbig[:, 0:2048]
        selrep3 = s_wbig[:, 2048:4096].rearrange("p (m j) -> p m j", j=P)
        selred3 = s_wbig[:, 4096:6144].rearrange("p (m j) -> p m j", j=P)
        dA_reg = [smf[:, 0:1024], smf[:, 4096:5120]]
        hc_reg = [smb[:, 14336:15360], smb[:, 15360:16384]]
        ya_stg = smb[:, 15360:16384]
        S4 = {}
        dbg_dt = any(nm == "dt_dbg" for nm, _ in DEBUG_COPY)
        dbg_scan = any(nm == "scan_dbg" for nm, _ in DEBUG_COPY)
        bds = barrier_deps()

        for th in range(2):
            tlo = th * 1024
            S4['bl', th] = step("sync", lambda E, dma, tlo=tlo:
                 dma(s_big0[0:32, 0:1024], dram["bc_dram"][:, tlo:tlo + 1024]), ndma=1,
                 deps=(bds if th == 0 else [S4['bt', 0]]))

            def bc_build(t):
                for half in range(2):
                    t.matmul(PS[4 + half][:, :], s_sB[:, :],
                             s_big0[0:32, half * 512:(half + 1) * 512],
                             start=True, stop=True)
                    last = t.matmul(PS[6 + half][:, :], s_sC[:, :],
                                    s_big0[0:32, half * 512:(half + 1) * 512],
                                    start=True, stop=True)
                return last
            S4['bt', th] = step("tensor", bc_build,
                deps=[S4['bl', th]] +
                     ([S4['v1', th - 1, DIHC - 1, N - 1], S4['ya', th - 1, DIHC - 1]]
                      if th else bds))

            def bc_ev(v):
                for half in range(2):
                    v.tensor_copy(smf[:, 5120 + half * 512:5120 + (half + 1) * 512],
                                  PS[4 + half][:, :])
                    last = v.tensor_copy(smf[:, 6144 + half * 512:6144 + (half + 1) * 512],
                                         PS[6 + half][:, :])
                return last
            S4['be', th] = step("vector", bc_ev, deps=[S4['bt', th]])

            for dc in range(DIHC):
                tb = dc % 2
                wbt = s_wbuf0 if tb == 0 else s_wbuf1

                def tile_ld(E, dma, dc=dc, tlo=tlo, wbt=wbt):
                    dma(wbt[:, 0:1024], dram["xt_scan"][dc * P:(dc + 1) * P, tlo:tlo + 1024])
                    dma(wbt[:, 1024:2048], dram["sz_scan"][dc * P:(dc + 1) * P, tlo:tlo + 1024])
                tl_deps = []
                if (th, dc) == (0, 0):
                    tl_deps = bds
                elif (th, dc) == (0, 1):
                    tl_deps = bds
                else:
                    pdc = dc - 2
                    pth = th
                    if pdc < 0:
                        pdc += DIHC
                        pth -= 1
                    tl_deps = [S4['ya', pth, pdc]]
                S4['tl', th, dc] = step("sync", tile_ld, ndma=2, deps=tl_deps)

                def dt_mm(t, dc=dc, tlo=tlo):
                    for half in range(2):
                        last = t.matmul(PS[half][:, :], s_wdt[:, dc * P:(dc + 1) * P],
                                        s_dtlow[:, tlo + half * 512: tlo + (half + 1) * 512],
                                        start=True, stop=True)
                    return last
                dtm_deps = []
                if dc == 0:
                    dtm_deps = [S4['be', th]] if th else bds
                else:
                    dtm_deps = [S4['sv1', th, dc - 1]]
                S4['dt', th, dc] = step("tensor", dt_mm, deps=dtm_deps)

                def sp_v1(v, dc=dc):
                    for half in range(2):
                        v.tensor_scalar_add(smf[:, 3072 + half * 512:3072 + (half + 1) * 512],
                                            PS[half][:, :], s_dtb[:, dc:dc + 1])
                    v.tensor_scalar_mul(smf[:, 4096:5120], smf[:, 3072:4096], -1.0)
                    return v.tensor_max(smf[:, 4096:5120], smf[:, 4096:5120],
                                        smf[:, 3072:4096])
                sv1_deps = [S4['dt', th, dc]]
                if (th, dc) != (0, 0):
                    pdc, pth = (dc - 1, th) if dc else (DIHC - 1, th - 1)
                    sv1_deps += [S4['v1', pth, pdc, N - 1]]
                S4['sv1', th, dc] = step("vector", sp_v1, deps=sv1_deps)

                def sp_s(s):
                    s.activation(smf[:, 4096:5120], smf[:, 4096:5120], AF.Exp, scale=-1.0)
                    return s.activation(smf[:, 4096:5120], smf[:, 4096:5120], AF.Ln, bias=1.0)
                S4['sp', th, dc] = step("scalar", sp_s, deps=[S4['sv1', th, dc]])

                def sp_v2(v, dc=dc, wbt=wbt):
                    v.tensor_scalar_max(smf[:, 0:1024], smf[:, 3072:4096], 0.0)
                    v.tensor_add(s_dtlow[:, 2048:3072], smf[:, 4096:5120], smf[:, 0:1024])
                    return v.tensor_mul(s_dtlow[:, 3072:4096], s_dtlow[:, 2048:3072],
                                        wbt[:, 0:1024])
                spv2_deps = [S4['sp', th, dc], S4['tl', th, dc]]
                if (th, dc) != (0, 0):
                    pdc, pth = (dc - 1, th) if dc else (DIHC - 1, th - 1)
                    spv2_deps += [S4['rp', pth, pdc, N - 1]]
                S4['dtu', th, dc] = step("vector", sp_v2, deps=spv2_deps)
                if dbg_dt:
                    step("sync", lambda E, dma, dc=dc, tlo=tlo:
                         dma(dram["dt_dbg"][dc * P:(dc + 1) * P, tlo:tlo + 1024],
                             s_dtlow[:, 2048:3072]), ndma=1)

                for m in range(N):
                    k = dc * N + m
                    mp = m % 2

                    def rep(t, m=m):
                        for sub in range(2):
                            t.matmul(PS[2 + sub][:, :], selrep3[:, m, :],
                                     s_dtlow[:, 2048 + sub * 512:2048 + (sub + 1) * 512],
                                     start=True, stop=True)
                            last = t.matmul(PS[4 + sub][:, :], selrep3[:, m, :],
                                            s_dtlow[:, 3072 + sub * 512:3072 + (sub + 1) * 512],
                                            start=True, stop=True)
                        return last
                    if m == 0:
                        rp_deps = [S4['dtu', th, dc]]
                        if (th, dc) != (0, 0):
                            pdc, pth = (dc - 1, th) if dc else (DIHC - 1, th - 1)
                            rp_deps += [S4['da', pth, pdc, N - 1],
                                        S4['v1', pth, pdc, N - 1]]
                    else:
                        rp_deps = [S4['da', th, dc, m - 1], S4['v1', th, dc, m - 1]]
                    S4['rp', th, dc, m] = step("tensor", rep, deps=rp_deps)

                    def dA(s, k=k, mp=mp):
                        s.activation(dA_reg[mp][:, 0:512], PS[2][:, :], AF.Exp,
                                     scale=s_alanes[:, k:k + 1])
                        return s.activation(dA_reg[mp][:, 512:1024], PS[3][:, :], AF.Exp,
                                            scale=s_alanes[:, k:k + 1])
                    da_deps = [S4['rp', th, dc, m]]
                    if m >= 2:
                        da_deps += [S4['v1', th, dc, m - 2]]
                    S4['da', th, dc, m] = step("scalar", dA, deps=da_deps)

                    def vmk2(v, k=k, th=th, mp=mp):
                        v.tensor_mul(smf[:, 1024:1536], PS[4][:, :], smf[:, 5120:5632])
                        v.tensor_mul(smf[:, 1536:2048], PS[5][:, :], smf[:, 5632:6144])
                        init = 0.0 if th == 0 else s_hcarry[:, k:k + 1]
                        last = v.tensor_tensor_scan(smf[:, 2048:3072], dA_reg[mp][:, 0:1024],
                                                    smf[:, 1024:2048], init,
                                                    ALU.mult, ALU.add)
                        if th == 0:
                            last = v.tensor_copy(s_hcarry[:, k:k + 1], smf[:, 3071:3072])
                        return last
                    S4['v1', th, dc, m] = step("vector", vmk2,
                        deps=[S4['rp', th, dc, m], S4['da', th, dc, m]])

                    def hCf(v, mp=mp):
                        return v.tensor_mul(hc_reg[mp][:, 0:1024], smf[:, 2048:3072],
                                            smf[:, 6144:7168])
                    v2_deps = [S4['v1', th, dc, m]]
                    if m >= 2:
                        v2_deps += [S4['rd', th, dc, m - 2]]
                    elif (th, dc) != (0, 0) and m == 1:
                        pdc, pth = (dc - 1, th) if dc else (DIHC - 1, th - 1)
                        v2_deps += [S4['rd', pth, pdc, N - 1], S4['yo', pth, pdc]]
                    S4['v2', th, dc, m] = step("vector", hCf, deps=v2_deps)

                    if dbg_scan and th == 0 and dc == 0 and m == 0:
                        step("sync", lambda E, dma:
                             dma(dram["scan_dbg"], smf[:, 0:3072]), ndma=1)

                    def red(t, m=m, mp=mp):
                        for half in range(2):
                            last = t.matmul(PS[6 + half][:, :], selred3[:, m, :],
                                            hc_reg[mp][:, half * 512:(half + 1) * 512],
                                            start=(m == 0), stop=(m == N - 1))
                        return last
                    rd_deps = [S4['v2', th, dc, m]]
                    if m == 0 and (th, dc) != (0, 0):
                        pdc, pth = (dc - 1, th) if dc else (DIHC - 1, th - 1)
                        rd_deps += [S4['ya', pth, pdc]]
                    S4['rd', th, dc, m] = step("tensor", red, deps=rd_deps)

                def ya(v, dc=dc, wbt=wbt):
                    for half in range(2):
                        v.scalar_tensor_tensor(smf[:, 3072 + half * 512:3072 + (half + 1) * 512],
                                               wbt[:, half * 512:(half + 1) * 512],
                                               s_dloc[:, dc:dc + 1],
                                               PS[6 + half][:, :], ALU.mult, ALU.add)
                    return v.tensor_mul(ya_stg[:, 0:1024], smf[:, 3072:4096],
                                        wbt[:, 1024:2048])
                S4['ya', th, dc] = step("vector", ya, deps=[S4['rd', th, dc, N - 1]])
                S4['yo', th, dc] = step("scalar_dma", lambda E, dma, dc=dc, tlo=tlo:
                     dma(dram["ya_loc"][dc * P:(dc + 1) * P, tlo:tlo + 1024],
                         ya_stg[:, 0:1024]), ndma=1, deps=[S4['ya', th, dc]])

        def e3_body(ab):
            me0 = (ab == "a")
            own = (0, T) if me0 else (T, TT)
            myrows = (0, DIH) if me0 else (DIH, DI)
            other = "b" if ab == "a" else "a"
            pr = (DIH, DI) if me0 else (0, DIH)
            pre = [
                (dram[f"sh_e3_{ab}"], dram["ya_loc"][:, (T if me0 else 0):(TT if me0 else T)]),
                (dram["ya_own"][myrows[0]:myrows[1], :], dram["ya_loc"][:, own[0]:own[1]]),
            ]
            post = [(dram["ya_own"][pr[0]:pr[1], :], dram[f"sh_e3_{other}"])]
            return pre, post
        phase("exch_e3")
        exchange("e3", 2, e3_body)

        phase("ph4b_outproj")
        G = {}
        bd4 = barrier_deps()
        ob4 = s_big4.bitcast(BF16)
        for grp in range(16):
            gb_ = grp % 2
            wb4 = s_wbig[:, 0:4096] if gb_ == 0 else ob4[:, 0:4096]
            yb4 = smb[:, 0:2048] if gb_ == 0 else smb[:, 2048:4096]

            def wld(E, dma, grp=grp, wb4=wb4):
                dma(wb4.rearrange("p (j d) -> p j d", d=DM),
                    dram["w_outT"][grp * 256:(grp + 1) * 256, :].rearrange("(j p) d -> p j d", p=P))
            G['w', grp] = step("gpsimd_dma", wld, ndma=1,
                deps=(bd4 if grp < 2 else [G['t', grp - 2, DMC - 1]]))

            def yld(E, dma, grp=grp, yb4=yb4):
                dma(yb4.rearrange("p (j t) -> p j t", t=T),
                    dram["ya_own"][grp * 256:(grp + 1) * 256, :].rearrange("(j p) t -> p j t", p=P))
            G['y', grp] = step("sync", yld, ndma=1,
                deps=(bd4 if grp < 2 else [G['t', grp - 2, DMC - 1]]))

            for dm in range(DMC):
                pb4 = 2 * (dm % 2)

                def opmm(t, dm=dm, wb4=wb4, yb4=yb4, pb4=pb4):
                    w3 = wb4.rearrange("p (j d) -> p j d", d=DM)
                    y3 = yb4.rearrange("p (j t) -> p j t", t=T)
                    for half in range(2):
                        for j in range(2):
                            last = t.matmul(PS[pb4 + half][:, :], w3[:, j, dm * P:(dm + 1) * P],
                                            y3[:, j, half * 512:(half + 1) * 512],
                                            start=(j == 0), stop=(j == 1))
                    return last
                op_deps = [G['w', grp], G['y', grp]]
                if dm >= 2:
                    op_deps += [G['v', grp, dm - 2]]
                elif grp >= 1:
                    op_deps += [G['v', grp - 1, DMC - 2 + dm]]
                else:
                    op_deps += bd4
                G['t', grp, dm] = step("tensor", opmm, deps=op_deps)

                def opadd(v, dm=dm, pb4=pb4):
                    for half in range(2):
                        last = v.scalar_tensor_tensor(
                            s_xacc[:, dm * T + half * 512: dm * T + (half + 1) * 512],
                            PS[pb4 + half][:, :], 1.0,
                            s_xacc[:, dm * T + half * 512: dm * T + (half + 1) * 512],
                            ALU.mult, ALU.add)
                    return last
                G['v', grp, dm] = step("vector", opadd, deps=[G['t', grp, dm]])

        phase("ph5_moe")
        HM = {}
        bdr = barrier_deps()
        for i in range(NT):
            bk = i % 2
            HM['rt', i] = step("tensor", lambda t, i=i, bk=bk:
                 t.matmul(PS[bk][:, 0:E_EXP], s_logits[:, i * P:(i + 1) * P],
                          s_idf[64:64 + E_EXP, 64:64 + E_EXP],
                          is_transpose=True, start=True, stop=True),
                 deps=(bdr if i < 2 else [HM['rv', i - 2]]))
            HM['rv', i] = step("vector", lambda v, i=i, bk=bk:
                 v.tensor_copy(s_big0[:, i * E_EXP:(i + 1) * E_EXP], PS[bk][:, 0:E_EXP]),
                 deps=[HM['rt', i]])

        step("scalar", lambda s: s.activation(s_big2[:, 0:64], s_big0[:, 0:64],
                                              AF.Sigmoid, scale=s_rtemp[:, 0:1]))

        def rv1(v):
            v.tensor_scalar_add(s_big2[:, 64:128], s_big2[:, 0:64], -GOLDEN_CENTER)
            v.tensor_scalar(s_big2[:, 128:192], s_big2[:, 0:64], GOLDEN_LOWER, None, ALU.is_ge)
            v.tensor_scalar(s_big2[:, 192:256], s_big2[:, 0:64], GOLDEN_UPPER, None, ALU.is_le)
            return v.tensor_mul(s_big2[:, 128:192], s_big2[:, 128:192], s_big2[:, 192:256])
        step("vector", rv1)

        def rsig(s):
            s.activation(s_big2[:, 256:320], s_big2[:, 64:128], AF.Abs)
            s.activation(s_big2[:, 320:384], s_big2[:, 256:320], AF.Exp, scale=-10.0)
            return s.activation(s_big2[:, 384:448], s_big2[:, 256:320], AF.Exp, scale=-1.0 / 0.3)
        step("scalar", rsig)

        def rv2(v):
            v.tensor_mul(s_big2[:, 448:512], s_big2[:, 320:384], s_big2[:, 128:192])
            for i in range(NT):
                def a(c):
                    return s_big2[:, c + i * E_EXP: c + (i + 1) * E_EXP]
                wt = a(448)
                fb = a(384)
                v.reduce_sum(s_big3[:, i:i + 1], wt, AX.X)
                v.tensor_scalar(s_big3[:, 8 + i:9 + i], s_big3[:, i:i + 1], 1e-8, None, ALU.is_lt)
                v.reduce_max(s_big3[:, 16 + i:17 + i], fb, AX.X)
                v.tensor_scalar(a(512), fb, s_big3[:, 16 + i:17 + i], None, ALU.is_ge)
                v.tensor_scalar_mul(a(576), a(512), 1e30)
                v.tensor_sub(a(576), fb, a(576))
                v.reduce_max(s_big3[:, 24 + i:25 + i], a(576), AX.X)
                v.tensor_scalar(a(640), a(576), s_big3[:, 24 + i:25 + i], None, ALU.is_ge)
                v.tensor_add(a(640), a(640), a(512))
                v.tensor_mul(a(640), a(640), fb)
                v.reduce_sum(s_big3[:, 32 + i:33 + i], a(640), AX.X)
                v.tensor_scalar_max(s_big3[:, 32 + i:33 + i], s_big3[:, 32 + i:33 + i], 1e-8)
                v.reciprocal(s_big3[:, 40 + i:41 + i], s_big3[:, 32 + i:33 + i])
                v.tensor_scalar_mul(a(640), a(640), s_big3[:, 40 + i:41 + i])
                v.tensor_sub(a(704), a(640), wt)
                v.scalar_tensor_tensor(a(704), a(704), s_big3[:, 8 + i:9 + i], wt,
                                       ALU.mult, ALU.add)
                v.reduce_sum(s_big3[:, 48 + i:49 + i], a(704), AX.X)
                v.tensor_scalar_max(s_big3[:, 48 + i:49 + i], s_big3[:, 48 + i:49 + i], 1e-8)
                v.reciprocal(s_big3[:, 56 + i:57 + i], s_big3[:, 48 + i:49 + i])
                last = v.tensor_scalar_mul(a(704), a(704), s_big3[:, 56 + i:57 + i])
            return last
        step("vector", rv2)

        hrv2 = chain[0]
        for i in range(NT):
            bk = i % 2
            HM['wt', i] = step("tensor", lambda t, i=i, bk=bk:
                 t.matmul(PS[bk][0:E_EXP, 0:P],
                          s_big2[:, 704 + i * E_EXP:704 + (i + 1) * E_EXP],
                          s_idf[:, :], is_transpose=True, start=True, stop=True),
                 deps=([hrv2] if i < 2 else [HM['wv', i - 2], hrv2]))
            HM['wv', i] = step("vector", lambda v, i=i, bk=bk:
                 v.tensor_copy(s_wfm[:, i * P:(i + 1) * P], PS[bk][0:E_EXP, 0:P]),
                 deps=[HM['wt', i]])

        moe_wg3 = dram["moe_wg"].rearrange("(e d) f -> e d f", e=E_EXP)
        moe_wu3 = dram["moe_wu"].rearrange("(e d) f -> e d f", e=E_EXP)
        moe_wd3 = dram["moe_wd"].rearrange("(e f) d -> e f d", e=E_EXP)
        ga3 = dram["moe_ga"].rearrange("(e d) r -> e d r", e=NL)
        gb3 = dram["moe_gb"].rearrange("(e r) f -> e r f", e=NL)
        ua3 = dram["moe_ua"].rearrange("(e d) r -> e d r", e=NL)
        ub3 = dram["moe_ub"].rearrange("(e r) f -> e r f", e=NL)
        da3 = dram["moe_da"].rearrange("(e f) r -> e f r", e=NL)
        db3 = dram["moe_db"].rearrange("(e r) d -> e r d", e=NL)
        mid3 = smb.rearrange("p (q t) -> p q t", t=T)

        big4b = s_big4.bitcast(BF16)
        bd0 = barrier_deps()
        for e in range(E_EXP):
            eb = e % 2

            def wrep(E, dma, e=e, eb=eb, cqs=C["qs"]):
                k_ = [0]
                off = eb * 1024

                def dmaw(dst, src):
                    dma(dst, src)
                    k_[0] += 16
                    E.wait_ge(qs, cqs + k_[0])
                dmaw(s_big2[0:1, off:off + 1024], s_wfm[e:e + 1, :])
                n = 1
                while n < P:
                    dmaw(s_big2[n:min(2 * n, P), off:off + 1024],
                         s_big2[0:min(n, P - n), off:off + 1024])
                    n *= 2
            HM['wrv', e] = step("sync", wrep, ndma=8,
                deps=(bd0 if e == 0 else [HM['mv', e - 1, 1], HM['wv', 7]]))

            if e < NL:
                def lr_load(E, dma, e=e):
                    dma(s_wbig[0:R_LR, 0:F_MOE], gb3[e])
                    dma(s_wbig[0:R_LR, F_MOE:2 * F_MOE], ub3[e])
                    dma(s_wbig[:, 4096:5120].rearrange("p (dc r) -> p dc r", r=R_LR),
                        ga3[e].rearrange("(dc p) r -> p dc r", p=P))
                    dma(s_wbig[:, 5120:6144].rearrange("p (dc r) -> p dc r", r=R_LR),
                        ua3[e].rearrange("(dc p) r -> p dc r", p=P))
                HM['lrl', e] = step("gpsimd_dma", lr_load, ndma=4,
                    deps=(bd0 if e == 0 else [HM['dm', e - 1, 15]]))
                ga_s = s_wbig[:, 4096:5120].rearrange("p (dc r) -> p dc r", r=R_LR)
                ua_s = s_wbig[:, 5120:6144].rearrange("p (dc r) -> p dc r", r=R_LR)

                for li, (off, wsrc, half) in enumerate(
                        [(0, ga_s, 0), (0, ga_s, 1), (1024, ua_s, 0), (1024, ua_s, 1)]):
                    def lrmm(t, wsrc=wsrc, half=half):
                        for dc in range(DMC):
                            last = t.matmul(PS[7][0:R_LR, :], wsrc[:, dc, :],
                                            xn3[:, dc, half * 512:(half + 1) * 512],
                                            start=(dc == 0), stop=(dc == DMC - 1))
                        return last
                    HM['lrt', e, li] = step("tensor", lrmm,
                        deps=[HM['lrl', e]] +
                             ([HM['lrv', e, li - 1]] if li else
                              ([HM['mv', e - 1, 15]] if e else bd0)))
                    HM['lrv', e, li] = step("vector", lambda v, off=off, half=half:
                         v.tensor_copy(s_lr[0:R_LR, off + half * 512:off + (half + 1) * 512],
                                       PS[7][0:R_LR, :]),
                        deps=[HM['lrt', e, li]])

            for fc in range(FC_MOE):
                fb = fc % 2
                wbg = s_wbuf0 if fb == 0 else big4b[:, 0:2048]
                wbu = s_wbuf1 if fb == 0 else big4b[:, 2048:4096]
                sb0 = s_big0[:, fb * 1024:(fb + 1) * 1024]
                sb1 = s_big1[:, fb * 1024:(fb + 1) * 1024]

                def gw(E, dma, e=e, fc=fc, wbg=wbg, wbu=wbu):
                    dma(wbg.rearrange("p (dc j) -> p dc j", j=P),
                        moe_wg3[e][:, fc * P:(fc + 1) * P].rearrange("(dc p) j -> p dc j", p=P))
                    dma(wbu.rearrange("p (dc j) -> p dc j", j=P),
                        moe_wu3[e][:, fc * P:(fc + 1) * P].rearrange("(dc p) j -> p dc j", p=P))
                HM['gw', e, fc] = step("gpsimd_dma", gw, ndma=2,
                    deps=(bd0 if (e == 0 and fc < 2) else
                          [HM['gu', e, fc - 2]] if fc >= 2 else [HM['dm', e - 1, 15]]))

                pb = 0 if fb == 0 else 4

                def gumm(t, e=e, fc=fc, wbg=wbg, wbu=wbu, pb=pb):
                    bg = wbg.rearrange("p (dc j) -> p dc j", j=P)
                    bu = wbu.rearrange("p (dc j) -> p dc j", j=P)
                    last = None
                    for half in range(2):
                        for dc in range(DMC):
                            last = t.matmul(PS[pb + half][:, :], bg[:, dc, :],
                                            xn3[:, dc, half * 512:(half + 1) * 512],
                                            start=(dc == 0),
                                            stop=(dc == DMC - 1 and e >= NL))
                        if e < NL:
                            last = t.matmul(PS[pb + half][:, :],
                                            s_wbig[0:R_LR, fc * P:(fc + 1) * P],
                                            s_lr[0:R_LR, half * 512:(half + 1) * 512],
                                            start=False, stop=True)
                        for dc in range(DMC):
                            last = t.matmul(PS[pb + 2 + half][:, :], bu[:, dc, :],
                                            xn3[:, dc, half * 512:(half + 1) * 512],
                                            start=(dc == 0),
                                            stop=(dc == DMC - 1 and e >= NL))
                        if e < NL:
                            last = t.matmul(PS[pb + 2 + half][:, :],
                                            s_wbig[0:R_LR, F_MOE + fc * P:F_MOE + (fc + 1) * P],
                                            s_lr[0:R_LR, 1024 + half * 512:1024 + (half + 1) * 512],
                                            start=False, stop=True)
                    return last
                gu_deps = [HM['gw', e, fc]]
                if fc >= 2:
                    gu_deps += [HM['sl', e, fc - 2], HM['mv', e, fc - 2]]
                elif e == 0:
                    gu_deps += bd0 + ([HM['lrv', e, 3]] if e < NL else [])
                elif fc == 0:
                    gu_deps += [HM['da', e - 1, 15]] + \
                        ([HM['lrv', e, 3]] if e < NL else [])
                else:
                    gu_deps += [HM['wrv', e], HM['mv', e - 1, 15], HM['sl', e - 1, 15]]
                    if e < NL:
                        gu_deps += [HM['lrv', e, 3]]
                HM['gu', e, fc] = step("tensor", gumm, deps=gu_deps)

                def silu_g(s, sb0=sb0, pb=pb):
                    for half in range(2):
                        last = s.activation(sb0[:, half * 512:(half + 1) * 512],
                                            PS[pb + half][:, :], AF.Silu)
                    return last
                HM['sl', e, fc] = step("scalar", silu_g,
                    deps=[HM['gu', e, fc]] +
                         ([HM['mv', e, fc - 2]] if fc >= 2 else []))

                def midv(v, fc=fc, eb=eb, sb0=sb0, sb1=sb1, pb=pb):
                    for half in range(2):
                        v.tensor_mul(sb1[:, half * 512:(half + 1) * 512],
                                     sb0[:, half * 512:(half + 1) * 512],
                                     PS[pb + 2 + half][:, :])
                        last = v.tensor_mul(mid3[:, fc, half * 512:(half + 1) * 512],
                                            sb1[:, half * 512:(half + 1) * 512],
                                            s_big2[:, eb * 1024 + half * 512:
                                                   eb * 1024 + (half + 1) * 512])
                    return last
                HM['mv', e, fc] = step("vector", midv,
                    deps=[HM['sl', e, fc]] + ([HM['wrv', e]] if fc < 2 else []))

            if e < NL:
                def da_load(E, dma, e=e):
                    dma(s_wbig[:, 4096:5120].rearrange("p (q r) -> p q r", r=R_LR),
                        da3[e].rearrange("(q p) r -> p q r", p=P))
                    dma(s_wbig[0:R_LR, 0:DM], db3[e])
                HM['dal', e] = step("gpsimd_dma", da_load, ndma=2,
                    deps=[HM['gu', e, 15]])
                da_s = s_wbig[:, 4096:5120].rearrange("p (q r) -> p q r", r=R_LR)

                for half in range(2):
                    def damm(t, half=half):
                        for q in range(FC_MOE):
                            last = t.matmul(PS[7][0:R_LR, :], da_s[:, q, :],
                                            mid3[:, q, half * 512:(half + 1) * 512],
                                            start=(q == 0), stop=(q == FC_MOE - 1))
                        return last
                    HM['dat', e, half] = step("tensor", damm,
                        deps=[HM['dal', e], HM['mv', e, 15]] +
                             ([HM['dav', e, 0]] if half else []))
                    HM['dav', e, half] = step("vector", lambda v, half=half:
                         v.tensor_copy(s_lr[0:R_LR, half * 512:(half + 1) * 512],
                                       PS[7][0:R_LR, :]),
                        deps=[HM['dat', e, half]])

            for dm in range(DMC):
                db_ = dm % 2
                wbd = s_wbuf0 if db_ == 0 else s_wbuf1

                def dw(E, dma, e=e, dm=dm, wbd=wbd):
                    dma(wbd.rearrange("p (q j) -> p q j", j=P),
                        moe_wd3[e][:, dm * P:(dm + 1) * P].rearrange("(q p) j -> p q j", p=P))
                HM['dw', e, dm] = step("gpsimd_dma", dw, ndma=1,
                    deps=([HM['dm', e, dm - 2]] if dm >= 2 else
                          [HM['gu', e, 14 + dm]]))

                def dmm(t, e=e, dm=dm, wbd=wbd, db_=db_):
                    b3 = wbd.rearrange("p (q j) -> p q j", j=P)
                    last = None
                    for half in range(2):
                        for q in range(FC_MOE):
                            last = t.matmul(PS[2 * db_ + half][:, :], b3[:, q, :],
                                            mid3[:, q, half * 512:(half + 1) * 512],
                                            start=(q == 0),
                                            stop=(q == FC_MOE - 1 and e >= NL))
                        if e < NL:
                            last = t.matmul(PS[2 * db_ + half][:, :],
                                            s_wbig[0:R_LR, dm * P:(dm + 1) * P],
                                            s_lr[0:R_LR, half * 512:(half + 1) * 512],
                                            start=False, stop=True)
                    return last
                dm_deps = [HM['dw', e, dm], HM['mv', e, 15]]
                if dm >= 2:
                    dm_deps += [HM['da', e, dm - 2]]
                else:
                    dm_deps += [HM['sl', e, 15]]
                    if e < NL:
                        dm_deps += [HM['dav', e, 1]]
                HM['dm', e, dm] = step("tensor", dmm, deps=dm_deps)

                def dadd(v, dm=dm, db_=db_):
                    for half in range(2):
                        last = v.scalar_tensor_tensor(
                            s_xacc[:, dm * T + half * 512: dm * T + (half + 1) * 512],
                            PS[2 * db_ + half][:, :], 1.0,
                            s_xacc[:, dm * T + half * 512: dm * T + (half + 1) * 512],
                            ALU.mult, ALU.add)
                    return last
                HM['da', e, dm] = step("vector", dadd, deps=[HM['dm', e, dm]])

        phase("ph6_ln1")
        for dc in range(DMC):
            step("scalar", lambda s, dc=dc:
                 s.activation(s_big0[:, 0:1024], s_xacc[:, dc * T:(dc + 1) * T], AF.Square))

            def st6(t, dc=dc):
                t.matmul(PS[6][0:1, :], s_ones_c_f[:, :], s_xacc[:, dc * T: dc * T + 512],
                         start=(dc == 0), stop=(dc == DMC - 1))
                t.matmul(PS[5][0:1, :], s_ones_c_f[:, :], s_xacc[:, dc * T + 512: dc * T + 1024],
                         start=(dc == 0), stop=(dc == DMC - 1))
                t.matmul(PS[7][0:1, :], s_ones_c_f[:, :], s_big0[:, 0:512],
                         start=(dc == 0), stop=(dc == DMC - 1))
                return t.matmul(PS[4][0:1, :], s_ones_c_f[:, :], s_big0[:, 512:1024],
                                start=(dc == 0), stop=(dc == DMC - 1))
            step("tensor", st6)

        def rows6(v):
            v.tensor_copy(row0[0:1, 0:512], PS[6][0:1, :])
            v.tensor_copy(row0[0:1, 512:1024], PS[5][0:1, :])
            v.tensor_copy(row0[0:1, 1024:1536], PS[7][0:1, :])
            v.tensor_copy(row0[0:1, 1536:2048], PS[4][0:1, :])
            v.tensor_scalar_mul(row0[0:1, 0:1024], row0[0:1, 0:1024], -1.0 / DM)
            v.tensor_scalar_mul(row0[0:1, 1024:2048], row0[0:1, 1024:2048], 1.0 / DM)
            v.tensor_mul(s_big1[0:1, 1024:2048], row0[0:1, 0:1024], row0[0:1, 0:1024])
            return v.tensor_sub(row0[0:1, 1024:2048], row0[0:1, 1024:2048],
                                s_big1[0:1, 1024:2048])
        step("vector", rows6)

        def rstd6(s):
            s.activation(row0[0:1, 1024:2048], row0[0:1, 1024:2048], AF.Ln,
                         bias=s_rtemp[0:1, 1:2])
            return s.activation(row0[0:1, 1024:2048], row0[0:1, 1024:2048],
                                AF.Exp, scale=-0.5)
        step("scalar", rstd6)

        def rep6(t):
            t.matmul(PS[6][:, :], s_ones_r_f[:, :], row0[0:1, 0:512], start=True, stop=True)
            t.matmul(PS[5][:, :], s_ones_r_f[:, :], row0[0:1, 512:1024], start=True, stop=True)
            t.matmul(PS[7][:, :], s_ones_r_f[:, :], row0[0:1, 1024:1536], start=True, stop=True)
            return t.matmul(PS[4][:, :], s_ones_r_f[:, :], row0[0:1, 1536:2048],
                            start=True, stop=True)
        step("tensor", rep6)

        def xn1f(v):
            v.tensor_copy(s_big1[:, 0:512], PS[6][:, :])
            v.tensor_copy(s_big1[:, 512:1024], PS[5][:, :])
            v.tensor_copy(s_big2[:, 0:512], PS[7][:, :])
            v.tensor_copy(s_big2[:, 512:1024], PS[4][:, :])
            for dc in range(DMC):
                v.tensor_add(s_big0[:, 0:1024], s_xacc[:, dc * T:(dc + 1) * T], s_big1[:, 0:1024])
                last = v.tensor_mul(xn3[:, dc, :], s_big0[:, 0:1024], s_big2[:, 0:1024])
            return last
        step("vector", xn1f)

        phase("ph7a_proj")
        l1w = [("wq1", 0, 8), ("wk1", 1024, 8), ("ws1", 2048, 8),
               ("wv1", 3072, 16), ("wg1", 5120, 16)]
        for wname, rbase, nfc in l1w:
            for fc in range(nfc):
                step("gpsimd_dma", lambda E, dma, wname=wname, fc=fc:
                     dma(s_wbuf0.rearrange("p (dc j) -> p dc j", j=P),
                         dram[wname][:, fc * P:(fc + 1) * P].rearrange("(dc p) j -> p dc j", p=P)),
                     ndma=1)

                def pmm(t, fc=fc):
                    b3 = s_wbuf0.rearrange("p (dc j) -> p dc j", j=P)
                    for half in range(2):
                        for dc in range(DMC):
                            last = t.matmul(PS[half][:, :], b3[:, dc, :],
                                            xn3[:, dc, half * 512:(half + 1) * 512],
                                            start=(dc == 0), stop=(dc == DMC - 1))
                    return last
                step("tensor", pmm)

                if wname == "ws1":
                    def pev(v):
                        for half in range(2):
                            last = v.tensor_scalar(
                                s_big0.bitcast(BF16)[:, half * 512:(half + 1) * 512],
                                PS[half][:, :], 32.0, -32.0, ALU.min, ALU.max)
                        return last
                else:
                    def pev(v):
                        for half in range(2):
                            last = v.tensor_copy(
                                s_big0.bitcast(BF16)[:, half * 512:(half + 1) * 512],
                                PS[half][:, :])
                        return last
                step("vector", pev)
                step("sync", lambda E, dma, rbase=rbase, fc=fc:
                     dma(dram["attn_loc"][rbase + fc * P: rbase + (fc + 1) * P, :],
                         s_big0.bitcast(BF16)[:, 0:T]), ndma=1)

        segs = [("attn_q", 0, 512, 0), ("attn_k", 1024, 512, 512),
                ("attn_s", 2048, 512, 1024), ("attn_v", 3072, 1024, 1536),
                ("attn_g", 5120, 1024, 2560)]

        def e4_body(ab):
            me0 = (ab == "a")
            own_lo, own_hi = (0, T) if me0 else (T, TT)
            other = "b" if ab == "a" else "a"
            po_lo, po_hi = (T, TT) if me0 else (0, T)
            pre, post = [], []
            for can, lbase, sz, shb in segs:
                mine_lo = lbase + (0 if me0 else sz)
                send_lo = lbase + (sz if me0 else 0)
                pre.append((dram[can][:, own_lo:own_hi],
                            dram["attn_loc"][mine_lo:mine_lo + sz, :]))
                pre.append((dram[f"sh_e4_{ab}"][shb:shb + sz, :],
                            dram["attn_loc"][send_lo:send_lo + sz, :]))
                post.append((dram[can][:, po_lo:po_hi],
                             dram[f"sh_e4_{other}"][shb:shb + sz, :]))
            return pre, post
        phase("exch_e4")
        exchange("e4", 3, e4_body)

        phase("ph7b_abc")
        b0b = s_big0.bitcast(BF16)
        b1b = s_big1.bitcast(BF16)
        b2b = s_big2.bitcast(BF16)
        b3b = s_big3.bitcast(BF16)
        p2v = s_big4.bitcast(BF16)[:, 2048:4096]
        o3 = smb.rearrange("p (hh t) -> p hh t", t=TT)

        for h in range(HH):
            def hld(E, dma, h=h):
                dma(b0b[0:DK, 0:TT], dram["attn_q"][h * DK:(h + 1) * DK, :])
                dma(b0b[0:DK, 2048:4096], dram["attn_k"][h * DK:(h + 1) * DK, :])
                dma(b1b[0:M_ABC, 2048:4096], dram["attn_s"][h * M_ABC:(h + 1) * M_ABC, :])
                dma(b3b[:, 2048:4096], dram["attn_v"][h * DV:(h + 1) * DV, :])
                dma(s_wbuf0[:, 0:TT], dram["attn_g"][h * DV:(h + 1) * DV, :])
            step("sync", hld, ndma=5)

            step("scalar", lambda s:
                 s.activation(b1b[0:M_ABC, 0:TT], b1b[0:M_ABC, 2048:4096], AF.Exp))
            step("scalar", lambda s:
                 s.activation(s_wbuf1[:, 0:TT], s_wbuf0[:, 0:TT], AF.Silu))

            ktm = b2b[:, 0:1024].rearrange("p (c x) -> p c x", x=DK)
            ektm = b2b[:, 1024:2048].rearrange("p (c x) -> p c x", x=DK)
            vtm = b3b[:, 0:2048].rearrange("p (c x) -> p c x", x=DV)
            sinv = s_big4[:, 0:1024].rearrange("p (c x) -> p c x", x=DK)
            rfm = b2b[:, 2048:4096]
            hk_sb = b1b[0:DK, 4096:4160]
            hv_sb = b1b[0:M_ABC, 4160:4288]
            rtm_sb = b1b[:, 4288:4352]
            s1m = b1b[:, 4352:4480]

            for cc in range(16):
                def trp(t, cc=cc):
                    t.matmul(PSB[0][:, 0:DK], b0b[0:DK, 2048 + cc * P:2048 + (cc + 1) * P],
                             s_idb[0:DK, 0:DK], is_transpose=True, start=True, stop=True)
                    t.matmul(PSB[1][:, 0:DK], b1b[0:M_ABC, cc * P:(cc + 1) * P],
                             s_idb[0:M_ABC, 0:M_ABC], is_transpose=True, start=True, stop=True)
                    return t.matmul(PSB[2][:, 0:DV], b3b[:, 2048 + cc * P:2048 + (cc + 1) * P],
                                    s_idb[:, :], is_transpose=True, start=True, stop=True)
                step("tensor", trp)

                def trev(v, cc=cc):
                    v.tensor_copy(ktm[:, cc, :], PSB[0][:, 0:DK])
                    v.tensor_copy(ektm[:, cc, :], PSB[1][:, 0:DK])
                    return v.tensor_copy(vtm[:, cc, :], PSB[2][:, 0:DV])
                step("vector", trev)

            for cc in range(16):
                def p1a(t, cc=cc):
                    t.matmul(PS[6][:, 0:DK], s_tri[:, :], ektm[:, cc, :],
                             start=True, stop=(cc == 0))
                    if cc > 0:
                        t.matmul(PS[6][:, 0:DK], s_ones_r_bf[:, :], s_crow[0:1, :],
                                 start=False, stop=True)
                    t.matmul(PS[7][0:1, 0:DK], s_ones_c_bf[:, :], ektm[:, cc, :],
                             start=(cc == 0), stop=(cc == 15))
                    return t.matmul(PS[5][:, 0:P], b0b[0:DK, 2048 + cc * P:2048 + (cc + 1) * P],
                                    b0b[0:DK, cc * P:(cc + 1) * P], start=True, stop=True)
                step("tensor", p1a)

                def p1b(v, cc=cc):
                    v.reciprocal(sinv[:, cc, :], PS[6][:, 0:DK])
                    v.tensor_copy(s_crow[0:1, :], PS[7][0:1, 0:DK])
                    return v.scalar_tensor_tensor(s1m, PS[5][:, 0:P], 0.125,
                                                  s_tri[:, :], ALU.mult, ALU.mult)
                step("vector", p1b)

                def p1c(t, cc=cc):
                    t.matmul(PS[4][:, 0:DK], s1m, ektm[:, cc, :],
                             start=True, stop=(cc == 0))
                    if cc > 0:
                        t.matmul(PS[4][:, 0:DK], b0b[0:DK, cc * P:(cc + 1) * P],
                                 hk_sb, start=False, stop=True)
                    return t.matmul(PS[3][0:DK, 0:DK], ktm[:, cc, :], ektm[:, cc, :],
                                    start=(cc == 0), stop=(cc == 15))
                step("tensor", p1c)

                step("scalar", lambda s:
                     s.activation(hk_sb, PS[3][0:DK, 0:DK], AF.Identity, scale=0.125))

                step("vector", lambda v, cc=cc:
                     v.tensor_mul(s_big4[:, 1024:1088], PS[4][:, 0:DK], sinv[:, cc, :]))
                step("scalar", lambda s:
                     s.activation(s_big4[:, 1088:1152], s_big4[:, 1024:1088], AF.Exp))

                def smax2(v, cc=cc):
                    v.reduce_sum(s_big4[:, 1152:1153], s_big4[:, 1088:1152], AX.X)
                    v.reciprocal(s_big4[:, 1153:1154], s_big4[:, 1152:1153])
                    v.tensor_mul(s_big4[:, 1216:1280], s_big4[:, 1088:1152], sinv[:, cc, :])
                    return v.tensor_scalar_mul(rtm_sb, s_big4[:, 1216:1280],
                                               s_big4[:, 1153:1154])
                step("vector", smax2)

                step("tensor", lambda t:
                     t.matmul(PSB[1][0:DK, 0:P], rtm_sb[:, 0:DK], s_idb[:, :],
                              is_transpose=True, start=True, stop=True))
                step("vector", lambda v, cc=cc:
                     v.tensor_copy(rfm[0:DK, cc * P:(cc + 1) * P], PSB[1][0:DK, 0:P]))

            for j in range(16):
                ncols = TT - j * P
                nmm = (ncols + 511) // 512

                for qbase in range(0, nmm, 3):
                    qgrp = list(range(qbase, min(qbase + 3, nmm)))

                    def p2a(t, j=j, qgrp=qgrp):
                        for idx, q in enumerate(qgrp):
                            cl = j * P + q * 512
                            cw = min(512, TT - cl)
                            last = t.matmul(PS[4 + idx][:, 0:cw],
                                            b1b[0:M_ABC, j * P:(j + 1) * P],
                                            rfm[0:M_ABC, cl:cl + cw],
                                            start=True, stop=True)
                        return last
                    step("tensor", p2a)

                    def p2b(v, j=j, qgrp=qgrp):
                        for idx, q in enumerate(qgrp):
                            cl = j * P + q * 512
                            cw = min(512, TT - cl)
                            if q == 0:
                                last = v.tensor_mul(p2v[:, 0:P],
                                                    PS[4 + idx][:, 0:P], s_tri[:, :])
                                if cw > P:
                                    last = v.tensor_copy(p2v[:, P:cw],
                                                         PS[4 + idx][:, P:cw])
                            else:
                                last = v.tensor_copy(p2v[:, cl - j * P:cl - j * P + cw],
                                                     PS[4 + idx][:, 0:cw])
                        return last
                    step("vector", p2b)

                def p2c(t, j=j):
                    last = None
                    if j > 0:
                        bank, off = divmod(j * P, 512)
                        last = t.matmul(PS[bank][:, off:off + P], hv_sb,
                                        rfm[0:M_ABC, j * P:(j + 1) * P], start=False, stop=False)
                    for q in range(4):
                        blo, bhi = q * 512, (q + 1) * 512
                        lo = max(j * P, blo)
                        if lo >= bhi:
                            continue
                        last = t.matmul(PS[q][:, lo - blo:bhi - blo], vtm[:, j, :],
                                        p2v[:, lo - j * P:bhi - j * P],
                                        start=(j == 0), stop=(j == min(4 * q + 3, 15)))
                    last = t.matmul(PS[7][0:M_ABC, 0:DV], ektm[:, j, :], vtm[:, j, :],
                                    start=(j == 0), stop=(j == 15))
                    return last
                step("tensor", p2c)

                step("scalar", lambda s:
                     s.activation(hv_sb, PS[7][0:M_ABC, 0:DV], AF.Identity))

            def rms_s(s):
                for q in range(4):
                    last = s.activation(b0b[:, q * 512:(q + 1) * 512], PS[q][:, :], AF.Square)
                return last
            step("scalar", rms_s)

            for q in range(4):
                step("tensor", lambda t, q=q:
                     t.matmul(PS[6][0:1, 0:512], s_ones_c_bf[:, :],
                              b0b[:, q * 512:(q + 1) * 512], start=True, stop=True))
                step("vector", lambda v, q=q:
                     v.tensor_copy(row0[0:1, q * 512:(q + 1) * 512], PS[6][0:1, 0:512]))

            step("vector", lambda v:
                 v.tensor_scalar_mul(row0, row0, 1.0 / DV))

            def rmsact(s):
                s.activation(row0, row0, AF.Ln, bias=s_rtemp[0:1, 1:2])
                return s.activation(row0, row0, AF.Exp, scale=-0.5)
            step("scalar", rmsact)

            for q in range(4):
                step("tensor", lambda t, q=q:
                     t.matmul(PS[5][:, :], s_ones_r_f[:, :], row0[0:1, q * 512:(q + 1) * 512],
                              start=True, stop=True))

                def gate(v, h=h, q=q):
                    v.tensor_copy(s_big4[:, 1024:1536], PS[5][:, :])
                    v.tensor_mul(s_big4[:, 1536:2048], PS[q][:, :], s_big4[:, 1024:1536])
                    v.tensor_scalar_mul(s_big4[:, 1536:2048], s_big4[:, 1536:2048], s_gnw[:, 0:1])
                    return v.tensor_mul(o3[:, h, q * 512:(q + 1) * 512], s_big4[:, 1536:2048],
                                        s_wbuf1[:, q * 512:(q + 1) * 512])
                step("vector", gate)

        def e5_body(ab):
            me0 = (ab == "a")
            own_lo = 0 if me0 else T
            myrows = 0 if me0 else 1024
            other = "b" if ab == "a" else "a"
            prr = 1024 if me0 else 0
            pre = []
            for hh_ in range(HH):
                pre.append((dram["o_own"][myrows + hh_ * DV: myrows + (hh_ + 1) * DV, :],
                            o3[:, hh_, own_lo:own_lo + T]))
                pre.append((dram[f"sh_e5_{ab}"][hh_ * DV:(hh_ + 1) * DV, :],
                            o3[:, hh_, (T if me0 else 0):(TT if me0 else T)]))
            post = [(dram["o_own"][prr:prr + 1024, :], dram[f"sh_e5_{other}"])]
            return pre, post
        phase("exch_e5")
        exchange("e5", 4, e5_body)

        phase("ph7c_wo")
        def oo_ld(E, dma):
            dma(smb.rearrange("p (j t) -> p j t", t=T)[:, 0:16, :],
                dram["o_own"].rearrange("(j p) t -> p j t", p=P))
        step("sync", oo_ld, ndma=1)

        W7 = {}
        bd7 = barrier_deps()
        wo4b = s_big4.bitcast(BF16)
        for grp in range(8):
            gb7 = grp % 2
            wb7 = s_wbig[:, 0:4096] if gb7 == 0 else wo4b[:, 0:4096]

            def wld7(E, dma, grp=grp, wb7=wb7):
                dma(wb7.rearrange("p (j d) -> p j d", d=DM),
                    dram["wo1"][grp * 256:(grp + 1) * 256, :].rearrange("(j p) d -> p j d", p=P))
            W7['w', grp] = step("gpsimd_dma", wld7, ndma=1,
                deps=(bd7 if grp < 2 else [W7['t', grp - 2, DMC - 1]]))

            for dm in range(DMC):
                pb7 = 2 * (dm % 2)

                def womm(t, grp=grp, dm=dm, wb7=wb7, pb7=pb7):
                    w3 = wb7.rearrange("p (j d) -> p j d", d=DM)
                    om = smb.rearrange("p (j t) -> p j t", t=T)
                    for half in range(2):
                        for j in range(2):
                            last = t.matmul(PS[pb7 + half][:, :], w3[:, j, dm * P:(dm + 1) * P],
                                            om[:, grp * 2 + j, half * 512:(half + 1) * 512],
                                            start=(j == 0), stop=(j == 1))
                    return last
                wo_deps = [W7['w', grp]]
                if dm >= 2:
                    wo_deps += [W7['v', grp, dm - 2]]
                elif grp >= 1:
                    wo_deps += [W7['v', grp - 1, DMC - 2 + dm]]
                else:
                    wo_deps += bd7
                W7['t', grp, dm] = step("tensor", womm, deps=wo_deps)

                def woadd(v, dm=dm, pb7=pb7):
                    for half in range(2):
                        last = v.scalar_tensor_tensor(
                            s_xacc[:, dm * T + half * 512: dm * T + (half + 1) * 512],
                            PS[pb7 + half][:, :], 1.0,
                            s_xacc[:, dm * T + half * 512: dm * T + (half + 1) * 512],
                            ALU.mult, ALU.add)
                    return last
                W7['v', grp, dm] = step("vector", woadd, deps=[W7['t', grp, dm]])

        phase("ph8_qffn")
        def rs_t(t):
            for dc in range(DMC):
                t.matmul(PS[6][0:1, :], s_ones_c_bf[:, :], xn3[:, dc, 0:512],
                         start=(dc == 0), stop=(dc == DMC - 1))
                last = t.matmul(PS[5][0:1, :], s_ones_c_bf[:, :], xn3[:, dc, 512:1024],
                                start=(dc == 0), stop=(dc == DMC - 1))
            return last
        step("tensor", rs_t, deps=barrier_deps())

        def rs_v(v):
            v.tensor_copy(row0[0:1, 0:512], PS[6][0:1, :])
            v.tensor_copy(row0[0:1, 512:1024], PS[5][0:1, :])
            v.tensor_copy(rowb0, row0[0:1, 0:1024])
            v.tensor_sub(s_big1[0:1, 0:1024], row0[0:1, 0:1024], rowb0)
            return v.tensor_copy(rowb1, s_big1[0:1, 0:1024])
        step("vector", rs_v)
        bd8 = barrier_deps()

        mid11 = smb.rearrange("p (q t) -> p q t", t=T)
        K8 = {}
        q4b = s_big4.bitcast(BF16)
        for sup in range(4):
            for hc_i in range(11):
                hc = sup * 11 + hc_i
                par8 = hc % 2
                pb8 = 0 if par8 == 0 else 4
                wg8 = s_wbuf0 if par8 == 0 else q4b[:, 0:2048]
                wu8 = s_wbuf1 if par8 == 0 else q4b[:, 2048:4096]
                sb8 = s_big0[:, par8 * 1024:(par8 + 1) * 1024]

                def qwl(E, dma, hc=hc, wg8=wg8, wu8=wu8):
                    dma(wg8.rearrange("p (dc j) -> p dc j", j=P),
                        dram["qg_T"][:, hc * P:(hc + 1) * P].rearrange("(dc p) j -> p dc j", p=P))
                    dma(wu8.rearrange("p (dc j) -> p dc j", j=P),
                        dram["qu_T"][:, hc * P:(hc + 1) * P].rearrange("(dc p) j -> p dc j", p=P))
                K8['w', sup, hc_i] = step("gpsimd_dma", qwl, ndma=2,
                    deps=(bd8 if (sup == 0 and hc_i < 2) else
                          [K8['t', sup, hc_i - 2]] if hc_i >= 2 else
                          [K8['dt', sup - 1, DMC - 1]]))

                def qmm(t, hc=hc, wg8=wg8, wu8=wu8, pb8=pb8):
                    bg = wg8.rearrange("p (dc j) -> p dc j", j=P)
                    bu = wu8.rearrange("p (dc j) -> p dc j", j=P)
                    for half in range(2):
                        for dc in range(DMC):
                            t.matmul(PS[pb8 + half][:, :], bg[:, dc, :],
                                     xn3[:, dc, half * 512:(half + 1) * 512],
                                     start=(dc == 0), stop=False)
                        t.matmul(PS[pb8 + half][:, :], s_m128[:, :],
                                 rowb0[0:1, half * 512:(half + 1) * 512], start=False, stop=False)
                        t.matmul(PS[pb8 + half][:, :], s_m128[:, :],
                                 rowb1[0:1, half * 512:(half + 1) * 512], start=False, stop=True)
                        for dc in range(DMC):
                            t.matmul(PS[pb8 + 2 + half][:, :], bu[:, dc, :],
                                     xn3[:, dc, half * 512:(half + 1) * 512],
                                     start=(dc == 0), stop=False)
                        t.matmul(PS[pb8 + 2 + half][:, :], s_m128[:, :],
                                 rowb0[0:1, half * 512:(half + 1) * 512], start=False, stop=False)
                        last = t.matmul(PS[pb8 + 2 + half][:, :], s_m128[:, :],
                                        rowb1[0:1, half * 512:(half + 1) * 512],
                                        start=False, stop=True)
                    return last
                qm_deps = [K8['w', sup, hc_i]]
                if hc_i >= 2:
                    qm_deps += [K8['sl', sup, hc_i - 2], K8['mv', sup, hc_i - 2]]
                elif sup == 0:
                    qm_deps += bd8
                elif par8 == 0:
                    qm_deps += [K8['dv', sup - 1, DMC - 1]]
                else:
                    qm_deps += [K8['r2v', sup - 1], K8['sl', sup - 1, 10],
                                K8['mv', sup - 1, 10]]
                K8['t', sup, hc_i] = step("tensor", qmm, deps=qm_deps)

                def qsil(s, hc=hc, sb8=sb8, pb8=pb8):
                    for half in range(2):
                        last = s.activation(sb8[:, half * 512:(half + 1) * 512],
                                            PS[pb8 + half][:, :], AF.Silu,
                                            scale=s_sgc[:, hc:hc + 1])
                    return last
                K8['sl', sup, hc_i] = step("scalar", qsil,
                    deps=[K8['t', sup, hc_i]] +
                         ([K8['mv', sup, hc_i - 2]] if hc_i >= 2 else []))

                def qmid(v, hc=hc, hc_i=hc_i, sb8=sb8, pb8=pb8):
                    for half in range(2):
                        last = v.scalar_tensor_tensor(
                            mid11[:, hc_i, half * 512:(half + 1) * 512],
                            PS[pb8 + 2 + half][:, :], s_suc[:, hc:hc + 1],
                            sb8[:, half * 512:(half + 1) * 512], ALU.mult, ALU.mult)
                    return last
                K8['mv', sup, hc_i] = step("vector", qmid, deps=[K8['sl', sup, hc_i]])

            def rs2t(t, sup=sup):
                for hc_i in range(11):
                    t.matmul(PS[6][0:1, :], s_ones_c_bf[:, :], mid11[:, hc_i, 0:512],
                             start=(hc_i == 0), stop=(hc_i == 10))
                    last = t.matmul(PS[7][0:1, :], s_ones_c_bf[:, :], mid11[:, hc_i, 512:1024],
                                    start=(hc_i == 0), stop=(hc_i == 10))
                return last
            K8['r2t', sup] = step("tensor", rs2t, deps=[K8['mv', sup, 10]])

            def rs2v(v, sup=sup):
                if sup == 0:
                    v.tensor_copy(row0[0:1, 0:512], PS[6][0:1, :])
                    return v.tensor_copy(row0[0:1, 512:1024], PS[7][0:1, :])
                v.tensor_add(row0[0:1, 0:512], row0[0:1, 0:512], PS[6][0:1, :])
                return v.tensor_add(row0[0:1, 512:1024], row0[0:1, 512:1024],
                                    PS[7][0:1, :])
            K8['r2v', sup] = step("vector", rs2v, deps=[K8['r2t', sup]])

            for dm in range(DMC):
                pbD = 2 * (dm % 2)
                wbD = s_wbuf0 if dm % 2 == 0 else s_wbuf1

                def qdw(E, dma, sup=sup, dm=dm, wbD=wbD):
                    dma(wbD[:, 0:11 * P].rearrange("p (q j) -> p q j", j=P),
                        dram["qd_T"][sup * 11 * P:(sup + 1) * 11 * P, dm * P:(dm + 1) * P]
                        .rearrange("(q p) j -> p q j", p=P))
                K8['dw', sup, dm] = step("gpsimd_dma", qdw, ndma=1,
                    deps=([K8['dt', sup, dm - 2]] if dm >= 2 else [K8['t', sup, 10]]))

                def qdmm(t, dm=dm, wbD=wbD, pbD=pbD):
                    b3 = wbD[:, 0:11 * P].rearrange("p (q j) -> p q j", j=P)
                    for half in range(2):
                        for q in range(11):
                            last = t.matmul(PS[pbD + half][:, :], b3[:, q, :],
                                            mid11[:, q, half * 512:(half + 1) * 512],
                                            start=(q == 0), stop=(q == 10))
                    return last
                K8['dt', sup, dm] = step("tensor", qdmm,
                    deps=[K8['dw', sup, dm], K8['mv', sup, 10]] +
                         ([K8['dv', sup, dm - 2]] if dm >= 2 else
                          [K8['sl', sup, 10]]))

                def qdadd(v, dm=dm, pbD=pbD):
                    for half in range(2):
                        last = v.scalar_tensor_tensor(
                            s_xacc[:, dm * T + half * 512: dm * T + (half + 1) * 512],
                            PS[pbD + half][:, :], s_sdc[:, dm:dm + 1],
                            s_xacc[:, dm * T + half * 512: dm * T + (half + 1) * 512],
                            ALU.mult, ALU.add)
                    return last
                K8['dv', sup, dm] = step("vector", qdadd, deps=[K8['dt', sup, dm]])

        def rs2rep(t):
            t.matmul(PS[6][:, :], s_ones_r_f[:, :], row0[0:1, 0:512], start=True, stop=True)
            return t.matmul(PS[5][:, :], s_ones_r_f[:, :], row0[0:1, 512:1024],
                            start=True, stop=True)
        step("tensor", rs2rep,
             deps=[K8['r2v', 3], K8['sl', 3, 10], K8['dv', 3, DMC - 1]])

        def rs2corr(v):
            for dm in range(DMC):
                for half in range(2):
                    bank = PS[6] if half == 0 else PS[5]
                    last = v.scalar_tensor_tensor(
                        s_xacc[:, dm * T + half * 512: dm * T + (half + 1) * 512],
                        bank[:, :], s_sdm128[:, dm:dm + 1],
                        s_xacc[:, dm * T + half * 512: dm * T + (half + 1) * 512],
                        ALU.mult, ALU.add)
            return last
        step("vector", rs2corr)

        phase("ph9_out")
        K9 = {}
        bd9 = barrier_deps()
        for i in range(NT):
            ob9 = s_big0 if i % 2 == 0 else s_big2
            for dc in range(DMC):
                k9 = i * DMC + dc
                bk9 = k9 % 4
                K9['t', k9] = step("tensor", lambda t, i=i, dc=dc, bk9=bk9:
                     t.matmul(PS[bk9][:, 0:P], s_xacc[:, dc * T + i * P: dc * T + (i + 1) * P],
                              s_idf[:], is_transpose=True, start=True, stop=True),
                     deps=(bd9 if k9 < 4 else [K9['v', k9 - 4]]))
                ev9 = [K9['t', k9]]
                if dc == 0 and i >= 2:
                    ev9 += [K9['o', i - 2]]
                K9['v', k9] = step("vector", lambda v, dc=dc, ob9=ob9, bk9=bk9:
                     v.tensor_copy(ob9[:, dc * P:(dc + 1) * P], PS[bk9][:, 0:P]),
                     deps=ev9)
            K9['o', i] = step("scalar_dma", lambda E, dma, i=i, ob9=ob9:
                 dma(dram["y"][i * P:(i + 1) * P, :], ob9[:, 0:DM]), ndma=1,
                 deps=[K9['v', i * DMC + DMC - 1]])

        for nm, outn in DEBUG_COPY:
            step("sync", lambda E, dma, nm=nm, outn=outn:
                 dma(dram[outn], dram[nm]), ndma=1)

        phase(None)
        cfin = dict(C)

        def fin(sy):
            if chain[0] is not None:
                sy.wait_ge(SEMS[chain[0][0]], chain[0][1])
            for nm in ("qc", "qs", "qg"):
                if cfin[nm] > 0:
                    sy.wait_ge(SEMS[nm], cfin[nm])
        block.sync(fin)

    return nc



def _prep_consts():
    c = {}
    c["id_f32"] = np.eye(P, dtype=np.float32)
    c["id_bf16"] = np.eye(P, dtype=np.float32)
    tri = (np.arange(P)[:, None] <= np.arange(P)[None, :]).astype(np.float32)
    c["tri_bf16"] = tri
    c["tri_scaled"] = tri * 0.125
    selrep = np.zeros((P, N * P), np.float32)
    selred = np.zeros((P, N * P), np.float32)
    for m in range(N):
        for j in range(P):
            selrep[8 * m + (j % 8), m * P + j] = 1.0
        for p_ in range(P):
            selred[p_, m * P + 8 * m + (p_ % 8)] = 1.0
    c["selrep"] = selrep
    c["selred"] = selred
    sB = np.zeros((32, P), np.float32)
    sC = np.zeros((32, P), np.float32)
    for j in range(P):
        sB[j // 8, j] = 1.0
        sC[16 + j // 8, j] = 1.0
    c["sB"] = sB
    c["sC"] = sC
    c["ones_c_bf"] = np.ones((P, 1), np.float32)
    c["ones_c_f32"] = np.ones((P, 1), np.float32)
    c["ones_r_bf"] = np.ones((1, P), np.float32)
    c["ones_r_f32"] = np.ones((1, P), np.float32)
    c["m128_r"] = np.full((1, P), -128.0, np.float32)
    return c


_BF16_NAMES = {"id_bf16", "tri_bf16", "tri_scaled", "selrep", "selred",
               "ones_c_bf", "ones_r_bf", "m128_r"}


def kernel(**inputs):
    import ml_dtypes
    inp = {k: np.asarray(v) for k, v in inputs.items()}

    x = np.asarray(inp["L0_attn_hidden_states"], np.float32)
    w_inT = np.ascontiguousarray(np.asarray(inp["L0_attn_in_proj_w"], np.float32).T)
    conv_w = np.asarray(inp["L0_attn_conv_w"], np.float32)
    conv_b = np.asarray(inp["L0_attn_conv_b"], np.float32)
    w_xpT = np.ascontiguousarray(np.asarray(inp["L0_attn_x_proj_w"], np.float32).T)
    w_dtT = np.ascontiguousarray(np.asarray(inp["L0_attn_dt_proj_w"], np.float32).T)
    dt_b = np.asarray(inp["L0_attn_dt_proj_b"], np.float32)
    A_log = np.asarray(inp["L0_attn_A_log"], np.float32)
    D_ = np.asarray(inp["L0_attn_D"], np.float32)
    w_outT = np.ascontiguousarray(np.asarray(inp["L0_attn_out_proj_w"], np.float32).T)

    consts = _prep_consts()
    rng = np.random.default_rng()
    magic_base = int(rng.integers(1 << 20, 1 << 30))
    magic = np.zeros((1, 40), np.int32)
    for k in range(5):
        magic[0, 8 * k:8 * k + 8] = magic_base + k
    magic_cmp = np.concatenate([np.arange(5, dtype=np.int32) + magic_base,
                                np.zeros(3, np.int32)])[None, :]

    def colsify(vec, nch):
        return np.ascontiguousarray(np.asarray(vec, np.float32).reshape(nch, P).T)

    nc = build_program()

    shared_f32 = {
        "w_inT": w_inT, "w_outT": w_outT,
        "moe_wg": np.asarray(inp["L0_ffn_Wg"], np.float32).reshape(E_EXP * DM, F_MOE),
        "moe_wu": np.asarray(inp["L0_ffn_Wu"], np.float32).reshape(E_EXP * DM, F_MOE),
        "moe_wd": np.asarray(inp["L0_ffn_Wd"], np.float32).reshape(E_EXP * F_MOE, DM),
        "moe_ga": np.asarray(inp["L0_ffn_Ga"], np.float32).reshape(NL * DM, R_LR),
        "moe_gb": np.asarray(inp["L0_ffn_Gb"], np.float32).reshape(NL * R_LR, F_MOE),
        "moe_ua": np.asarray(inp["L0_ffn_Ua"], np.float32).reshape(NL * DM, R_LR),
        "moe_ub": np.asarray(inp["L0_ffn_Ub"], np.float32).reshape(NL * R_LR, F_MOE),
        "moe_da": np.asarray(inp["L0_ffn_Da"], np.float32).reshape(NL * F_MOE, R_LR),
        "moe_db": np.asarray(inp["L0_ffn_Db"], np.float32).reshape(NL * R_LR, DM),
        "wr": np.asarray(inp["L0_ffn_Wr"], np.float32),
        "wq1": np.asarray(inp["L1_attn_Wq"], np.float32),
        "wk1": np.asarray(inp["L1_attn_Wk"], np.float32),
        "wv1": np.asarray(inp["L1_attn_Wv"], np.float32),
        "wg1": np.asarray(inp["L1_attn_Wg"], np.float32),
        "ws1": np.asarray(inp["L1_attn_Ws"], np.float32),
        "wo1": np.asarray(inp["L1_attn_Wo"], np.float32),
    }
    qg_T = np.ascontiguousarray(np.asarray(inp["L1_ffn_wq_gate"], np.int32).T)
    qu_T = np.ascontiguousarray(np.asarray(inp["L1_ffn_wq_up"], np.int32).T)
    qd_T = np.ascontiguousarray(np.asarray(inp["L1_ffn_wq_down"], np.int32).T)

    in_maps = []
    for c in range(8):
        b, role = c // 2, c % 2
        dlo = role * DIH
        m = dict(shared_f32)
        m["qg_T"] = qg_T
        m["qu_T"] = qu_T
        m["qd_T"] = qd_T
        m["x_own"] = np.ascontiguousarray(x[b, role * T:(role + 1) * T, :])
        cw = conv_w[dlo:dlo + DIH]
        m["conv_w_loc"] = np.ascontiguousarray(
            cw.reshape(DIHC, P, 4).transpose(1, 0, 2).reshape(P, DIHC * 4))
        m["conv_b_loc"] = colsify(conv_b[dlo:dlo + DIH], DIHC)
        m["w_xpT_loc"] = np.ascontiguousarray(w_xpT[dlo:dlo + DIH])
        m["w_dtT_loc"] = np.ascontiguousarray(w_dtT[:, dlo:dlo + DIH])
        m["dtb_loc"] = colsify(dt_b[dlo:dlo + DIH], DIHC)
        kk = np.arange(DIHC * N)
        pp = np.arange(P)
        m["alog_lanes"] = np.ascontiguousarray(
            A_log[dlo + 8 * kk[None, :] + (pp[:, None] % 8), pp[:, None] // 8]).astype(np.float32)
        m["d_loc"] = colsify(D_[dlo:dlo + DIH], DIHC)
        m["temp"] = np.asarray(inp["L0_ffn_temperature"], np.float32).reshape(1, 1)
        m["gnw"] = np.asarray(inp["L1_attn_g_norm_weight"], np.float32).reshape(P, 1)
        m["sg_cols"] = colsify(inp["L1_ffn_s_gate"], HIDC)
        m["su_cols"] = colsify(inp["L1_ffn_s_up"], HIDC)
        m["sd_cols"] = colsify(inp["L1_ffn_s_down"], DMC)
        for k_, v_ in consts.items():
            m[k_] = v_.astype(ml_dtypes.bfloat16) if k_ in _BF16_NAMES else v_
        m["role_in"] = np.array([[role]], np.int32)
        m["magic"] = magic
        m["magic_cmp"] = magic_cmp
        in_maps.append(m)

    res = run_bass_kernel_spmd(nc, in_maps, core_ids=list(range(8)))
    global LAST_RESULTS
    LAST_RESULTS = res
    out = np.zeros((4, TT, DM), np.float32)
    for c in range(8):
        b, role = c // 2, c % 2
        out[b, role * T:(role + 1) * T, :] = res.results[c]["y"]
    return out


# version 04
# speedup vs baseline: 1.7157x; 1.0692x over previous
import numpy as np
import sys
sys.path.insert(0, "/opt/trn_rl_repo")
import concourse.bass as bass
import concourse.mybir as mybir
from concourse.bass_utils import run_bass_kernel_spmd
from contextlib import ExitStack

F32 = mybir.dt.float32
BF16 = mybir.dt.bfloat16
I32 = mybir.dt.int32
AF = mybir.ActivationFunctionType
ALU = mybir.AluOpType
AX = mybir.AxisListType

P = 128
DM = 2048
DMC = DM // P
DI = 4096
DIH = DI // 2
DIHC = DIH // P
T = 1024
TT = 2048
N = 16
E_EXP = 8
F_MOE = 2048
FC_MOE = F_MOE // P
R_LR = 64
NL = 2
H = 16
HH = 8
DK = 64
DV = 128
M_ABC = 64
HID = 5632
HIDC = HID // P
GOLDEN_CENTER = 1.0 / np.e
GOLDEN_LOWER = 0.5 - np.log(4.0 / 3.0)
GOLDEN_UPPER = 0.5
EPS_LN = 1e-5
EPS_RMS = 1e-5

DEBUG_COPY = []
SIM_RACECHECK = False
LAST_RESULTS = None



def build_program():
    nc = bass.Bass("TRN2", target_bir_lowering=False)
    dram = {}

    def din(name, shape, dt=F32):
        dram[name] = nc.dram_tensor(name, shape, dt, kind="ExternalInput").ap()

    def dout(name, shape, dt=F32):
        dram[name] = nc.dram_tensor(name, shape, dt, kind="ExternalOutput").ap()

    def dint(name, shape, dt=F32, shared=False):
        if SIM_RACECHECK:
            shared = False
        if shared:
            dram[name] = nc.dram_tensor(name, shape, dt, addr_space="Shared").ap()
        else:
            dram[name] = nc.dram_tensor(name, shape, dt).ap()

    din("x_own", [T, DM])
    din("w_inT", [DM, 2 * DI])
    din("conv_w_loc", [P, DIHC * 4])
    din("conv_b_loc", [P, DIHC])
    din("w_xpT_loc", [DIH, 160])
    din("w_dtT_loc", [128, DIH])
    din("dtb_loc", [P, DIHC])
    din("alog_lanes", [P, DIHC * N])
    din("d_loc", [P, DIHC])
    din("w_outT", [DI, DM])
    din("moe_wg", [E_EXP * DM, F_MOE])
    din("moe_wu", [E_EXP * DM, F_MOE])
    din("moe_wd", [E_EXP * F_MOE, DM])
    din("moe_ga", [NL * DM, R_LR])
    din("moe_gb", [NL * R_LR, F_MOE])
    din("moe_ua", [NL * DM, R_LR])
    din("moe_ub", [NL * R_LR, F_MOE])
    din("moe_da", [NL * F_MOE, R_LR])
    din("moe_db", [NL * R_LR, DM])
    din("wr", [DM, E_EXP])
    din("temp", [1, 1])
    din("wq1", [DM, H * DK])
    din("wk1", [DM, H * DK])
    din("wv1", [DM, H * DV])
    din("wg1", [DM, H * DV])
    din("ws1", [DM, H * M_ABC])
    din("wo1", [H * DV, DM])
    din("gnw", [P, 1])
    din("qg_T", [DM, HID], I32)
    din("qu_T", [DM, HID], I32)
    din("qd_T", [HID, DM], I32)
    din("sg_cols", [P, HIDC])
    din("su_cols", [P, HIDC])
    din("sd_cols", [P, DMC])
    din("id_f32", [P, P])
    din("id_bf16", [P, P], BF16)
    din("tri_bf16", [P, P], BF16)
    din("tri_scaled", [P, P], BF16)
    din("selrep", [P, N * P], BF16)
    din("selred", [P, N * P], BF16)
    din("sB", [32, P])
    din("sC", [32, P])
    din("ones_c_bf", [P, 1], BF16)
    din("ones_c_f32", [P, 1])
    din("ones_r_bf", [1, P], BF16)
    din("ones_r_f32", [1, P])
    din("m128_r", [1, P], BF16)
    din("role_in", [1, 1], I32)
    din("magic", [1, 40], I32)
    din("magic_cmp", [1, 8], I32)

    dout("y", [T, DM])

    dint("xc_dram", [DI, T], BF16)
    dint("sz_dram", [DI, T], BF16)
    dint("xc_scan", [DIH, TT], BF16)
    dint("sz_scan", [DIH, TT], BF16)
    dint("xt_scan", [DIH, TT], BF16)
    dint("ya_loc", [DIH, TT], BF16)
    dint("dt_dbg", [DIH, TT], BF16)
    dint("scan_dbg", [P, 3072], F32)
    dint("ya_own", [DI, T], BF16)
    dint("bc_dram", [32, TT], F32)
    dint("attn_loc", [7168, T], BF16)
    dint("attn_q", [HH * DK, TT], BF16)
    dint("attn_k", [HH * DK, TT], BF16)
    dint("attn_s", [HH * M_ABC, TT], BF16)
    dint("attn_v", [HH * DV, TT], BF16)
    dint("attn_g", [HH * DV, TT], BF16)
    dint("o_own", [H * DV, T], BF16)
    for ab in "ab":
        dint(f"sh_e1_{ab}", [DI, T], BF16, shared=True)
        dint(f"sh_e2_{ab}", [160, TT], F32, shared=True)
        dint(f"sh_e3_{ab}", [DIH, T], BF16, shared=True)
        dint(f"sh_e4_{ab}", [3584, T], BF16, shared=True)
        dint(f"sh_e5_{ab}", [HH * DV, T], BF16, shared=True)
        for k in range(5):
            dint(f"fl{k}_{ab}", [1, 8], I32, shared=True)
    for nm, outn in DEBUG_COPY:
        shp = {"xc_dram": [DI, T], "sz_dram": [DI, T], "xc_scan": [DIH, TT],
               "sz_scan": [DIH, TT], "xt_scan": [DIH, TT], "ya_loc": [DIH, TT],
               "ya_own": [DI, T], "attn_loc": [7168, T], "o_own": [H * DV, T],
               "dt_dbg": [DIH, TT], "scan_dbg": [P, 3072],
               "attn_q": [HH * DK, TT], "attn_s": [HH * M_ABC, TT]}[nm]
        dout(outn, shp, F32 if nm == "scan_dbg" else BF16)

    with ExitStack() as es:
        def sb(name, shape, dt=F32):
            return es.enter_context(nc.sbuf_tensor(name, shape, dt))

        def psum(name, shape, dt=F32):
            return es.enter_context(nc.psum_tensor(name, shape, dt))

        s_xacc = sb("s_xacc", [P, DMC * T])
        s_xn = sb("s_xn", [P, DMC * T], BF16)
        s_mid = sb("s_mid", [P, FC_MOE * T], BF16)
        s_big0 = sb("s_big0", [P, 2048])
        s_big1 = sb("s_big1", [P, 2304])
        s_big2 = sb("s_big2", [P, 2048])
        s_big3 = sb("s_big3", [P, 2048])
        s_big4 = sb("s_big4", [P, 2048])
        s_wbuf0 = sb("s_wbuf0", [P, DM], BF16)
        s_wbuf1 = sb("s_wbuf1", [P, DM], BF16)
        s_wbig = sb("s_wbig", [P, 6144], BF16)
        s_lr = sb("s_lr", [P, DM], BF16)
        s_rows = sb("s_rows", [1, 2048])
        s_wr = sb("s_wr", [P, P])
        s_hcarry = sb("s_hcarry", [P, DIHC * N])
        s_idf = sb("s_idf", [P, P])
        s_idb = sb("s_idb", [P, P], BF16)
        s_tri = sb("s_tri", [P, P], BF16)
        s_sB = sb("s_sB", [32, P])
        s_sC = sb("s_sC", [32, P])
        s_ones_c_bf = sb("s_ones_c_bf", [P, 1], BF16)
        s_ones_c_f = sb("s_ones_c_f", [P, 1])
        s_ones_r_bf = sb("s_ones_r_bf", [1, P], BF16)
        s_ones_r_f = sb("s_ones_r_f", [1, P])
        s_m128 = sb("s_m128", [1, P], BF16)
        s_role = sb("s_role", [1, 1], I32)
        s_magic = sb("s_magic", [1, 40], I32)
        s_mcmp = sb("s_mcmp", [1, 8], I32)
        s_gnw = sb("s_gnw", [P, 1])
        s_temp = sb("s_temp", [1, 1])
        s_rtemp = sb("s_rtemp", [P, 2])
        s_ceps = s_rtemp[:, 1:2]
        s_convw = sb("s_convw", [P, DIHC * 4])
        s_convb = sb("s_convb", [P, DIHC])
        s_dtb = sb("s_dtb", [P, DIHC])
        s_alanes = sb("s_alanes", [P, DIHC * N])
        s_dloc = sb("s_dloc", [P, DIHC])
        s_sgc = sb("s_sgc", [P, HIDC])
        s_suc = sb("s_suc", [P, HIDC])
        s_sdc = sb("s_sdc", [P, DMC])
        s_sdm128 = sb("s_sdm128", [P, DMC])
        s_flag = sb("s_flag", [1, 8], I32)

        s_logits = s_lr.bitcast(F32)[64:64 + E_EXP, 0:T]

        row0 = s_rows[0:1, :]
        rowb0 = s_rows.bitcast(BF16)[0:1, 2048:3072]
        rowb1 = s_rows.bitcast(BF16)[0:1, 3072:4096]
        s_crow = s_rows.bitcast(BF16)[0:1, 0:64]
        s_wfm = s_big3[0:8, 1024:2048]

        PS = [psum(f"PS{i}", [P, 512]) for i in range(8)]
        PSB = [p.bitcast(BF16) for p in PS]

        qg = es.enter_context(nc.semaphore("qg"))
        qs = es.enter_context(nc.semaphore("qs"))
        qc = es.enter_context(nc.semaphore("qc"))
        tS = es.enter_context(nc.semaphore("tS"))
        vS = es.enter_context(nc.semaphore("vS"))
        sS = es.enter_context(nc.semaphore("sS"))
        spin = es.enter_context(nc.semaphore("spin"))
        SEMS = {"qg": qg, "qs": qs, "qc": qc, "tS": tS, "vS": vS, "sS": sS}
        ENGSEM = {"tensor": "tS", "vector": "vS", "scalar": "sS"}
        C = {k: 0 for k in SEMS}
        chain = [None]

        block = es.enter_context(nc.Block())

        _SYNCED_OPS = {
            "tensor_tensor", "tensor_add", "tensor_sub", "tensor_mul", "tensor_max",
            "tensor_copy", "tensor_scalar", "tensor_single_scalar",
            "tensor_scalar_add", "tensor_scalar_sub", "tensor_scalar_mul",
            "tensor_scalar_max", "tensor_scalar_min", "tensor_reduce",
            "reduce_sum", "reduce_max", "reciprocal", "memset", "memzero",
            "tensor_tensor_scan", "scalar_tensor_tensor", "select",
            "activation", "copy", "add", "mul", "sqrt", "square", "sign",
        }

        class _SeqProxy:
            def __init__(self, E, sem, base):
                self._E = E
                self._sem = sem
                self._n = [base]

            def __getattr__(self, name):
                fnb = getattr(self._E, name)
                if name in _SYNCED_OPS:
                    def wrapped(*a, **k):
                        if self._n[0] > 0:
                            self._E.wait_ge(self._sem, self._n[0])
                        inst = fnb(*a, **k)
                        inst.then_inc(self._sem, 1)
                        self._n[0] += 1
                        return inst
                    return wrapped
                return fnb

        def _mkwaits(deps):
            if deps is None:
                dl = [chain[0]] if chain[0] is not None else []
            else:
                dl = [d for d in deps if d is not None]
            md = {}
            for sname, cntv in dl:
                if cntv > 0:
                    md[sname] = max(md.get(sname, 0), cntv)
            return md

        def barrier_deps():
            return [(s, C[s]) for s in SEMS if C[s] > 0]

        class _NoSeqProxy:
            def __init__(self, E, sem, base):
                self._E = E
                self._sem = sem
                self._n = [base]

            def __getattr__(self, name):
                fnb = getattr(self._E, name)
                if name in _SYNCED_OPS:
                    def wrapped(*a, **k):
                        inst = fnb(*a, **k)
                        inst.then_inc(self._sem, 1)
                        self._n[0] += 1
                        return inst
                    return wrapped
                return fnb

        def step(eng, fn, ndma=None, deps=None):
            md = _mkwaits(deps)
            if eng in ("vector_ns", "scalar_ns"):
                ename = eng[:-3]
                semn = ENGSEM[ename]
                base = C[semn]
                holder = [None]

                def body(E):
                    for sname, cntv in md.items():
                        E.wait_ge(SEMS[sname], cntv)
                    px = _NoSeqProxy(E, SEMS[semn], base)
                    fn(px)
                    holder[0] = px._n[0]
                getattr(block, ename)(body)
                assert holder[0] is not None and holder[0] > base, "empty group"
                C[semn] = holder[0]
                chain[0] = (semn, C[semn])
                return chain[0]
            if eng in ("sync", "gpsimd_dma", "scalar_dma"):
                qname = {"sync": "qs", "gpsimd_dma": "qg", "scalar_dma": "qc"}[eng]
                qsem = SEMS[qname]
                blkfn = {"sync": block.sync, "gpsimd_dma": block.gpsimd,
                         "scalar_dma": block.scalar}[eng]
                qprev = C[qname]

                def body(E):
                    if qprev > 0:
                        E.wait_ge(qsem, qprev)
                    for sname, cntv in md.items():
                        E.wait_ge(SEMS[sname], cntv)

                    def dma(dst, src):
                        E.dma_start(out=dst, in_=src).then_inc(qsem, 16)
                    dma.sem = qsem
                    dma.base = qprev
                    fn(E, dma)
                blkfn(body)
                C[qname] += 16 * ndma
                chain[0] = (qname, C[qname])
            elif eng == "tensor":
                semn = ENGSEM[eng]

                def body(E):
                    for sname, cntv in md.items():
                        E.wait_ge(SEMS[sname], cntv)
                    last = fn(E)
                    last.then_inc(SEMS[semn], 1)
                getattr(block, eng)(body)
                C[semn] += 1
                chain[0] = (semn, C[semn])
            else:
                semn = ENGSEM[eng]
                base = C[semn]
                holder = [None]

                def body(E):
                    for sname, cntv in md.items():
                        E.wait_ge(SEMS[sname], cntv)
                    px = _SeqProxy(E, SEMS[semn], base)
                    fn(px)
                    holder[0] = px._n[0]
                getattr(block, eng)(body)
                assert holder[0] is not None and holder[0] > base, "empty group"
                C[semn] = holder[0]
                chain[0] = (semn, C[semn])
            return chain[0]

        _scope_state = [None]

        def phase(name):
            if _scope_state[0] is not None:
                pn, pid = _scope_state[0]
                nc.leave_named_scope(pn, pid, False)
                _scope_state[0] = None
            if name is not None:
                sid, _ = nc.enter_named_scope(name, False)
                _scope_state[0] = (name, sid)

        xn3 = s_xn.rearrange("p (dc t) -> p dc t", t=T)
        smf = s_mid.bitcast(F32)
        smb = s_mid

        phase("ph0_consts")
        const_pairs = [
            (s_idf, "id_f32"), (s_idb, "id_bf16"), (s_tri, "tri_bf16"),
            (s_sB, "sB"), (s_sC, "sC"),
            (s_ones_c_bf, "ones_c_bf"), (s_ones_c_f, "ones_c_f32"),
            (s_ones_r_bf, "ones_r_bf"), (s_ones_r_f, "ones_r_f32"),
            (s_m128, "m128_r"), (s_role, "role_in"), (s_magic, "magic"),
            (s_mcmp, "magic_cmp"), (s_gnw, "gnw"), (s_temp, "temp"),
            (s_convw, "conv_w_loc"), (s_convb, "conv_b_loc"), (s_dtb, "dtb_loc"),
            (s_alanes, "alog_lanes"), (s_dloc, "d_loc"), (s_sgc, "sg_cols"),
            (s_suc, "su_cols"), (s_sdc, "sd_cols"), (s_wr, None),
        ]

        def ph0(E, dma):
            for dst, nm in const_pairs:
                if nm is None:
                    continue
                dma(dst[:], dram[nm])
            dma(s_wr.rearrange("p (dc e) -> p dc e", e=E_EXP)[:, 0:DMC, :],
                dram["wr"].rearrange("(dc p) e -> p dc e", p=P))
        step("gpsimd_dma", ph0, ndma=len(const_pairs))

        step("vector", lambda v: v.memset(s_ceps, EPS_LN))
        step("scalar", lambda s: s.activation(s_alanes[:], s_alanes[:], AF.Exp))
        step("vector", lambda v: v.tensor_scalar_mul(s_alanes[:], s_alanes[:], -1.0))
        step("vector", lambda v: v.tensor_scalar_mul(s_sdm128[:], s_sdc[:], -128.0))
        step("vector", lambda v: v.reciprocal(s_rtemp[0:1, 0:1], s_temp[0:1, 0:1]))
        step("tensor", lambda t: t.matmul(PS[0][:, 0:1], s_ones_r_f[:], s_rtemp[0:1, 0:1],
                                          start=True, stop=True))
        step("vector", lambda v: v.tensor_copy(s_rtemp[:, 0:1], PS[0][:, 0:1]))

        phase("ph1_ln0")
        NT = T // P
        K1 = {}
        bd1 = barrier_deps()
        xbufs1 = [s_big0, s_big4]
        for i in range(NT):
            xb1 = xbufs1[i % 2]
            K1['l', i] = step("sync", lambda E, dma, i=i, xb1=xb1:
                 dma(xb1[:, 0:DM], dram["x_own"][i * P:(i + 1) * P, :]), ndma=1,
                 deps=(bd1 if i < 2 else [K1['t', i - 2, DMC - 1]]))

            def stats1(v, xb1=xb1):
                v.reduce_sum(s_big3[:, 0:1], xb1[:, 0:DM], AX.X)
                v.tensor_mul(s_big2[:, 0:DM], xb1[:, 0:DM], xb1[:, 0:DM])
                v.reduce_sum(s_big3[:, 1:2], s_big2[:, 0:DM], AX.X)
                v.tensor_scalar_mul(s_big3[:, 2:3], s_big3[:, 0:1], -1.0 / DM)
                v.tensor_scalar_mul(s_big3[:, 3:4], s_big3[:, 1:2], 1.0 / DM)
                v.tensor_mul(s_big3[:, 4:5], s_big3[:, 2:3], s_big3[:, 2:3])
                return v.tensor_sub(s_big3[:, 5:6], s_big3[:, 3:4], s_big3[:, 4:5])
            K1['v1', i] = step("vector", stats1,
                deps=[K1['l', i]] + ([K1['rm', i - 1]] if i else []))

            def rstd(s):
                s.activation(s_big3[:, 6:7], s_big3[:, 5:6], AF.Ln, bias=s_ceps)
                return s.activation(s_big3[:, 7:8], s_big3[:, 6:7], AF.Exp, scale=-0.5)
            K1['s1', i] = step("scalar", rstd, deps=[K1['v1', i]])
            K1['v2', i] = step("vector", lambda v, xb1=xb1:
                 v.tensor_scalar(s_big1[:, 0:DM], xb1[:, 0:DM],
                                 s_big3[:, 2:3], s_big3[:, 7:8],
                                 ALU.add, ALU.mult),
                 deps=[K1['s1', i]] + ([K1['t', i - 1, DMC - 1]] if i else []))
            for dc in range(DMC):
                par = dc % 2

                def tr(t, dc=dc, par=par, xb1=xb1):
                    t.matmul(PS[par][:, 0:P], xb1[:, dc * P:(dc + 1) * P], s_idf[:],
                             is_transpose=True, start=True, stop=True)
                    return t.matmul(PS[2 + par][:, 0:P], s_big1[:, dc * P:(dc + 1) * P],
                                    s_idf[:], is_transpose=True, start=True, stop=True)
                K1['t', i, dc] = step("tensor", tr,
                    deps=[K1['v2', i]] + ([K1['e', i, dc - 2]] if dc >= 2 else
                                          (bd1 if i == 0 else [])))

                def ev(v, dc=dc, par=par, i=i):
                    v.tensor_copy(s_xacc[:, dc * T + i * P: dc * T + (i + 1) * P], PS[par][:, 0:P])
                    v.tensor_copy(xn3[:, dc, i * P:(i + 1) * P], PS[2 + par][:, 0:P])
                    return v.tensor_copy(s_big2[:, dc * P:(dc + 1) * P], PS[2 + par][:, 0:P])
                K1['e', i, dc] = step("vector", ev, deps=[K1['t', i, dc]])

            def router_mm(t, i=i):
                for dc in range(DMC):
                    last = t.matmul(PS[4][0:E_EXP, 0:P],
                                    s_wr.rearrange("p (dc e) -> p dc e", e=E_EXP)[:, dc, :],
                                    s_big2[:, dc * P:(dc + 1) * P],
                                    start=(dc == 0), stop=(dc == DMC - 1))
                return last
            K1['rm', i] = step("tensor", router_mm,
                deps=[K1['e', i, DMC - 1]] + ([K1['lc', i - 1]] if i else []))
            K1['lc', i] = step("vector", lambda v, i=i:
                 v.tensor_copy(s_logits[:, i * P:(i + 1) * P], PS[4][0:E_EXP, 0:P]),
                 deps=[K1['rm', i]])

        phase("ph2_inproj")
        K2 = {}
        bd2 = barrier_deps()
        for fc in range(64):
            fb2 = fc % 2
            wb2 = s_wbuf0 if fb2 == 0 else s_wbuf1
            eb2 = s_big0.bitcast(BF16)[:, fb2 * 2048:fb2 * 2048 + 2048]

            def wload(E, dma, fc=fc, wb2=wb2):
                dma(wb2.rearrange("p (dc j) -> p dc j", j=P),
                    dram["w_inT"][:, fc * P:(fc + 1) * P].rearrange("(dc p) j -> p dc j", p=P))
            K2['w', fc] = step("gpsimd_dma", wload, ndma=1,
                deps=(bd2 if fc < 2 else [K2['t', fc - 2]]))

            def mm(t, fc=fc, wb2=wb2, fb2=fb2):
                b3 = wb2.rearrange("p (dc j) -> p dc j", j=P)
                for half in range(2):
                    for dc in range(DMC):
                        last = t.matmul(PS[2 * fb2 + half][:, :], b3[:, dc, :],
                                        xn3[:, dc, half * 512:(half + 1) * 512],
                                        start=(dc == 0), stop=(dc == DMC - 1))
                return last
            K2['t', fc] = step("tensor", mm,
                deps=[K2['w', fc]] + ([K2['e', fc - 2]] if fc >= 2 else bd2))

            if fc < 32:
                def evx(v, eb2=eb2, fb2=fb2):
                    v.tensor_copy(eb2[:, 0:512], PS[2 * fb2][:, :])
                    return v.tensor_copy(eb2[:, 512:1024], PS[2 * fb2 + 1][:, :])
                K2['e', fc] = step("vector", evx,
                    deps=[K2['t', fc]] + ([K2['o', fc - 2]] if fc >= 2 else []))
            else:
                def evz(s, eb2=eb2, fb2=fb2):
                    s.activation(eb2[:, 0:512], PS[2 * fb2][:, :], AF.Silu)
                    return s.activation(eb2[:, 512:1024], PS[2 * fb2 + 1][:, :], AF.Silu)
                K2['e', fc] = step("scalar", evz,
                    deps=[K2['t', fc]] + ([K2['o', fc - 2]] if fc >= 2 else
                                          [K2['o', 30 + fc - 32]] if fc < 34 else []))
            dst2 = (dram["xc_dram"][fc * P:(fc + 1) * P, :] if fc < 32 else
                    dram["sz_dram"][(fc - 32) * P:(fc - 31) * P, :])
            K2['o', fc] = step("sync", lambda E, dma, dst2=dst2, eb2=eb2:
                 dma(dst2, eb2[:, 0:T]), ndma=1, deps=[K2['e', fc]])

        def exchange(tag, k, body_fn):
            pre_a, post_a = body_fn("a")
            pre_b, post_b = body_fn("b")
            assert len(pre_a) == len(pre_b) and len(post_a) == len(post_b)
            n_total = len(pre_a) + 1 + len(post_a)
            prevs = barrier_deps()
            cqs0 = C["qs"]

            def body(sy):
                for pn, pv in prevs:
                    sy.wait_ge(SEMS[pn], pv)
                role = sy.value_load(s_role[0:1, 0:1])
                mreg = sy.value_load(s_mcmp[0:1, k:k + 1])
                with sy.register(f"ca_{tag}") as ca, sy.register(f"cb_{tag}") as cb:
                    sy.reg_alu(ca, role, 0, ALU.is_equal)
                    sy.reg_alu(cb, role, 1, ALU.is_equal)
                    for ab, creg, pre, post in (("a", ca, pre_a, post_a),
                                                ("b", cb, pre_b, post_b)):
                        other = "b" if ab == "a" else "a"
                        with sy.While(creg):
                            nd = 0
                            for dst, src in pre:
                                sy.dma_start(out=dst, in_=src).then_inc(qs, 16)
                                nd += 16
                            sy.wait_ge(qs, cqs0 + nd)
                            sy.dma_start(out=dram[f"fl{k}_{ab}"],
                                         in_=s_magic[0:1, 8 * k:8 * k + 8]).then_inc(qs, 16)
                            nd += 16
                            sy.wait_ge(qs, cqs0 + nd)
                            if not SIM_RACECHECK:
                                with sy.register(f"cw_{tag}_{ab}") as cond:
                                    sy.sem_clear(spin)
                                    sy.dma_start(out=s_flag[:], in_=dram[f"fl{k}_{other}"]).then_inc(spin, 16)
                                    sy.wait_ge(spin, 16)
                                    fl = sy.value_load(s_flag[0:1, 0:1])
                                    sy.reg_alu(cond, fl, mreg, ALU.not_equal)
                                    with sy.While(cond):
                                        sy.sem_clear(spin)
                                        sy.dma_start(out=s_flag[:], in_=dram[f"fl{k}_{other}"]).then_inc(spin, 16)
                                        sy.wait_ge(spin, 16)
                                        fl2 = sy.value_load(s_flag[0:1, 0:1])
                                        sy.reg_alu(cond, fl2, mreg, ALU.not_equal)
                            for dst, src in post:
                                sy.dma_start(out=dst, in_=src).then_inc(qs, 16)
                                nd += 16
                            sy.wait_ge(qs, cqs0 + nd)
                            sy.reg_alu(creg, creg, 0, ALU.mult)
            block.sync(body)
            C["qs"] += 16 * n_total
            chain[0] = ("qs", C["qs"])

        def e1_body(ab):
            me0 = (ab == "a")
            own_lo, own_hi = (0, T) if me0 else (T, TT)
            send = (DIH, DI) if me0 else (0, DIH)
            keep = (0, DIH) if me0 else (DIH, DI)
            other = "b" if ab == "a" else "a"
            po_lo, po_hi = (T, TT) if me0 else (0, T)
            pre = [
                (dram[f"sh_e1_{ab}"][0:DIH, :], dram["xc_dram"][send[0]:send[1], :]),
                (dram[f"sh_e1_{ab}"][DIH:DI, :], dram["sz_dram"][send[0]:send[1], :]),
                (dram["xc_scan"][:, own_lo:own_hi], dram["xc_dram"][keep[0]:keep[1], :]),
                (dram["sz_scan"][:, own_lo:own_hi], dram["sz_dram"][keep[0]:keep[1], :]),
            ]
            post = [
                (dram["xc_scan"][:, po_lo:po_hi], dram[f"sh_e1_{other}"][0:DIH, :]),
                (dram["sz_scan"][:, po_lo:po_hi], dram[f"sh_e1_{other}"][DIH:DI, :]),
            ]
            return pre, post
        phase("exch_e1")
        exchange("e1", 0, e1_body)

        phase("ph3_conv_xproj")
        def wxp_load(E, dma):
            dma(s_wbig[:, 0:DIHC * 160].rearrange("p (dc o) -> p dc o", o=160),
                dram["w_xpT_loc"].rearrange("(dc p) o -> p dc o", p=P))
        step("gpsimd_dma", wxp_load, ndma=1)
        wxp3 = s_wbig[:, 0:DIHC * 160].rearrange("p (dc o) -> p dc o", o=160)
        cw3 = s_convw.rearrange("p (dc j) -> p dc j", j=4)
        K3 = {}
        bd3 = barrier_deps()
        xcbufs = [s_big2.bitcast(BF16), s_big4.bitcast(BF16)]
        accbufs = [s_big1, s_big0]

        def pad0(v):
            v.memset(xcbufs[0][:, 0:3], 0.0)
            return v.memset(xcbufs[1][:, 0:3], 0.0)
        K3['pad'] = step("vector", pad0, deps=bd3)

        for dc in range(DIHC):
            p3 = dc % 2
            xc_in = xcbufs[p3]
            acc = accbufs[p3]
            xts = s_big3.bitcast(BF16)[:, p3 * 2048:p3 * 2048 + 2048]

            K3['l', dc] = step("sync", lambda E, dma, dc=dc, xc_in=xc_in:
                 dma(xc_in[:, 3:3 + TT], dram["xc_scan"][dc * P:(dc + 1) * P, :]), ndma=1,
                 deps=(bd3 if dc < 2 else [K3['c', dc - 2]]))

            def conv(v, dc=dc, xc_in=xc_in, acc=acc):
                v.tensor_scalar_mul(acc[:, 0:TT], xc_in[:, 0:TT], cw3[:, dc, 0:1])
                v.scalar_tensor_tensor(acc[:, 0:TT], xc_in[:, 1:1 + TT], cw3[:, dc, 1:2],
                                       acc[:, 0:TT], ALU.mult, ALU.add)
                v.scalar_tensor_tensor(acc[:, 0:TT], xc_in[:, 2:2 + TT], cw3[:, dc, 2:3],
                                       acc[:, 0:TT], ALU.mult, ALU.add)
                return v.scalar_tensor_tensor(acc[:, 0:TT], xc_in[:, 3:3 + TT], cw3[:, dc, 3:4],
                                              acc[:, 0:TT], ALU.mult, ALU.add)
            K3['c', dc] = step("vector", conv,
                deps=[K3['l', dc]] + ([K3['s', dc - 2]] if dc >= 2 else [K3['pad']]))

            K3['s', dc] = step("scalar", lambda s, dc=dc, acc=acc, xts=xts:
                 s.activation(xts[:, 0:TT], acc[:, 0:TT],
                              AF.Silu, bias=s_convb[:, dc:dc + 1]),
                 deps=[K3['c', dc]] +
                      ([K3['t', dc - 2], K3['o', dc - 2]] if dc >= 2 else []))
            K3['o', dc] = step("sync", lambda E, dma, dc=dc, xts=xts:
                 dma(dram["xt_scan"][dc * P:(dc + 1) * P, :], xts[:, 0:TT]),
                 ndma=1, deps=[K3['s', dc]])

            def xp(t, dc=dc, xts=xts):
                for tg in range(4):
                    t.matmul(PS[tg][:, :], wxp3[:, dc, 0:P],
                             xts[:, tg * 512:(tg + 1) * 512],
                             start=(dc == 0), stop=(dc == DIHC - 1))
                    last = t.matmul(PS[4 + tg][0:32, :], wxp3[:, dc, P:160],
                                    xts[:, tg * 512:(tg + 1) * 512],
                                    start=(dc == 0), stop=(dc == DIHC - 1))
                return last
            K3['t', dc] = step("tensor", xp,
                deps=[K3['s', dc]] + (bd3 if dc == 0 else []))

        def xp_evac(v):
            for tg in range(4):
                v.tensor_copy(s_big0[:, tg * 512:(tg + 1) * 512], PS[tg][:, :])
                last = v.tensor_copy(s_big1[0:32, tg * 512:(tg + 1) * 512], PS[4 + tg][0:32, :])
            return last
        step("vector", xp_evac)

        def e2_body(ab):
            other = "b" if ab == "a" else "a"
            pre = [
                (dram[f"sh_e2_{ab}"][0:P, :], s_big0[:, 0:TT]),
                (dram[f"sh_e2_{ab}"][P:160, :], s_big1[0:32, 0:TT]),
            ]
            post = [
                (s_big2[:, 0:TT], dram[f"sh_e2_{other}"][0:P, :]),
                (s_big3[0:32, 0:TT], dram[f"sh_e2_{other}"][P:160, :]),
            ]
            return pre, post
        phase("exch_e2")
        exchange("e2", 1, e2_body)

        s_dtlow = s_big4.bitcast(BF16)

        def sums(v):
            v.tensor_add(s_dtlow[:, 0:TT], s_big0[:, 0:TT], s_big2[:, 0:TT])
            return v.tensor_add(s_big3[0:32, 0:TT], s_big1[0:32, 0:TT], s_big3[0:32, 0:TT])
        step("vector", sums)
        step("sync", lambda E, dma: dma(dram["bc_dram"], s_big3[0:32, 0:TT]), ndma=1)

        phase("ph4_scan")
        def ph4_wload(E, dma):
            dma(s_wbig[:, 0:2048], dram["w_dtT_loc"])
            dma(s_wbig[:, 2048:4096], dram["selrep"])
            dma(s_wbig[:, 4096:6144], dram["selred"])
        step("gpsimd_dma", ph4_wload, ndma=3)
        s_wdt = s_wbig[:, 0:2048]
        selrep3 = s_wbig[:, 2048:4096].rearrange("p (m j) -> p m j", j=P)
        selred3 = s_wbig[:, 4096:6144].rearrange("p (m j) -> p m j", j=P)
        dA_reg = [smf[:, 0:1024], smf[:, 4096:5120]]
        v_reg = [smf[:, 1024:2048], smf[:, 3072:4096]]
        h_reg = [smf[:, 2048:3072], s_big1[:, 0:1024]]
        hc_reg = [smb[:, 14336:15360], smb[:, 15360:16384]]
        ya_stg = smb[:, 15360:16384]
        S4 = {}
        dbg_dt = any(nm == "dt_dbg" for nm, _ in DEBUG_COPY)
        dbg_scan = any(nm == "scan_dbg" for nm, _ in DEBUG_COPY)
        bds = barrier_deps()

        for th in range(2):
            tlo = th * 1024
            S4['bl', th] = step("sync", lambda E, dma, tlo=tlo:
                 dma(s_big0[0:32, 0:1024], dram["bc_dram"][:, tlo:tlo + 1024]), ndma=1,
                 deps=(bds if th == 0 else [S4['bt', 0]]))

            def bc_build(t):
                for half in range(2):
                    t.matmul(PS[4 + half][:, :], s_sB[:, :],
                             s_big0[0:32, half * 512:(half + 1) * 512],
                             start=True, stop=True)
                    last = t.matmul(PS[6 + half][:, :], s_sC[:, :],
                                    s_big0[0:32, half * 512:(half + 1) * 512],
                                    start=True, stop=True)
                return last
            S4['bt', th] = step("tensor", bc_build,
                deps=[S4['bl', th]] +
                     ([S4['v1a', th - 1, DIHC - 1, N - 1], S4['ya', th - 1, DIHC - 1]]
                      if th else bds))

            def bc_ev(v):
                for half in range(2):
                    v.tensor_copy(smf[:, 5120 + half * 512:5120 + (half + 1) * 512],
                                  PS[4 + half][:, :])
                    last = v.tensor_copy(smf[:, 6144 + half * 512:6144 + (half + 1) * 512],
                                         PS[6 + half][:, :])
                return last
            S4['be', th] = step("vector", bc_ev, deps=[S4['bt', th]])

            for dc in range(DIHC):
                tb = dc % 2
                wbt = s_wbuf0 if tb == 0 else s_wbuf1

                def tile_ld(E, dma, dc=dc, tlo=tlo, wbt=wbt):
                    dma(wbt[:, 0:1024], dram["xt_scan"][dc * P:(dc + 1) * P, tlo:tlo + 1024])
                    dma(wbt[:, 1024:2048], dram["sz_scan"][dc * P:(dc + 1) * P, tlo:tlo + 1024])
                tl_deps = []
                if (th, dc) == (0, 0):
                    tl_deps = bds
                elif (th, dc) == (0, 1):
                    tl_deps = bds
                else:
                    pdc = dc - 2
                    pth = th
                    if pdc < 0:
                        pdc += DIHC
                        pth -= 1
                    tl_deps = [S4['ya', pth, pdc]]
                S4['tl', th, dc] = step("sync", tile_ld, ndma=2, deps=tl_deps)

                def dt_mm(t, dc=dc, tlo=tlo):
                    for half in range(2):
                        last = t.matmul(PS[half][:, :], s_wdt[:, dc * P:(dc + 1) * P],
                                        s_dtlow[:, tlo + half * 512: tlo + (half + 1) * 512],
                                        start=True, stop=True)
                    return last
                dtm_deps = []
                if dc == 0:
                    dtm_deps = [S4['be', th]] if th else bds
                else:
                    dtm_deps = [S4['sv1', th, dc - 1]]
                S4['dt', th, dc] = step("tensor", dt_mm, deps=dtm_deps)

                def sp_v1(v, dc=dc):
                    for half in range(2):
                        v.tensor_scalar_add(smf[:, 3072 + half * 512:3072 + (half + 1) * 512],
                                            PS[half][:, :], s_dtb[:, dc:dc + 1])
                    v.tensor_scalar_mul(smf[:, 4096:5120], smf[:, 3072:4096], -1.0)
                    return v.tensor_max(smf[:, 4096:5120], smf[:, 4096:5120],
                                        smf[:, 3072:4096])
                sv1_deps = [S4['dt', th, dc]]
                if (th, dc) != (0, 0):
                    pdc, pth = (dc - 1, th) if dc else (DIHC - 1, th - 1)
                    sv1_deps += [S4['v1b', pth, pdc, N - 1]]
                S4['sv1', th, dc] = step("vector", sp_v1, deps=sv1_deps)

                def sp_s(s):
                    s.activation(smf[:, 4096:5120], smf[:, 4096:5120], AF.Exp, scale=-1.0)
                    return s.activation(smf[:, 4096:5120], smf[:, 4096:5120], AF.Ln, bias=1.0)
                S4['sp', th, dc] = step("scalar", sp_s, deps=[S4['sv1', th, dc]])

                def sp_v2(v, dc=dc, wbt=wbt):
                    v.tensor_scalar_max(smf[:, 0:1024], smf[:, 3072:4096], 0.0)
                    v.tensor_add(s_dtlow[:, 2048:3072], smf[:, 4096:5120], smf[:, 0:1024])
                    return v.tensor_mul(s_dtlow[:, 3072:4096], s_dtlow[:, 2048:3072],
                                        wbt[:, 0:1024])
                spv2_deps = [S4['sp', th, dc], S4['tl', th, dc]]
                if (th, dc) != (0, 0):
                    pdc, pth = (dc - 1, th) if dc else (DIHC - 1, th - 1)
                    spv2_deps += [S4['rp', pth, pdc, N - 1]]
                S4['dtu', th, dc] = step("vector", sp_v2, deps=spv2_deps)
                if dbg_dt:
                    step("sync", lambda E, dma, dc=dc, tlo=tlo:
                         dma(dram["dt_dbg"][dc * P:(dc + 1) * P, tlo:tlo + 1024],
                             s_dtlow[:, 2048:3072]), ndma=1)

                if (th, dc) != (0, 0):
                    pdc, pth = (dc - 1, th) if dc else (DIHC - 1, th - 1)
                else:
                    pdc = pth = None

                def mk_red(m):
                    mp = m % 2

                    def red(t, m=m, mp=mp):
                        for half in range(2):
                            last = t.matmul(PS[6 + half][:, :], selred3[:, m, :],
                                            hc_reg[mp][:, half * 512:(half + 1) * 512],
                                            start=(m == 0), stop=(m == N - 1))
                        return last
                    rd_deps = [S4['vh', th, dc, m]]
                    if m == 0 and pdc is not None:
                        rd_deps += [S4['ya', pth, pdc]]
                    S4['rd', th, dc, m] = step("tensor", red, deps=rd_deps)

                def mk_vh(m):
                    mp = m % 2
                    k = dc * N + m

                    def hCf(v, m=m, mp=mp, k=k, th=th):
                        last = v.tensor_mul(hc_reg[mp][:, 0:1024], h_reg[mp][:, 0:1024],
                                            smf[:, 6144:7168])
                        if th == 0:
                            last = v.tensor_copy(s_hcarry[:, k:k + 1],
                                                 h_reg[mp][:, 1023:1024])
                        return last
                    vh_deps = [S4['v1b', th, dc, m]]
                    if m >= 2:
                        vh_deps += [S4['rd', th, dc, m - 2]]
                    elif pdc is not None:
                        vh_deps += [S4['rd', pth, pdc, N - 1]]
                        if m == 1:
                            vh_deps += [S4['yo', pth, pdc]]
                    S4['vh', th, dc, m] = step("vector_ns", hCf, deps=vh_deps)

                for m in range(N):
                    k = dc * N + m
                    mp = m % 2

                    def rep(t, m=m):
                        for sub in range(2):
                            t.matmul(PS[2 + sub][:, :], selrep3[:, m, :],
                                     s_dtlow[:, 2048 + sub * 512:2048 + (sub + 1) * 512],
                                     start=True, stop=True)
                            last = t.matmul(PS[4 + sub][:, :], selrep3[:, m, :],
                                            s_dtlow[:, 3072 + sub * 512:3072 + (sub + 1) * 512],
                                            start=True, stop=True)
                        return last
                    if m == 0:
                        rp_deps = [S4['dtu', th, dc]]
                        if pdc is not None:
                            rp_deps += [S4['da', pth, pdc, N - 1],
                                        S4['v1a', pth, pdc, N - 1]]
                    else:
                        rp_deps = [S4['da', th, dc, m - 1], S4['v1a', th, dc, m - 1]]
                    S4['rp', th, dc, m] = step("tensor", rep, deps=rp_deps)

                    def dA(s, k=k, mp=mp):
                        s.activation(dA_reg[mp][:, 0:512], PS[2][:, :], AF.Exp,
                                     scale=s_alanes[:, k:k + 1])
                        return s.activation(dA_reg[mp][:, 512:1024], PS[3][:, :], AF.Exp,
                                            scale=s_alanes[:, k:k + 1])
                    da_deps = [S4['rp', th, dc, m]]
                    if m >= 2:
                        da_deps += [S4['v1b', th, dc, m - 2]]
                    elif pdc is not None:
                        da_deps += [S4['v1b', pth, pdc, N - 1]]
                    S4['da', th, dc, m] = step("scalar_ns", dA, deps=da_deps)

                    def v1af(v, mp=mp):
                        v.tensor_mul(v_reg[mp][:, 0:512], PS[4][:, :], smf[:, 5120:5632])
                        return v.tensor_mul(v_reg[mp][:, 512:1024], PS[5][:, :],
                                            smf[:, 5632:6144])
                    va_deps = [S4['rp', th, dc, m]]
                    if m >= 2:
                        va_deps += [S4['v1b', th, dc, m - 2]]
                    elif pdc is not None:
                        va_deps += [S4['v1b', pth, pdc, N - 1]]
                    S4['v1a', th, dc, m] = step("vector_ns", v1af, deps=va_deps)

                    if m > 0:
                        mk_vh(m - 1)
                        mk_red(m - 1)

                    def scanf(v, k=k, th=th, mp=mp):
                        init = 0.0 if th == 0 else s_hcarry[:, k:k + 1]
                        return v.tensor_tensor_scan(h_reg[mp][:, 0:1024],
                                                    dA_reg[mp][:, 0:1024],
                                                    v_reg[mp][:, 0:1024], init,
                                                    ALU.mult, ALU.add)
                    sc_deps = [S4['v1a', th, dc, m], S4['da', th, dc, m]]
                    if m >= 2:
                        sc_deps += [S4['vh', th, dc, m - 2]]
                    elif pdc is not None:
                        sc_deps += [S4['vh', pth, pdc, N - 1]]
                    S4['v1b', th, dc, m] = step("vector_ns", scanf, deps=sc_deps)

                    if dbg_scan and th == 0 and dc == 0 and m == 0:
                        step("sync", lambda E, dma:
                             dma(dram["scan_dbg"], smf[:, 0:3072]), ndma=1)

                mk_vh(N - 1)
                mk_red(N - 1)

                def ya(v, dc=dc, wbt=wbt):
                    for half in range(2):
                        v.scalar_tensor_tensor(smf[:, 3072 + half * 512:3072 + (half + 1) * 512],
                                               wbt[:, half * 512:(half + 1) * 512],
                                               s_dloc[:, dc:dc + 1],
                                               PS[6 + half][:, :], ALU.mult, ALU.add)
                    return v.tensor_mul(ya_stg[:, 0:1024], smf[:, 3072:4096],
                                        wbt[:, 1024:2048])
                S4['ya', th, dc] = step("vector", ya, deps=[S4['rd', th, dc, N - 1]])
                S4['yo', th, dc] = step("scalar_dma", lambda E, dma, dc=dc, tlo=tlo:
                     dma(dram["ya_loc"][dc * P:(dc + 1) * P, tlo:tlo + 1024],
                         ya_stg[:, 0:1024]), ndma=1, deps=[S4['ya', th, dc]])

        def e3_body(ab):
            me0 = (ab == "a")
            own = (0, T) if me0 else (T, TT)
            myrows = (0, DIH) if me0 else (DIH, DI)
            other = "b" if ab == "a" else "a"
            pr = (DIH, DI) if me0 else (0, DIH)
            pre = [
                (dram[f"sh_e3_{ab}"], dram["ya_loc"][:, (T if me0 else 0):(TT if me0 else T)]),
                (dram["ya_own"][myrows[0]:myrows[1], :], dram["ya_loc"][:, own[0]:own[1]]),
            ]
            post = [(dram["ya_own"][pr[0]:pr[1], :], dram[f"sh_e3_{other}"])]
            return pre, post
        phase("exch_e3")
        exchange("e3", 2, e3_body)

        phase("ph4b_outproj")
        G = {}
        bd4 = barrier_deps()
        ob4 = s_big4.bitcast(BF16)
        for grp in range(16):
            gb_ = grp % 2
            wb4 = s_wbig[:, 0:4096] if gb_ == 0 else ob4[:, 0:4096]
            yb4 = smb[:, 0:2048] if gb_ == 0 else smb[:, 2048:4096]

            def wld(E, dma, grp=grp, wb4=wb4):
                dma(wb4.rearrange("p (j d) -> p j d", d=DM),
                    dram["w_outT"][grp * 256:(grp + 1) * 256, :].rearrange("(j p) d -> p j d", p=P))
            G['w', grp] = step("gpsimd_dma", wld, ndma=1,
                deps=(bd4 if grp < 2 else [G['t', grp - 2, DMC - 1]]))

            def yld(E, dma, grp=grp, yb4=yb4):
                dma(yb4.rearrange("p (j t) -> p j t", t=T),
                    dram["ya_own"][grp * 256:(grp + 1) * 256, :].rearrange("(j p) t -> p j t", p=P))
            G['y', grp] = step("sync", yld, ndma=1,
                deps=(bd4 if grp < 2 else [G['t', grp - 2, DMC - 1]]))

            for dm in range(DMC):
                pb4 = 2 * (dm % 2)

                def opmm(t, dm=dm, wb4=wb4, yb4=yb4, pb4=pb4):
                    w3 = wb4.rearrange("p (j d) -> p j d", d=DM)
                    y3 = yb4.rearrange("p (j t) -> p j t", t=T)
                    for half in range(2):
                        for j in range(2):
                            last = t.matmul(PS[pb4 + half][:, :], w3[:, j, dm * P:(dm + 1) * P],
                                            y3[:, j, half * 512:(half + 1) * 512],
                                            start=(j == 0), stop=(j == 1))
                    return last
                op_deps = [G['w', grp], G['y', grp]]
                if dm >= 2:
                    op_deps += [G['v', grp, dm - 2]]
                elif grp >= 1:
                    op_deps += [G['v', grp - 1, DMC - 2 + dm]]
                else:
                    op_deps += bd4
                G['t', grp, dm] = step("tensor", opmm, deps=op_deps)

                def opadd(v, dm=dm, pb4=pb4):
                    for half in range(2):
                        last = v.scalar_tensor_tensor(
                            s_xacc[:, dm * T + half * 512: dm * T + (half + 1) * 512],
                            PS[pb4 + half][:, :], 1.0,
                            s_xacc[:, dm * T + half * 512: dm * T + (half + 1) * 512],
                            ALU.mult, ALU.add)
                    return last
                G['v', grp, dm] = step("vector", opadd, deps=[G['t', grp, dm]])

        phase("ph5_moe")
        HM = {}
        bdr = barrier_deps()
        for i in range(NT):
            bk = i % 2
            HM['rt', i] = step("tensor", lambda t, i=i, bk=bk:
                 t.matmul(PS[bk][:, 0:E_EXP], s_logits[:, i * P:(i + 1) * P],
                          s_idf[64:64 + E_EXP, 64:64 + E_EXP],
                          is_transpose=True, start=True, stop=True),
                 deps=(bdr if i < 2 else [HM['rv', i - 2]]))
            HM['rv', i] = step("vector", lambda v, i=i, bk=bk:
                 v.tensor_copy(s_big0[:, i * E_EXP:(i + 1) * E_EXP], PS[bk][:, 0:E_EXP]),
                 deps=[HM['rt', i]])

        step("scalar", lambda s: s.activation(s_big2[:, 0:64], s_big0[:, 0:64],
                                              AF.Sigmoid, scale=s_rtemp[:, 0:1]))

        def rv1(v):
            v.tensor_scalar_add(s_big2[:, 64:128], s_big2[:, 0:64], -GOLDEN_CENTER)
            v.tensor_scalar(s_big2[:, 128:192], s_big2[:, 0:64], GOLDEN_LOWER, None, ALU.is_ge)
            v.tensor_scalar(s_big2[:, 192:256], s_big2[:, 0:64], GOLDEN_UPPER, None, ALU.is_le)
            return v.tensor_mul(s_big2[:, 128:192], s_big2[:, 128:192], s_big2[:, 192:256])
        step("vector", rv1)

        def rsig(s):
            s.activation(s_big2[:, 256:320], s_big2[:, 64:128], AF.Abs)
            s.activation(s_big2[:, 320:384], s_big2[:, 256:320], AF.Exp, scale=-10.0)
            return s.activation(s_big2[:, 384:448], s_big2[:, 256:320], AF.Exp, scale=-1.0 / 0.3)
        step("scalar", rsig)

        def rv2(v):
            v.tensor_mul(s_big2[:, 448:512], s_big2[:, 320:384], s_big2[:, 128:192])
            for i in range(NT):
                def a(c):
                    return s_big2[:, c + i * E_EXP: c + (i + 1) * E_EXP]
                wt = a(448)
                fb = a(384)
                v.reduce_sum(s_big3[:, i:i + 1], wt, AX.X)
                v.tensor_scalar(s_big3[:, 8 + i:9 + i], s_big3[:, i:i + 1], 1e-8, None, ALU.is_lt)
                v.reduce_max(s_big3[:, 16 + i:17 + i], fb, AX.X)
                v.tensor_scalar(a(512), fb, s_big3[:, 16 + i:17 + i], None, ALU.is_ge)
                v.tensor_scalar_mul(a(576), a(512), 1e30)
                v.tensor_sub(a(576), fb, a(576))
                v.reduce_max(s_big3[:, 24 + i:25 + i], a(576), AX.X)
                v.tensor_scalar(a(640), a(576), s_big3[:, 24 + i:25 + i], None, ALU.is_ge)
                v.tensor_add(a(640), a(640), a(512))
                v.tensor_mul(a(640), a(640), fb)
                v.reduce_sum(s_big3[:, 32 + i:33 + i], a(640), AX.X)
                v.tensor_scalar_max(s_big3[:, 32 + i:33 + i], s_big3[:, 32 + i:33 + i], 1e-8)
                v.reciprocal(s_big3[:, 40 + i:41 + i], s_big3[:, 32 + i:33 + i])
                v.tensor_scalar_mul(a(640), a(640), s_big3[:, 40 + i:41 + i])
                v.tensor_sub(a(704), a(640), wt)
                v.scalar_tensor_tensor(a(704), a(704), s_big3[:, 8 + i:9 + i], wt,
                                       ALU.mult, ALU.add)
                v.reduce_sum(s_big3[:, 48 + i:49 + i], a(704), AX.X)
                v.tensor_scalar_max(s_big3[:, 48 + i:49 + i], s_big3[:, 48 + i:49 + i], 1e-8)
                v.reciprocal(s_big3[:, 56 + i:57 + i], s_big3[:, 48 + i:49 + i])
                last = v.tensor_scalar_mul(a(704), a(704), s_big3[:, 56 + i:57 + i])
            return last
        step("vector", rv2)

        hrv2 = chain[0]
        for i in range(NT):
            bk = i % 2
            HM['wt', i] = step("tensor", lambda t, i=i, bk=bk:
                 t.matmul(PS[bk][0:E_EXP, 0:P],
                          s_big2[:, 704 + i * E_EXP:704 + (i + 1) * E_EXP],
                          s_idf[:, :], is_transpose=True, start=True, stop=True),
                 deps=([hrv2] if i < 2 else [HM['wv', i - 2], hrv2]))
            HM['wv', i] = step("vector", lambda v, i=i, bk=bk:
                 v.tensor_copy(s_wfm[:, i * P:(i + 1) * P], PS[bk][0:E_EXP, 0:P]),
                 deps=[HM['wt', i]])

        moe_wg3 = dram["moe_wg"].rearrange("(e d) f -> e d f", e=E_EXP)
        moe_wu3 = dram["moe_wu"].rearrange("(e d) f -> e d f", e=E_EXP)
        moe_wd3 = dram["moe_wd"].rearrange("(e f) d -> e f d", e=E_EXP)
        ga3 = dram["moe_ga"].rearrange("(e d) r -> e d r", e=NL)
        gb3 = dram["moe_gb"].rearrange("(e r) f -> e r f", e=NL)
        ua3 = dram["moe_ua"].rearrange("(e d) r -> e d r", e=NL)
        ub3 = dram["moe_ub"].rearrange("(e r) f -> e r f", e=NL)
        da3 = dram["moe_da"].rearrange("(e f) r -> e f r", e=NL)
        db3 = dram["moe_db"].rearrange("(e r) d -> e r d", e=NL)
        mid3 = smb.rearrange("p (q t) -> p q t", t=T)

        big4b = s_big4.bitcast(BF16)
        bd0 = barrier_deps()
        for e in range(E_EXP):
            eb = e % 2

            def wrep(E, dma, e=e, eb=eb, cqs=C["qs"]):
                k_ = [0]
                off = eb * 1024

                def dmaw(dst, src):
                    dma(dst, src)
                    k_[0] += 16
                    E.wait_ge(qs, cqs + k_[0])
                dmaw(s_big2[0:1, off:off + 1024], s_wfm[e:e + 1, :])
                n = 1
                while n < P:
                    dmaw(s_big2[n:min(2 * n, P), off:off + 1024],
                         s_big2[0:min(n, P - n), off:off + 1024])
                    n *= 2
            HM['wrv', e] = step("sync", wrep, ndma=8,
                deps=(bd0 if e == 0 else [HM['mv', e - 1, 1], HM['wv', 7]]))

            if e < NL:
                def lr_load(E, dma, e=e):
                    dma(s_wbig[0:R_LR, 0:F_MOE], gb3[e])
                    dma(s_wbig[0:R_LR, F_MOE:2 * F_MOE], ub3[e])
                    dma(s_wbig[:, 4096:5120].rearrange("p (dc r) -> p dc r", r=R_LR),
                        ga3[e].rearrange("(dc p) r -> p dc r", p=P))
                    dma(s_wbig[:, 5120:6144].rearrange("p (dc r) -> p dc r", r=R_LR),
                        ua3[e].rearrange("(dc p) r -> p dc r", p=P))
                HM['lrl', e] = step("gpsimd_dma", lr_load, ndma=4,
                    deps=(bd0 if e == 0 else [HM['dm', e - 1, 15]]))
                ga_s = s_wbig[:, 4096:5120].rearrange("p (dc r) -> p dc r", r=R_LR)
                ua_s = s_wbig[:, 5120:6144].rearrange("p (dc r) -> p dc r", r=R_LR)

                for li, (off, wsrc, half) in enumerate(
                        [(0, ga_s, 0), (0, ga_s, 1), (1024, ua_s, 0), (1024, ua_s, 1)]):
                    def lrmm(t, wsrc=wsrc, half=half):
                        for dc in range(DMC):
                            last = t.matmul(PS[7][0:R_LR, :], wsrc[:, dc, :],
                                            xn3[:, dc, half * 512:(half + 1) * 512],
                                            start=(dc == 0), stop=(dc == DMC - 1))
                        return last
                    HM['lrt', e, li] = step("tensor", lrmm,
                        deps=[HM['lrl', e]] +
                             ([HM['lrv', e, li - 1]] if li else
                              ([HM['mv', e - 1, 15], HM['dav', e - 1, 1]]
                               if e else bd0)))
                    HM['lrv', e, li] = step("vector", lambda v, off=off, half=half:
                         v.tensor_copy(s_lr[0:R_LR, off + half * 512:off + (half + 1) * 512],
                                       PS[7][0:R_LR, :]),
                        deps=[HM['lrt', e, li]])

            for fc in range(FC_MOE):
                fb = fc % 2
                wbg = s_wbuf0 if fb == 0 else big4b[:, 0:2048]
                wbu = s_wbuf1 if fb == 0 else big4b[:, 2048:4096]
                sb0 = s_big0[:, fb * 1024:(fb + 1) * 1024]
                sb1 = s_big1[:, fb * 1024:(fb + 1) * 1024]

                def gw(E, dma, e=e, fc=fc, wbg=wbg, wbu=wbu):
                    dma(wbg.rearrange("p (dc j) -> p dc j", j=P),
                        moe_wg3[e][:, fc * P:(fc + 1) * P].rearrange("(dc p) j -> p dc j", p=P))
                    dma(wbu.rearrange("p (dc j) -> p dc j", j=P),
                        moe_wu3[e][:, fc * P:(fc + 1) * P].rearrange("(dc p) j -> p dc j", p=P))
                HM['gw', e, fc] = step("gpsimd_dma", gw, ndma=2,
                    deps=(bd0 if (e == 0 and fc < 2) else
                          [HM['gu', e, fc - 2]] if fc >= 2 else [HM['dm', e - 1, 15]]))

                pb = 0 if fb == 0 else 4

                def gumm(t, e=e, fc=fc, wbg=wbg, wbu=wbu, pb=pb):
                    bg = wbg.rearrange("p (dc j) -> p dc j", j=P)
                    bu = wbu.rearrange("p (dc j) -> p dc j", j=P)
                    last = None
                    for half in range(2):
                        for dc in range(DMC):
                            last = t.matmul(PS[pb + half][:, :], bg[:, dc, :],
                                            xn3[:, dc, half * 512:(half + 1) * 512],
                                            start=(dc == 0),
                                            stop=(dc == DMC - 1 and e >= NL))
                        if e < NL:
                            last = t.matmul(PS[pb + half][:, :],
                                            s_wbig[0:R_LR, fc * P:(fc + 1) * P],
                                            s_lr[0:R_LR, half * 512:(half + 1) * 512],
                                            start=False, stop=True)
                        for dc in range(DMC):
                            last = t.matmul(PS[pb + 2 + half][:, :], bu[:, dc, :],
                                            xn3[:, dc, half * 512:(half + 1) * 512],
                                            start=(dc == 0),
                                            stop=(dc == DMC - 1 and e >= NL))
                        if e < NL:
                            last = t.matmul(PS[pb + 2 + half][:, :],
                                            s_wbig[0:R_LR, F_MOE + fc * P:F_MOE + (fc + 1) * P],
                                            s_lr[0:R_LR, 1024 + half * 512:1024 + (half + 1) * 512],
                                            start=False, stop=True)
                    return last
                gu_deps = [HM['gw', e, fc]]
                if fc >= 2:
                    gu_deps += [HM['sl', e, fc - 2], HM['mv', e, fc - 2]]
                elif e == 0:
                    gu_deps += bd0 + ([HM['lrv', e, 3]] if e < NL else [])
                elif fc == 0:
                    gu_deps += [HM['da', e - 1, 15]] + \
                        ([HM['lrv', e, 3]] if e < NL else [])
                else:
                    gu_deps += [HM['wrv', e], HM['mv', e - 1, 15], HM['sl', e - 1, 15]]
                    if e < NL:
                        gu_deps += [HM['lrv', e, 3]]
                    if e - 1 < NL:
                        gu_deps += [HM['dav', e - 1, 1]]
                HM['gu', e, fc] = step("tensor", gumm, deps=gu_deps)

                def silu_g(s, sb0=sb0, pb=pb):
                    for half in range(2):
                        last = s.activation(sb0[:, half * 512:(half + 1) * 512],
                                            PS[pb + half][:, :], AF.Silu)
                    return last
                HM['sl', e, fc] = step("scalar", silu_g,
                    deps=[HM['gu', e, fc]] +
                         ([HM['mv', e, fc - 2]] if fc >= 2 else []))

                def midv(v, fc=fc, eb=eb, sb0=sb0, sb1=sb1, pb=pb):
                    for half in range(2):
                        v.tensor_mul(sb1[:, half * 512:(half + 1) * 512],
                                     sb0[:, half * 512:(half + 1) * 512],
                                     PS[pb + 2 + half][:, :])
                        last = v.tensor_mul(mid3[:, fc, half * 512:(half + 1) * 512],
                                            sb1[:, half * 512:(half + 1) * 512],
                                            s_big2[:, eb * 1024 + half * 512:
                                                   eb * 1024 + (half + 1) * 512])
                    return last
                HM['mv', e, fc] = step("vector", midv,
                    deps=[HM['sl', e, fc]] + ([HM['wrv', e]] if fc < 2 else []))

            if e < NL:
                def da_load(E, dma, e=e):
                    dma(s_wbig[:, 4096:5120].rearrange("p (q r) -> p q r", r=R_LR),
                        da3[e].rearrange("(q p) r -> p q r", p=P))
                    dma(s_wbig[0:R_LR, 0:DM], db3[e])
                HM['dal', e] = step("gpsimd_dma", da_load, ndma=2,
                    deps=[HM['gu', e, 15]])
                da_s = s_wbig[:, 4096:5120].rearrange("p (q r) -> p q r", r=R_LR)

                for half in range(2):
                    def damm(t, half=half):
                        for q in range(FC_MOE):
                            last = t.matmul(PS[7][0:R_LR, :], da_s[:, q, :],
                                            mid3[:, q, half * 512:(half + 1) * 512],
                                            start=(q == 0), stop=(q == FC_MOE - 1))
                        return last
                    HM['dat', e, half] = step("tensor", damm,
                        deps=[HM['dal', e], HM['mv', e, 15]] +
                             ([HM['dav', e, 0]] if half else []))
                    HM['dav', e, half] = step("vector", lambda v, half=half:
                         v.tensor_copy(s_lr[0:R_LR, half * 512:(half + 1) * 512],
                                       PS[7][0:R_LR, :]),
                        deps=[HM['dat', e, half]])

            for dm in range(DMC):
                db_ = dm % 2
                wbd = s_wbuf0 if db_ == 0 else s_wbuf1

                def dw(E, dma, e=e, dm=dm, wbd=wbd):
                    dma(wbd.rearrange("p (q j) -> p q j", j=P),
                        moe_wd3[e][:, dm * P:(dm + 1) * P].rearrange("(q p) j -> p q j", p=P))
                HM['dw', e, dm] = step("gpsimd_dma", dw, ndma=1,
                    deps=([HM['dm', e, dm - 2]] if dm >= 2 else
                          [HM['gu', e, 14 + dm]]))

                def dmm(t, e=e, dm=dm, wbd=wbd, db_=db_):
                    b3 = wbd.rearrange("p (q j) -> p q j", j=P)
                    last = None
                    for half in range(2):
                        for q in range(FC_MOE):
                            last = t.matmul(PS[2 * db_ + half][:, :], b3[:, q, :],
                                            mid3[:, q, half * 512:(half + 1) * 512],
                                            start=(q == 0),
                                            stop=(q == FC_MOE - 1 and e >= NL))
                        if e < NL:
                            last = t.matmul(PS[2 * db_ + half][:, :],
                                            s_wbig[0:R_LR, dm * P:(dm + 1) * P],
                                            s_lr[0:R_LR, half * 512:(half + 1) * 512],
                                            start=False, stop=True)
                    return last
                dm_deps = [HM['dw', e, dm], HM['mv', e, 15]]
                if dm >= 2:
                    dm_deps += [HM['da', e, dm - 2]]
                else:
                    dm_deps += [HM['sl', e, 15]]
                    if e < NL:
                        dm_deps += [HM['dav', e, 1]]
                HM['dm', e, dm] = step("tensor", dmm, deps=dm_deps)

                def dadd(v, dm=dm, db_=db_):
                    for half in range(2):
                        last = v.scalar_tensor_tensor(
                            s_xacc[:, dm * T + half * 512: dm * T + (half + 1) * 512],
                            PS[2 * db_ + half][:, :], 1.0,
                            s_xacc[:, dm * T + half * 512: dm * T + (half + 1) * 512],
                            ALU.mult, ALU.add)
                    return last
                HM['da', e, dm] = step("vector", dadd, deps=[HM['dm', e, dm]])

        phase("ph6_ln1")
        for dc in range(DMC):
            step("scalar", lambda s, dc=dc:
                 s.activation(s_big0[:, 0:1024], s_xacc[:, dc * T:(dc + 1) * T], AF.Square))

            def st6(t, dc=dc):
                t.matmul(PS[6][0:1, :], s_ones_c_f[:, :], s_xacc[:, dc * T: dc * T + 512],
                         start=(dc == 0), stop=(dc == DMC - 1))
                t.matmul(PS[5][0:1, :], s_ones_c_f[:, :], s_xacc[:, dc * T + 512: dc * T + 1024],
                         start=(dc == 0), stop=(dc == DMC - 1))
                t.matmul(PS[7][0:1, :], s_ones_c_f[:, :], s_big0[:, 0:512],
                         start=(dc == 0), stop=(dc == DMC - 1))
                return t.matmul(PS[4][0:1, :], s_ones_c_f[:, :], s_big0[:, 512:1024],
                                start=(dc == 0), stop=(dc == DMC - 1))
            step("tensor", st6)

        def rows6(v):
            v.tensor_copy(row0[0:1, 0:512], PS[6][0:1, :])
            v.tensor_copy(row0[0:1, 512:1024], PS[5][0:1, :])
            v.tensor_copy(row0[0:1, 1024:1536], PS[7][0:1, :])
            v.tensor_copy(row0[0:1, 1536:2048], PS[4][0:1, :])
            v.tensor_scalar_mul(row0[0:1, 0:1024], row0[0:1, 0:1024], -1.0 / DM)
            v.tensor_scalar_mul(row0[0:1, 1024:2048], row0[0:1, 1024:2048], 1.0 / DM)
            v.tensor_mul(s_big1[0:1, 1024:2048], row0[0:1, 0:1024], row0[0:1, 0:1024])
            return v.tensor_sub(row0[0:1, 1024:2048], row0[0:1, 1024:2048],
                                s_big1[0:1, 1024:2048])
        step("vector", rows6)

        def rstd6(s):
            s.activation(row0[0:1, 1024:2048], row0[0:1, 1024:2048], AF.Ln,
                         bias=s_rtemp[0:1, 1:2])
            return s.activation(row0[0:1, 1024:2048], row0[0:1, 1024:2048],
                                AF.Exp, scale=-0.5)
        step("scalar", rstd6)

        def rep6(t):
            t.matmul(PS[6][:, :], s_ones_r_f[:, :], row0[0:1, 0:512], start=True, stop=True)
            t.matmul(PS[5][:, :], s_ones_r_f[:, :], row0[0:1, 512:1024], start=True, stop=True)
            t.matmul(PS[7][:, :], s_ones_r_f[:, :], row0[0:1, 1024:1536], start=True, stop=True)
            return t.matmul(PS[4][:, :], s_ones_r_f[:, :], row0[0:1, 1536:2048],
                            start=True, stop=True)
        step("tensor", rep6)

        def xn1f(v):
            v.tensor_copy(s_big1[:, 0:512], PS[6][:, :])
            v.tensor_copy(s_big1[:, 512:1024], PS[5][:, :])
            v.tensor_copy(s_big2[:, 0:512], PS[7][:, :])
            v.tensor_copy(s_big2[:, 512:1024], PS[4][:, :])
            for dc in range(DMC):
                v.tensor_add(s_big0[:, 0:1024], s_xacc[:, dc * T:(dc + 1) * T], s_big1[:, 0:1024])
                last = v.tensor_mul(xn3[:, dc, :], s_big0[:, 0:1024], s_big2[:, 0:1024])
            return last
        step("vector", xn1f)

        phase("ph7a_proj")
        l1w = [("wq1", 0, 8), ("wk1", 1024, 8), ("ws1", 2048, 8),
               ("wv1", 3072, 16), ("wg1", 5120, 16)]
        K7 = {}
        bd7a = barrier_deps()
        gidx = 0
        for wname, rbase, nfc in l1w:
            for fc in range(nfc):
                g = gidx
                gidx += 1
                gp = g % 2
                wb7a = s_wbuf0 if gp == 0 else s_wbuf1
                eb7a = s_big0.bitcast(BF16)[:, gp * 2048:gp * 2048 + 2048]

                K7['w', g] = step("gpsimd_dma", lambda E, dma, wname=wname, fc=fc, wb7a=wb7a:
                     dma(wb7a.rearrange("p (dc j) -> p dc j", j=P),
                         dram[wname][:, fc * P:(fc + 1) * P].rearrange("(dc p) j -> p dc j", p=P)),
                     ndma=1, deps=(bd7a if g < 2 else [K7['t', g - 2]]))

                def pmm(t, wb7a=wb7a, gp=gp):
                    b3 = wb7a.rearrange("p (dc j) -> p dc j", j=P)
                    for half in range(2):
                        for dc in range(DMC):
                            last = t.matmul(PS[2 * gp + half][:, :], b3[:, dc, :],
                                            xn3[:, dc, half * 512:(half + 1) * 512],
                                            start=(dc == 0), stop=(dc == DMC - 1))
                    return last
                K7['t', g] = step("tensor", pmm,
                    deps=[K7['w', g]] + ([K7['e', g - 2]] if g >= 2 else bd7a))

                if wname == "ws1":
                    def pev(v, eb7a=eb7a, gp=gp):
                        for half in range(2):
                            last = v.tensor_scalar(
                                eb7a[:, half * 512:(half + 1) * 512],
                                PS[2 * gp + half][:, :], 32.0, -32.0, ALU.min, ALU.max)
                        return last
                else:
                    def pev(v, eb7a=eb7a, gp=gp):
                        for half in range(2):
                            last = v.tensor_copy(
                                eb7a[:, half * 512:(half + 1) * 512],
                                PS[2 * gp + half][:, :])
                        return last
                K7['e', g] = step("vector", pev,
                    deps=[K7['t', g]] + ([K7['o', g - 2]] if g >= 2 else []))
                K7['o', g] = step("sync", lambda E, dma, rbase=rbase, fc=fc, eb7a=eb7a:
                     dma(dram["attn_loc"][rbase + fc * P: rbase + (fc + 1) * P, :],
                         eb7a[:, 0:T]), ndma=1, deps=[K7['e', g]])

        segs = [("attn_q", 0, 512, 0), ("attn_k", 1024, 512, 512),
                ("attn_s", 2048, 512, 1024), ("attn_v", 3072, 1024, 1536),
                ("attn_g", 5120, 1024, 2560)]

        def e4_body(ab):
            me0 = (ab == "a")
            own_lo, own_hi = (0, T) if me0 else (T, TT)
            other = "b" if ab == "a" else "a"
            po_lo, po_hi = (T, TT) if me0 else (0, T)
            pre, post = [], []
            for can, lbase, sz, shb in segs:
                mine_lo = lbase + (0 if me0 else sz)
                send_lo = lbase + (sz if me0 else 0)
                pre.append((dram[can][:, own_lo:own_hi],
                            dram["attn_loc"][mine_lo:mine_lo + sz, :]))
                pre.append((dram[f"sh_e4_{ab}"][shb:shb + sz, :],
                            dram["attn_loc"][send_lo:send_lo + sz, :]))
                post.append((dram[can][:, po_lo:po_hi],
                             dram[f"sh_e4_{other}"][shb:shb + sz, :]))
            return pre, post
        phase("exch_e4")
        exchange("e4", 3, e4_body)

        phase("ph7b_abc")
        b0b = s_big0.bitcast(BF16)
        b1b = s_big1.bitcast(BF16)
        b2b = s_big2.bitcast(BF16)
        b3b = s_big3.bitcast(BF16)
        p2v = s_big4.bitcast(BF16)[:, 2048:4096]
        o3 = smb.rearrange("p (hh t) -> p hh t", t=TT)

        for h in range(HH):
            def hld(E, dma, h=h):
                dma(b0b[0:DK, 0:TT], dram["attn_q"][h * DK:(h + 1) * DK, :])
                dma(b0b[0:DK, 2048:4096], dram["attn_k"][h * DK:(h + 1) * DK, :])
                dma(b1b[0:M_ABC, 2048:4096], dram["attn_s"][h * M_ABC:(h + 1) * M_ABC, :])
                dma(b3b[:, 2048:4096], dram["attn_v"][h * DV:(h + 1) * DV, :])
                dma(s_wbuf0[:, 0:TT], dram["attn_g"][h * DV:(h + 1) * DV, :])
            step("sync", hld, ndma=5)

            step("scalar", lambda s:
                 s.activation(b1b[0:M_ABC, 0:TT], b1b[0:M_ABC, 2048:4096], AF.Exp))
            step("scalar", lambda s:
                 s.activation(s_wbuf1[:, 0:TT], s_wbuf0[:, 0:TT], AF.Silu))

            ktm = b2b[:, 0:1024].rearrange("p (c x) -> p c x", x=DK)
            ektm = b2b[:, 1024:2048].rearrange("p (c x) -> p c x", x=DK)
            vtm = b3b[:, 0:2048].rearrange("p (c x) -> p c x", x=DV)
            sinv = s_big4[:, 0:1024].rearrange("p (c x) -> p c x", x=DK)
            rfm = b2b[:, 2048:4096]
            hk_sb = b1b[0:DK, 4096:4160]
            hv_sb = b1b[0:M_ABC, 4160:4288]
            rtm_sb = b1b[:, 4288:4352]
            s1m = b1b[:, 4352:4480]

            for cc in range(16):
                def trp(t, cc=cc):
                    t.matmul(PSB[0][:, 0:DK], b0b[0:DK, 2048 + cc * P:2048 + (cc + 1) * P],
                             s_idb[0:DK, 0:DK], is_transpose=True, start=True, stop=True)
                    t.matmul(PSB[1][:, 0:DK], b1b[0:M_ABC, cc * P:(cc + 1) * P],
                             s_idb[0:M_ABC, 0:M_ABC], is_transpose=True, start=True, stop=True)
                    return t.matmul(PSB[2][:, 0:DV], b3b[:, 2048 + cc * P:2048 + (cc + 1) * P],
                                    s_idb[:, :], is_transpose=True, start=True, stop=True)
                step("tensor", trp)

                def trev(v, cc=cc):
                    v.tensor_copy(ktm[:, cc, :], PSB[0][:, 0:DK])
                    v.tensor_copy(ektm[:, cc, :], PSB[1][:, 0:DK])
                    return v.tensor_copy(vtm[:, cc, :], PSB[2][:, 0:DV])
                step("vector", trev)

            for cc in range(16):
                def p1a(t, cc=cc):
                    t.matmul(PS[6][:, 0:DK], s_tri[:, :], ektm[:, cc, :],
                             start=True, stop=(cc == 0))
                    if cc > 0:
                        t.matmul(PS[6][:, 0:DK], s_ones_r_bf[:, :], s_crow[0:1, :],
                                 start=False, stop=True)
                    t.matmul(PS[7][0:1, 0:DK], s_ones_c_bf[:, :], ektm[:, cc, :],
                             start=(cc == 0), stop=(cc == 15))
                    return t.matmul(PS[5][:, 0:P], b0b[0:DK, 2048 + cc * P:2048 + (cc + 1) * P],
                                    b0b[0:DK, cc * P:(cc + 1) * P], start=True, stop=True)
                step("tensor", p1a)

                def p1b(v, cc=cc):
                    v.reciprocal(sinv[:, cc, :], PS[6][:, 0:DK])
                    v.tensor_copy(s_crow[0:1, :], PS[7][0:1, 0:DK])
                    return v.scalar_tensor_tensor(s1m, PS[5][:, 0:P], 0.125,
                                                  s_tri[:, :], ALU.mult, ALU.mult)
                step("vector", p1b)

                def p1c(t, cc=cc):
                    t.matmul(PS[4][:, 0:DK], s1m, ektm[:, cc, :],
                             start=True, stop=(cc == 0))
                    if cc > 0:
                        t.matmul(PS[4][:, 0:DK], b0b[0:DK, cc * P:(cc + 1) * P],
                                 hk_sb, start=False, stop=True)
                    return t.matmul(PS[3][0:DK, 0:DK], ktm[:, cc, :], ektm[:, cc, :],
                                    start=(cc == 0), stop=(cc == 15))
                step("tensor", p1c)

                step("scalar", lambda s:
                     s.activation(hk_sb, PS[3][0:DK, 0:DK], AF.Identity, scale=0.125))

                step("vector", lambda v, cc=cc:
                     v.tensor_mul(s_big4[:, 1024:1088], PS[4][:, 0:DK], sinv[:, cc, :]))
                step("scalar", lambda s:
                     s.activation(s_big4[:, 1088:1152], s_big4[:, 1024:1088], AF.Exp))

                def smax2(v, cc=cc):
                    v.reduce_sum(s_big4[:, 1152:1153], s_big4[:, 1088:1152], AX.X)
                    v.reciprocal(s_big4[:, 1153:1154], s_big4[:, 1152:1153])
                    v.tensor_mul(s_big4[:, 1216:1280], s_big4[:, 1088:1152], sinv[:, cc, :])
                    return v.tensor_scalar_mul(rtm_sb, s_big4[:, 1216:1280],
                                               s_big4[:, 1153:1154])
                step("vector", smax2)

                step("tensor", lambda t:
                     t.matmul(PSB[1][0:DK, 0:P], rtm_sb[:, 0:DK], s_idb[:, :],
                              is_transpose=True, start=True, stop=True))
                step("vector", lambda v, cc=cc:
                     v.tensor_copy(rfm[0:DK, cc * P:(cc + 1) * P], PSB[1][0:DK, 0:P]))

            for j in range(16):
                ncols = TT - j * P
                nmm = (ncols + 511) // 512

                for qbase in range(0, nmm, 3):
                    qgrp = list(range(qbase, min(qbase + 3, nmm)))

                    def p2a(t, j=j, qgrp=qgrp):
                        for idx, q in enumerate(qgrp):
                            cl = j * P + q * 512
                            cw = min(512, TT - cl)
                            last = t.matmul(PS[4 + idx][:, 0:cw],
                                            b1b[0:M_ABC, j * P:(j + 1) * P],
                                            rfm[0:M_ABC, cl:cl + cw],
                                            start=True, stop=True)
                        return last
                    step("tensor", p2a)

                    def p2b(v, j=j, qgrp=qgrp):
                        for idx, q in enumerate(qgrp):
                            cl = j * P + q * 512
                            cw = min(512, TT - cl)
                            if q == 0:
                                last = v.tensor_mul(p2v[:, 0:P],
                                                    PS[4 + idx][:, 0:P], s_tri[:, :])
                                if cw > P:
                                    last = v.tensor_copy(p2v[:, P:cw],
                                                         PS[4 + idx][:, P:cw])
                            else:
                                last = v.tensor_copy(p2v[:, cl - j * P:cl - j * P + cw],
                                                     PS[4 + idx][:, 0:cw])
                        return last
                    step("vector", p2b)

                def p2c(t, j=j):
                    last = None
                    if j > 0:
                        bank, off = divmod(j * P, 512)
                        last = t.matmul(PS[bank][:, off:off + P], hv_sb,
                                        rfm[0:M_ABC, j * P:(j + 1) * P], start=False, stop=False)
                    for q in range(4):
                        blo, bhi = q * 512, (q + 1) * 512
                        lo = max(j * P, blo)
                        if lo >= bhi:
                            continue
                        last = t.matmul(PS[q][:, lo - blo:bhi - blo], vtm[:, j, :],
                                        p2v[:, lo - j * P:bhi - j * P],
                                        start=(j == 0), stop=(j == min(4 * q + 3, 15)))
                    last = t.matmul(PS[7][0:M_ABC, 0:DV], ektm[:, j, :], vtm[:, j, :],
                                    start=(j == 0), stop=(j == 15))
                    return last
                step("tensor", p2c)

                step("scalar", lambda s:
                     s.activation(hv_sb, PS[7][0:M_ABC, 0:DV], AF.Identity))

            def rms_s(s):
                for q in range(4):
                    last = s.activation(b0b[:, q * 512:(q + 1) * 512], PS[q][:, :], AF.Square)
                return last
            step("scalar", rms_s)

            for q in range(4):
                step("tensor", lambda t, q=q:
                     t.matmul(PS[6][0:1, 0:512], s_ones_c_bf[:, :],
                              b0b[:, q * 512:(q + 1) * 512], start=True, stop=True))
                step("vector", lambda v, q=q:
                     v.tensor_copy(row0[0:1, q * 512:(q + 1) * 512], PS[6][0:1, 0:512]))

            step("vector", lambda v:
                 v.tensor_scalar_mul(row0, row0, 1.0 / DV))

            def rmsact(s):
                s.activation(row0, row0, AF.Ln, bias=s_rtemp[0:1, 1:2])
                return s.activation(row0, row0, AF.Exp, scale=-0.5)
            step("scalar", rmsact)

            for q in range(4):
                step("tensor", lambda t, q=q:
                     t.matmul(PS[5][:, :], s_ones_r_f[:, :], row0[0:1, q * 512:(q + 1) * 512],
                              start=True, stop=True))

                def gate(v, h=h, q=q):
                    v.tensor_copy(s_big4[:, 1024:1536], PS[5][:, :])
                    v.tensor_mul(s_big4[:, 1536:2048], PS[q][:, :], s_big4[:, 1024:1536])
                    v.tensor_scalar_mul(s_big4[:, 1536:2048], s_big4[:, 1536:2048], s_gnw[:, 0:1])
                    return v.tensor_mul(o3[:, h, q * 512:(q + 1) * 512], s_big4[:, 1536:2048],
                                        s_wbuf1[:, q * 512:(q + 1) * 512])
                step("vector", gate)

        def e5_body(ab):
            me0 = (ab == "a")
            own_lo = 0 if me0 else T
            myrows = 0 if me0 else 1024
            other = "b" if ab == "a" else "a"
            prr = 1024 if me0 else 0
            pre = []
            for hh_ in range(HH):
                pre.append((dram["o_own"][myrows + hh_ * DV: myrows + (hh_ + 1) * DV, :],
                            o3[:, hh_, own_lo:own_lo + T]))
                pre.append((dram[f"sh_e5_{ab}"][hh_ * DV:(hh_ + 1) * DV, :],
                            o3[:, hh_, (T if me0 else 0):(TT if me0 else T)]))
            post = [(dram["o_own"][prr:prr + 1024, :], dram[f"sh_e5_{other}"])]
            return pre, post
        phase("exch_e5")
        exchange("e5", 4, e5_body)

        phase("ph7c_wo")
        def oo_ld(E, dma):
            dma(smb.rearrange("p (j t) -> p j t", t=T)[:, 0:16, :],
                dram["o_own"].rearrange("(j p) t -> p j t", p=P))
        step("sync", oo_ld, ndma=1)

        W7 = {}
        bd7 = barrier_deps()
        wo4b = s_big4.bitcast(BF16)
        for grp in range(8):
            gb7 = grp % 2
            wb7 = s_wbig[:, 0:4096] if gb7 == 0 else wo4b[:, 0:4096]

            def wld7(E, dma, grp=grp, wb7=wb7):
                dma(wb7.rearrange("p (j d) -> p j d", d=DM),
                    dram["wo1"][grp * 256:(grp + 1) * 256, :].rearrange("(j p) d -> p j d", p=P))
            W7['w', grp] = step("gpsimd_dma", wld7, ndma=1,
                deps=(bd7 if grp < 2 else [W7['t', grp - 2, DMC - 1]]))

            for dm in range(DMC):
                pb7 = 2 * (dm % 2)

                def womm(t, grp=grp, dm=dm, wb7=wb7, pb7=pb7):
                    w3 = wb7.rearrange("p (j d) -> p j d", d=DM)
                    om = smb.rearrange("p (j t) -> p j t", t=T)
                    for half in range(2):
                        for j in range(2):
                            last = t.matmul(PS[pb7 + half][:, :], w3[:, j, dm * P:(dm + 1) * P],
                                            om[:, grp * 2 + j, half * 512:(half + 1) * 512],
                                            start=(j == 0), stop=(j == 1))
                    return last
                wo_deps = [W7['w', grp]]
                if dm >= 2:
                    wo_deps += [W7['v', grp, dm - 2]]
                elif grp >= 1:
                    wo_deps += [W7['v', grp - 1, DMC - 2 + dm]]
                else:
                    wo_deps += bd7
                W7['t', grp, dm] = step("tensor", womm, deps=wo_deps)

                def woadd(v, dm=dm, pb7=pb7):
                    for half in range(2):
                        last = v.scalar_tensor_tensor(
                            s_xacc[:, dm * T + half * 512: dm * T + (half + 1) * 512],
                            PS[pb7 + half][:, :], 1.0,
                            s_xacc[:, dm * T + half * 512: dm * T + (half + 1) * 512],
                            ALU.mult, ALU.add)
                    return last
                W7['v', grp, dm] = step("vector", woadd, deps=[W7['t', grp, dm]])

        phase("ph8_qffn")
        def rs_t(t):
            for dc in range(DMC):
                t.matmul(PS[6][0:1, :], s_ones_c_bf[:, :], xn3[:, dc, 0:512],
                         start=(dc == 0), stop=(dc == DMC - 1))
                last = t.matmul(PS[5][0:1, :], s_ones_c_bf[:, :], xn3[:, dc, 512:1024],
                                start=(dc == 0), stop=(dc == DMC - 1))
            return last
        step("tensor", rs_t, deps=barrier_deps())

        def rs_v(v):
            v.tensor_copy(row0[0:1, 0:512], PS[6][0:1, :])
            v.tensor_copy(row0[0:1, 512:1024], PS[5][0:1, :])
            v.tensor_copy(rowb0, row0[0:1, 0:1024])
            v.tensor_sub(s_big1[0:1, 0:1024], row0[0:1, 0:1024], rowb0)
            return v.tensor_copy(rowb1, s_big1[0:1, 0:1024])
        step("vector", rs_v)
        bd8 = barrier_deps()

        mid11 = smb.rearrange("p (q t) -> p q t", t=T)
        K8 = {}
        q4b = s_big4.bitcast(BF16)
        for sup in range(4):
            for hc_i in range(11):
                hc = sup * 11 + hc_i
                par8 = hc % 2
                pb8 = 0 if par8 == 0 else 4
                wg8 = s_wbuf0 if par8 == 0 else q4b[:, 0:2048]
                wu8 = s_wbuf1 if par8 == 0 else q4b[:, 2048:4096]
                sb8 = s_big0[:, par8 * 1024:(par8 + 1) * 1024]

                def qwl(E, dma, hc=hc, wg8=wg8, wu8=wu8):
                    dma(wg8.rearrange("p (dc j) -> p dc j", j=P),
                        dram["qg_T"][:, hc * P:(hc + 1) * P].rearrange("(dc p) j -> p dc j", p=P))
                    dma(wu8.rearrange("p (dc j) -> p dc j", j=P),
                        dram["qu_T"][:, hc * P:(hc + 1) * P].rearrange("(dc p) j -> p dc j", p=P))
                K8['w', sup, hc_i] = step("gpsimd_dma", qwl, ndma=2,
                    deps=(bd8 if (sup == 0 and hc_i < 2) else
                          [K8['t', sup, hc_i - 2]] if hc_i >= 2 else
                          [K8['dt', sup - 1, DMC - 1]]))

                def qmm(t, hc=hc, wg8=wg8, wu8=wu8, pb8=pb8):
                    bg = wg8.rearrange("p (dc j) -> p dc j", j=P)
                    bu = wu8.rearrange("p (dc j) -> p dc j", j=P)
                    for half in range(2):
                        for dc in range(DMC):
                            t.matmul(PS[pb8 + half][:, :], bg[:, dc, :],
                                     xn3[:, dc, half * 512:(half + 1) * 512],
                                     start=(dc == 0), stop=False)
                        t.matmul(PS[pb8 + half][:, :], s_m128[:, :],
                                 rowb0[0:1, half * 512:(half + 1) * 512], start=False, stop=False)
                        t.matmul(PS[pb8 + half][:, :], s_m128[:, :],
                                 rowb1[0:1, half * 512:(half + 1) * 512], start=False, stop=True)
                        for dc in range(DMC):
                            t.matmul(PS[pb8 + 2 + half][:, :], bu[:, dc, :],
                                     xn3[:, dc, half * 512:(half + 1) * 512],
                                     start=(dc == 0), stop=False)
                        t.matmul(PS[pb8 + 2 + half][:, :], s_m128[:, :],
                                 rowb0[0:1, half * 512:(half + 1) * 512], start=False, stop=False)
                        last = t.matmul(PS[pb8 + 2 + half][:, :], s_m128[:, :],
                                        rowb1[0:1, half * 512:(half + 1) * 512],
                                        start=False, stop=True)
                    return last
                qm_deps = [K8['w', sup, hc_i]]
                if hc_i >= 2:
                    qm_deps += [K8['sl', sup, hc_i - 2], K8['mv', sup, hc_i - 2]]
                elif sup == 0:
                    qm_deps += bd8
                elif par8 == 0:
                    qm_deps += [K8['dv', sup - 1, DMC - 1]]
                else:
                    qm_deps += [K8['r2v', sup - 1], K8['sl', sup - 1, 10],
                                K8['mv', sup - 1, 10]]
                K8['t', sup, hc_i] = step("tensor", qmm, deps=qm_deps)

                def qsil(s, hc=hc, sb8=sb8, pb8=pb8):
                    for half in range(2):
                        last = s.activation(sb8[:, half * 512:(half + 1) * 512],
                                            PS[pb8 + half][:, :], AF.Silu,
                                            scale=s_sgc[:, hc:hc + 1])
                    return last
                K8['sl', sup, hc_i] = step("scalar", qsil,
                    deps=[K8['t', sup, hc_i]] +
                         ([K8['mv', sup, hc_i - 2]] if hc_i >= 2 else []))

                def qmid(v, hc=hc, hc_i=hc_i, sb8=sb8, pb8=pb8):
                    for half in range(2):
                        last = v.scalar_tensor_tensor(
                            mid11[:, hc_i, half * 512:(half + 1) * 512],
                            PS[pb8 + 2 + half][:, :], s_suc[:, hc:hc + 1],
                            sb8[:, half * 512:(half + 1) * 512], ALU.mult, ALU.mult)
                    return last
                K8['mv', sup, hc_i] = step("vector", qmid, deps=[K8['sl', sup, hc_i]])

            def rs2t(t, sup=sup):
                for hc_i in range(11):
                    t.matmul(PS[6][0:1, :], s_ones_c_bf[:, :], mid11[:, hc_i, 0:512],
                             start=(hc_i == 0), stop=(hc_i == 10))
                    last = t.matmul(PS[7][0:1, :], s_ones_c_bf[:, :], mid11[:, hc_i, 512:1024],
                                    start=(hc_i == 0), stop=(hc_i == 10))
                return last
            K8['r2t', sup] = step("tensor", rs2t, deps=[K8['mv', sup, 10]])

            def rs2v(v, sup=sup):
                if sup == 0:
                    v.tensor_copy(row0[0:1, 0:512], PS[6][0:1, :])
                    return v.tensor_copy(row0[0:1, 512:1024], PS[7][0:1, :])
                v.tensor_add(row0[0:1, 0:512], row0[0:1, 0:512], PS[6][0:1, :])
                return v.tensor_add(row0[0:1, 512:1024], row0[0:1, 512:1024],
                                    PS[7][0:1, :])
            K8['r2v', sup] = step("vector", rs2v, deps=[K8['r2t', sup]])

            for dm in range(DMC):
                pbD = 2 * (dm % 2)
                wbD = s_wbuf0 if dm % 2 == 0 else s_wbuf1

                def qdw(E, dma, sup=sup, dm=dm, wbD=wbD):
                    dma(wbD[:, 0:11 * P].rearrange("p (q j) -> p q j", j=P),
                        dram["qd_T"][sup * 11 * P:(sup + 1) * 11 * P, dm * P:(dm + 1) * P]
                        .rearrange("(q p) j -> p q j", p=P))
                K8['dw', sup, dm] = step("gpsimd_dma", qdw, ndma=1,
                    deps=([K8['dt', sup, dm - 2]] if dm >= 2 else [K8['t', sup, 10]]))

                def qdmm(t, dm=dm, wbD=wbD, pbD=pbD):
                    b3 = wbD[:, 0:11 * P].rearrange("p (q j) -> p q j", j=P)
                    for half in range(2):
                        for q in range(11):
                            last = t.matmul(PS[pbD + half][:, :], b3[:, q, :],
                                            mid11[:, q, half * 512:(half + 1) * 512],
                                            start=(q == 0), stop=(q == 10))
                    return last
                K8['dt', sup, dm] = step("tensor", qdmm,
                    deps=[K8['dw', sup, dm], K8['mv', sup, 10]] +
                         ([K8['dv', sup, dm - 2]] if dm >= 2 else
                          [K8['sl', sup, 10]]))

                def qdadd(v, dm=dm, pbD=pbD):
                    for half in range(2):
                        last = v.scalar_tensor_tensor(
                            s_xacc[:, dm * T + half * 512: dm * T + (half + 1) * 512],
                            PS[pbD + half][:, :], s_sdc[:, dm:dm + 1],
                            s_xacc[:, dm * T + half * 512: dm * T + (half + 1) * 512],
                            ALU.mult, ALU.add)
                    return last
                K8['dv', sup, dm] = step("vector", qdadd, deps=[K8['dt', sup, dm]])

        def rs2rep(t):
            t.matmul(PS[6][:, :], s_ones_r_f[:, :], row0[0:1, 0:512], start=True, stop=True)
            return t.matmul(PS[5][:, :], s_ones_r_f[:, :], row0[0:1, 512:1024],
                            start=True, stop=True)
        step("tensor", rs2rep,
             deps=[K8['r2v', 3], K8['sl', 3, 10], K8['dv', 3, DMC - 1]])

        def rs2corr(v):
            for dm in range(DMC):
                for half in range(2):
                    bank = PS[6] if half == 0 else PS[5]
                    last = v.scalar_tensor_tensor(
                        s_xacc[:, dm * T + half * 512: dm * T + (half + 1) * 512],
                        bank[:, :], s_sdm128[:, dm:dm + 1],
                        s_xacc[:, dm * T + half * 512: dm * T + (half + 1) * 512],
                        ALU.mult, ALU.add)
            return last
        step("vector", rs2corr)

        phase("ph9_out")
        K9 = {}
        bd9 = barrier_deps()
        for i in range(NT):
            ob9 = s_big0 if i % 2 == 0 else s_big2
            for dc in range(DMC):
                k9 = i * DMC + dc
                bk9 = k9 % 4
                K9['t', k9] = step("tensor", lambda t, i=i, dc=dc, bk9=bk9:
                     t.matmul(PS[bk9][:, 0:P], s_xacc[:, dc * T + i * P: dc * T + (i + 1) * P],
                              s_idf[:], is_transpose=True, start=True, stop=True),
                     deps=(bd9 if k9 < 4 else [K9['v', k9 - 4]]))
                ev9 = [K9['t', k9]]
                if dc == 0 and i >= 2:
                    ev9 += [K9['o', i - 2]]
                K9['v', k9] = step("vector", lambda v, dc=dc, ob9=ob9, bk9=bk9:
                     v.tensor_copy(ob9[:, dc * P:(dc + 1) * P], PS[bk9][:, 0:P]),
                     deps=ev9)
            K9['o', i] = step("scalar_dma", lambda E, dma, i=i, ob9=ob9:
                 dma(dram["y"][i * P:(i + 1) * P, :], ob9[:, 0:DM]), ndma=1,
                 deps=[K9['v', i * DMC + DMC - 1]])

        for nm, outn in DEBUG_COPY:
            step("sync", lambda E, dma, nm=nm, outn=outn:
                 dma(dram[outn], dram[nm]), ndma=1)

        phase(None)
        cfin = dict(C)

        def fin(sy):
            if chain[0] is not None:
                sy.wait_ge(SEMS[chain[0][0]], chain[0][1])
            for nm in ("qc", "qs", "qg"):
                if cfin[nm] > 0:
                    sy.wait_ge(SEMS[nm], cfin[nm])
        block.sync(fin)

    return nc



def _prep_consts():
    c = {}
    c["id_f32"] = np.eye(P, dtype=np.float32)
    c["id_bf16"] = np.eye(P, dtype=np.float32)
    tri = (np.arange(P)[:, None] <= np.arange(P)[None, :]).astype(np.float32)
    c["tri_bf16"] = tri
    c["tri_scaled"] = tri * 0.125
    selrep = np.zeros((P, N * P), np.float32)
    selred = np.zeros((P, N * P), np.float32)
    for m in range(N):
        for j in range(P):
            selrep[8 * m + (j % 8), m * P + j] = 1.0
        for p_ in range(P):
            selred[p_, m * P + 8 * m + (p_ % 8)] = 1.0
    c["selrep"] = selrep
    c["selred"] = selred
    sB = np.zeros((32, P), np.float32)
    sC = np.zeros((32, P), np.float32)
    for j in range(P):
        sB[j // 8, j] = 1.0
        sC[16 + j // 8, j] = 1.0
    c["sB"] = sB
    c["sC"] = sC
    c["ones_c_bf"] = np.ones((P, 1), np.float32)
    c["ones_c_f32"] = np.ones((P, 1), np.float32)
    c["ones_r_bf"] = np.ones((1, P), np.float32)
    c["ones_r_f32"] = np.ones((1, P), np.float32)
    c["m128_r"] = np.full((1, P), -128.0, np.float32)
    return c


_BF16_NAMES = {"id_bf16", "tri_bf16", "tri_scaled", "selrep", "selred",
               "ones_c_bf", "ones_r_bf", "m128_r"}


def kernel(**inputs):
    import ml_dtypes
    inp = {k: np.asarray(v) for k, v in inputs.items()}

    x = np.asarray(inp["L0_attn_hidden_states"], np.float32)
    w_inT = np.ascontiguousarray(np.asarray(inp["L0_attn_in_proj_w"], np.float32).T)
    conv_w = np.asarray(inp["L0_attn_conv_w"], np.float32)
    conv_b = np.asarray(inp["L0_attn_conv_b"], np.float32)
    w_xpT = np.ascontiguousarray(np.asarray(inp["L0_attn_x_proj_w"], np.float32).T)
    w_dtT = np.ascontiguousarray(np.asarray(inp["L0_attn_dt_proj_w"], np.float32).T)
    dt_b = np.asarray(inp["L0_attn_dt_proj_b"], np.float32)
    A_log = np.asarray(inp["L0_attn_A_log"], np.float32)
    D_ = np.asarray(inp["L0_attn_D"], np.float32)
    w_outT = np.ascontiguousarray(np.asarray(inp["L0_attn_out_proj_w"], np.float32).T)

    consts = _prep_consts()
    rng = np.random.default_rng()
    magic_base = int(rng.integers(1 << 20, 1 << 30))
    magic = np.zeros((1, 40), np.int32)
    for k in range(5):
        magic[0, 8 * k:8 * k + 8] = magic_base + k
    magic_cmp = np.concatenate([np.arange(5, dtype=np.int32) + magic_base,
                                np.zeros(3, np.int32)])[None, :]

    def colsify(vec, nch):
        return np.ascontiguousarray(np.asarray(vec, np.float32).reshape(nch, P).T)

    nc = build_program()

    shared_f32 = {
        "w_inT": w_inT, "w_outT": w_outT,
        "moe_wg": np.asarray(inp["L0_ffn_Wg"], np.float32).reshape(E_EXP * DM, F_MOE),
        "moe_wu": np.asarray(inp["L0_ffn_Wu"], np.float32).reshape(E_EXP * DM, F_MOE),
        "moe_wd": np.asarray(inp["L0_ffn_Wd"], np.float32).reshape(E_EXP * F_MOE, DM),
        "moe_ga": np.asarray(inp["L0_ffn_Ga"], np.float32).reshape(NL * DM, R_LR),
        "moe_gb": np.asarray(inp["L0_ffn_Gb"], np.float32).reshape(NL * R_LR, F_MOE),
        "moe_ua": np.asarray(inp["L0_ffn_Ua"], np.float32).reshape(NL * DM, R_LR),
        "moe_ub": np.asarray(inp["L0_ffn_Ub"], np.float32).reshape(NL * R_LR, F_MOE),
        "moe_da": np.asarray(inp["L0_ffn_Da"], np.float32).reshape(NL * F_MOE, R_LR),
        "moe_db": np.asarray(inp["L0_ffn_Db"], np.float32).reshape(NL * R_LR, DM),
        "wr": np.asarray(inp["L0_ffn_Wr"], np.float32),
        "wq1": np.asarray(inp["L1_attn_Wq"], np.float32),
        "wk1": np.asarray(inp["L1_attn_Wk"], np.float32),
        "wv1": np.asarray(inp["L1_attn_Wv"], np.float32),
        "wg1": np.asarray(inp["L1_attn_Wg"], np.float32),
        "ws1": np.asarray(inp["L1_attn_Ws"], np.float32),
        "wo1": np.asarray(inp["L1_attn_Wo"], np.float32),
    }
    qg_T = np.ascontiguousarray(np.asarray(inp["L1_ffn_wq_gate"], np.int32).T)
    qu_T = np.ascontiguousarray(np.asarray(inp["L1_ffn_wq_up"], np.int32).T)
    qd_T = np.ascontiguousarray(np.asarray(inp["L1_ffn_wq_down"], np.int32).T)

    in_maps = []
    for c in range(8):
        b, role = c // 2, c % 2
        dlo = role * DIH
        m = dict(shared_f32)
        m["qg_T"] = qg_T
        m["qu_T"] = qu_T
        m["qd_T"] = qd_T
        m["x_own"] = np.ascontiguousarray(x[b, role * T:(role + 1) * T, :])
        cw = conv_w[dlo:dlo + DIH]
        m["conv_w_loc"] = np.ascontiguousarray(
            cw.reshape(DIHC, P, 4).transpose(1, 0, 2).reshape(P, DIHC * 4))
        m["conv_b_loc"] = colsify(conv_b[dlo:dlo + DIH], DIHC)
        m["w_xpT_loc"] = np.ascontiguousarray(w_xpT[dlo:dlo + DIH])
        m["w_dtT_loc"] = np.ascontiguousarray(w_dtT[:, dlo:dlo + DIH])
        m["dtb_loc"] = colsify(dt_b[dlo:dlo + DIH], DIHC)
        kk = np.arange(DIHC * N)
        pp = np.arange(P)
        m["alog_lanes"] = np.ascontiguousarray(
            A_log[dlo + 8 * kk[None, :] + (pp[:, None] % 8), pp[:, None] // 8]).astype(np.float32)
        m["d_loc"] = colsify(D_[dlo:dlo + DIH], DIHC)
        m["temp"] = np.asarray(inp["L0_ffn_temperature"], np.float32).reshape(1, 1)
        m["gnw"] = np.asarray(inp["L1_attn_g_norm_weight"], np.float32).reshape(P, 1)
        m["sg_cols"] = colsify(inp["L1_ffn_s_gate"], HIDC)
        m["su_cols"] = colsify(inp["L1_ffn_s_up"], HIDC)
        m["sd_cols"] = colsify(inp["L1_ffn_s_down"], DMC)
        for k_, v_ in consts.items():
            m[k_] = v_.astype(ml_dtypes.bfloat16) if k_ in _BF16_NAMES else v_
        m["role_in"] = np.array([[role]], np.int32)
        m["magic"] = magic
        m["magic_cmp"] = magic_cmp
        in_maps.append(m)

    res = run_bass_kernel_spmd(nc, in_maps, core_ids=list(range(8)))
    global LAST_RESULTS
    LAST_RESULTS = res
    out = np.zeros((4, TT, DM), np.float32)
    for c in range(8):
        b, role = c // 2, c % 2
        out[b, role * T:(role + 1) * T, :] = res.results[c]["y"]
    return out


# version 05 (final)
# speedup vs baseline: 1.9709x; 1.1487x over previous
import numpy as np
import sys
sys.path.insert(0, "/opt/trn_rl_repo")
import concourse.bass as bass
import concourse.mybir as mybir
from concourse.bass_utils import run_bass_kernel_spmd
from contextlib import ExitStack

F32 = mybir.dt.float32
BF16 = mybir.dt.bfloat16
I32 = mybir.dt.int32
AF = mybir.ActivationFunctionType
ALU = mybir.AluOpType
AX = mybir.AxisListType

P = 128
DM = 2048
DMC = DM // P
DI = 4096
DIH = DI // 2
DIHC = DIH // P
T = 1024
TT = 2048
N = 16
E_EXP = 8
F_MOE = 2048
FC_MOE = F_MOE // P
R_LR = 64
NL = 2
H = 16
HH = 8
DK = 64
DV = 128
M_ABC = 64
HID = 5632
HIDC = HID // P
GOLDEN_CENTER = 1.0 / np.e
GOLDEN_LOWER = 0.5 - np.log(4.0 / 3.0)
GOLDEN_UPPER = 0.5
EPS_LN = 1e-5
EPS_RMS = 1e-5

DEBUG_COPY = []
SIM_RACECHECK = False
LAST_RESULTS = None



def build_program():
    nc = bass.Bass("TRN2", target_bir_lowering=False)
    dram = {}

    def din(name, shape, dt=F32):
        dram[name] = nc.dram_tensor(name, shape, dt, kind="ExternalInput").ap()

    def dout(name, shape, dt=F32):
        dram[name] = nc.dram_tensor(name, shape, dt, kind="ExternalOutput").ap()

    def dint(name, shape, dt=F32, shared=False):
        if SIM_RACECHECK:
            shared = False
        if shared:
            dram[name] = nc.dram_tensor(name, shape, dt, addr_space="Shared").ap()
        else:
            dram[name] = nc.dram_tensor(name, shape, dt).ap()

    din("x_own", [T, DM])
    din("w_inT", [DM, 2 * DI])
    din("conv_w_loc", [P, DIHC * 4])
    din("conv_b_loc", [P, DIHC])
    din("w_xpT_loc", [DIH, 160])
    din("w_dtT_loc", [128, DIH])
    din("dtb_loc", [P, DIHC])
    din("alog_lanes", [P, DIHC * N])
    din("d_loc", [P, DIHC])
    din("w_outT", [DI, DM])
    din("moe_wg", [E_EXP * DM, F_MOE])
    din("moe_wu", [E_EXP * DM, F_MOE])
    din("moe_wd", [E_EXP * F_MOE, DM])
    din("moe_ga", [NL * DM, R_LR])
    din("moe_gb", [NL * R_LR, F_MOE])
    din("moe_ua", [NL * DM, R_LR])
    din("moe_ub", [NL * R_LR, F_MOE])
    din("moe_da", [NL * F_MOE, R_LR])
    din("moe_db", [NL * R_LR, DM])
    din("wr", [DM, E_EXP])
    din("temp", [1, 1])
    din("wq1", [DM, H * DK])
    din("wk1", [DM, H * DK])
    din("wv1", [DM, H * DV])
    din("wg1", [DM, H * DV])
    din("ws1", [DM, H * M_ABC])
    din("wo1", [H * DV, DM])
    din("gnw", [P, 1])
    din("qg_T", [DM, HID], I32)
    din("qu_T", [DM, HID], I32)
    din("qd_T", [HID, DM], I32)
    din("sg_cols", [P, HIDC])
    din("su_cols", [P, HIDC])
    din("sd_cols", [P, DMC])
    din("id_f32", [P, P])
    din("id_bf16", [P, P], BF16)
    din("tri_bf16", [P, P], BF16)
    din("tri_scaled", [P, P], BF16)
    din("selrep", [P, N * P], BF16)
    din("selred", [P, N * P], BF16)
    din("sB", [32, P])
    din("sC", [32, P])
    din("ones_c_bf", [P, 1], BF16)
    din("ones_c_f32", [P, 1])
    din("ones_r_bf", [1, P], BF16)
    din("ones_r_f32", [1, P])
    din("m128_r", [1, P], BF16)
    din("role_in", [1, 1], I32)
    din("magic", [1, 40], I32)
    din("magic_cmp", [1, 8], I32)

    dout("y", [T, DM])

    dint("xc_dram", [DI, T], BF16)
    dint("sz_dram", [DI, T], BF16)
    dint("xc_scan", [DIH, TT], BF16)
    dint("sz_scan", [DIH, TT], BF16)
    dint("xt_scan", [DIH, TT], BF16)
    dint("ya_loc", [DIH, TT], BF16)
    dint("dt_dbg", [DIH, TT], BF16)
    dint("scan_dbg", [P, 3072], F32)
    dint("ya_own", [DI, T], BF16)
    dint("bc_dram", [32, TT], F32)
    dint("attn_loc", [7168, T], BF16)
    dint("attn_q", [HH * DK, TT], BF16)
    dint("attn_k", [HH * DK, TT], BF16)
    dint("attn_s", [HH * M_ABC, TT], BF16)
    dint("attn_v", [HH * DV, TT], BF16)
    dint("attn_g", [HH * DV, TT], BF16)
    dint("o_own", [H * DV, T], BF16)
    for ab in "ab":
        dint(f"sh_e1_{ab}", [DI, T], BF16, shared=True)
        dint(f"sh_e2_{ab}", [160, TT], F32, shared=True)
        dint(f"sh_e3_{ab}", [DIH, T], BF16, shared=True)
        dint(f"sh_e4_{ab}", [3584, T], BF16, shared=True)
        dint(f"sh_e5_{ab}", [HH * DV, T], BF16, shared=True)
        for k in range(5):
            dint(f"fl{k}_{ab}", [1, 8], I32, shared=True)
    for nm, outn in DEBUG_COPY:
        shp = {"xc_dram": [DI, T], "sz_dram": [DI, T], "xc_scan": [DIH, TT],
               "sz_scan": [DIH, TT], "xt_scan": [DIH, TT], "ya_loc": [DIH, TT],
               "ya_own": [DI, T], "attn_loc": [7168, T], "o_own": [H * DV, T],
               "dt_dbg": [DIH, TT], "scan_dbg": [P, 3072],
               "attn_q": [HH * DK, TT], "attn_s": [HH * M_ABC, TT]}[nm]
        dout(outn, shp, F32 if nm == "scan_dbg" else BF16)

    with ExitStack() as es:
        def sb(name, shape, dt=F32):
            return es.enter_context(nc.sbuf_tensor(name, shape, dt))

        def psum(name, shape, dt=F32):
            return es.enter_context(nc.psum_tensor(name, shape, dt))

        s_xacc = sb("s_xacc", [P, DMC * T])
        s_xn = sb("s_xn", [P, DMC * T], BF16)
        s_mid = sb("s_mid", [P, FC_MOE * T], BF16)
        s_big0 = sb("s_big0", [P, 2048])
        s_big1 = sb("s_big1", [P, 2304])
        s_big2 = sb("s_big2", [P, 2048])
        s_big3 = sb("s_big3", [P, 2048])
        s_big4 = sb("s_big4", [P, 2048])
        s_wbuf0 = sb("s_wbuf0", [P, DM], BF16)
        s_wbuf1 = sb("s_wbuf1", [P, DM], BF16)
        s_wbig = sb("s_wbig", [P, 6144], BF16)
        s_lr = sb("s_lr", [P, DM], BF16)
        s_rows = sb("s_rows", [1, 2048])
        s_wr = sb("s_wr", [P, P])
        s_hcarry = sb("s_hcarry", [P, DIHC * N])
        s_idf = sb("s_idf", [P, P])
        s_idb = sb("s_idb", [P, P], BF16)
        s_tri = sb("s_tri", [P, P], BF16)
        s_sB = sb("s_sB", [32, P])
        s_sC = sb("s_sC", [32, P])
        s_ones_c_bf = sb("s_ones_c_bf", [P, 1], BF16)
        s_ones_c_f = sb("s_ones_c_f", [P, 1])
        s_ones_r_bf = sb("s_ones_r_bf", [1, P], BF16)
        s_ones_r_f = sb("s_ones_r_f", [1, P])
        s_m128 = sb("s_m128", [1, P], BF16)
        s_role = sb("s_role", [1, 1], I32)
        s_magic = sb("s_magic", [1, 40], I32)
        s_mcmp = sb("s_mcmp", [1, 8], I32)
        s_gnw = sb("s_gnw", [P, 1])
        s_temp = sb("s_temp", [1, 1])
        s_rtemp = sb("s_rtemp", [P, 2])
        s_ceps = s_rtemp[:, 1:2]
        s_convw = sb("s_convw", [P, DIHC * 4])
        s_convb = sb("s_convb", [P, DIHC])
        s_dtb = sb("s_dtb", [P, DIHC])
        s_alanes = sb("s_alanes", [P, DIHC * N])
        s_dloc = sb("s_dloc", [P, DIHC])
        s_sgc = sb("s_sgc", [P, HIDC])
        s_suc = sb("s_suc", [P, HIDC])
        s_sdc = sb("s_sdc", [P, DMC])
        s_sdm128 = sb("s_sdm128", [P, DMC])
        s_flag = sb("s_flag", [1, 8], I32)

        s_logits = s_lr.bitcast(F32)[64:64 + E_EXP, 0:T]

        row0 = s_rows[0:1, :]
        rowb0 = s_rows.bitcast(BF16)[0:1, 2048:3072]
        rowb1 = s_rows.bitcast(BF16)[0:1, 3072:4096]
        s_crow = s_rows.bitcast(BF16)[0:1, 0:64]
        s_wfm = s_big3[0:8, 1024:2048]

        PS = [psum(f"PS{i}", [P, 512]) for i in range(8)]
        PSB = [p.bitcast(BF16) for p in PS]

        qg = es.enter_context(nc.semaphore("qg"))
        qs = es.enter_context(nc.semaphore("qs"))
        qc = es.enter_context(nc.semaphore("qc"))
        tS = es.enter_context(nc.semaphore("tS"))
        vS = es.enter_context(nc.semaphore("vS"))
        sS = es.enter_context(nc.semaphore("sS"))
        spin = es.enter_context(nc.semaphore("spin"))
        SEMS = {"qg": qg, "qs": qs, "qc": qc, "tS": tS, "vS": vS, "sS": sS}
        ENGSEM = {"tensor": "tS", "vector": "vS", "scalar": "sS"}
        C = {k: 0 for k in SEMS}
        chain = [None]

        block = es.enter_context(nc.Block())

        _SYNCED_OPS = {
            "tensor_tensor", "tensor_add", "tensor_sub", "tensor_mul", "tensor_max",
            "tensor_copy", "tensor_scalar", "tensor_single_scalar",
            "tensor_scalar_add", "tensor_scalar_sub", "tensor_scalar_mul",
            "tensor_scalar_max", "tensor_scalar_min", "tensor_reduce",
            "reduce_sum", "reduce_max", "reciprocal", "memset", "memzero",
            "tensor_tensor_scan", "scalar_tensor_tensor", "select",
            "activation", "copy", "add", "mul", "sqrt", "square", "sign",
        }

        class _SeqProxy:
            def __init__(self, E, sem, base):
                self._E = E
                self._sem = sem
                self._n = [base]

            def __getattr__(self, name):
                fnb = getattr(self._E, name)
                if name in _SYNCED_OPS:
                    def wrapped(*a, **k):
                        if self._n[0] > 0:
                            self._E.wait_ge(self._sem, self._n[0])
                        inst = fnb(*a, **k)
                        inst.then_inc(self._sem, 1)
                        self._n[0] += 1
                        return inst
                    return wrapped
                return fnb

        def _mkwaits(deps):
            if deps is None:
                dl = [chain[0]] if chain[0] is not None else []
            else:
                dl = [d for d in deps if d is not None]
            md = {}
            for sname, cntv in dl:
                if cntv > 0:
                    md[sname] = max(md.get(sname, 0), cntv)
            return md

        def barrier_deps():
            return [(s, C[s]) for s in SEMS if C[s] > 0]

        class _NoSeqProxy:
            def __init__(self, E, sem, base):
                self._E = E
                self._sem = sem
                self._n = [base]

            def __getattr__(self, name):
                fnb = getattr(self._E, name)
                if name in _SYNCED_OPS:
                    def wrapped(*a, **k):
                        inst = fnb(*a, **k)
                        inst.then_inc(self._sem, 1)
                        self._n[0] += 1
                        return inst
                    return wrapped
                return fnb

        def step(eng, fn, ndma=None, deps=None):
            md = _mkwaits(deps)
            if eng in ("vector_ns", "scalar_ns"):
                ename = eng[:-3]
                semn = ENGSEM[ename]
                base = C[semn]
                holder = [None]

                def body(E):
                    for sname, cntv in md.items():
                        E.wait_ge(SEMS[sname], cntv)
                    px = _NoSeqProxy(E, SEMS[semn], base)
                    fn(px)
                    holder[0] = px._n[0]
                getattr(block, ename)(body)
                assert holder[0] is not None and holder[0] > base, "empty group"
                C[semn] = holder[0]
                chain[0] = (semn, C[semn])
                return chain[0]
            if eng in ("sync", "gpsimd_dma", "scalar_dma"):
                qname = {"sync": "qs", "gpsimd_dma": "qg", "scalar_dma": "qc"}[eng]
                qsem = SEMS[qname]
                blkfn = {"sync": block.sync, "gpsimd_dma": block.gpsimd,
                         "scalar_dma": block.scalar}[eng]
                qprev = C[qname]

                def body(E):
                    if qprev > 0:
                        E.wait_ge(qsem, qprev)
                    for sname, cntv in md.items():
                        E.wait_ge(SEMS[sname], cntv)

                    def dma(dst, src):
                        E.dma_start(out=dst, in_=src).then_inc(qsem, 16)
                    dma.sem = qsem
                    dma.base = qprev
                    fn(E, dma)
                blkfn(body)
                C[qname] += 16 * ndma
                chain[0] = (qname, C[qname])
            elif eng == "tensor":
                semn = ENGSEM[eng]

                def body(E):
                    for sname, cntv in md.items():
                        E.wait_ge(SEMS[sname], cntv)
                    last = fn(E)
                    last.then_inc(SEMS[semn], 1)
                getattr(block, eng)(body)
                C[semn] += 1
                chain[0] = (semn, C[semn])
            else:
                semn = ENGSEM[eng]
                base = C[semn]
                holder = [None]

                def body(E):
                    for sname, cntv in md.items():
                        E.wait_ge(SEMS[sname], cntv)
                    px = _SeqProxy(E, SEMS[semn], base)
                    fn(px)
                    holder[0] = px._n[0]
                getattr(block, eng)(body)
                assert holder[0] is not None and holder[0] > base, "empty group"
                C[semn] = holder[0]
                chain[0] = (semn, C[semn])
            return chain[0]

        _scope_state = [None]

        def phase(name):
            if _scope_state[0] is not None:
                pn, pid = _scope_state[0]
                nc.leave_named_scope(pn, pid, False)
                _scope_state[0] = None
            if name is not None:
                sid, _ = nc.enter_named_scope(name, False)
                _scope_state[0] = (name, sid)

        xn3 = s_xn.rearrange("p (dc t) -> p dc t", t=T)
        smf = s_mid.bitcast(F32)
        smb = s_mid

        phase("ph0_consts")
        const_pairs = [
            (s_idf, "id_f32"), (s_idb, "id_bf16"), (s_tri, "tri_bf16"),
            (s_sB, "sB"), (s_sC, "sC"),
            (s_ones_c_bf, "ones_c_bf"), (s_ones_c_f, "ones_c_f32"),
            (s_ones_r_bf, "ones_r_bf"), (s_ones_r_f, "ones_r_f32"),
            (s_m128, "m128_r"), (s_role, "role_in"), (s_magic, "magic"),
            (s_mcmp, "magic_cmp"), (s_gnw, "gnw"), (s_temp, "temp"),
            (s_convw, "conv_w_loc"), (s_convb, "conv_b_loc"), (s_dtb, "dtb_loc"),
            (s_alanes, "alog_lanes"), (s_dloc, "d_loc"), (s_sgc, "sg_cols"),
            (s_suc, "su_cols"), (s_sdc, "sd_cols"), (s_wr, None),
        ]

        def ph0(E, dma):
            for dst, nm in const_pairs:
                if nm is None:
                    continue
                dma(dst[:], dram[nm])
            dma(s_wr.rearrange("p (dc e) -> p dc e", e=E_EXP)[:, 0:DMC, :],
                dram["wr"].rearrange("(dc p) e -> p dc e", p=P))
        step("gpsimd_dma", ph0, ndma=len(const_pairs))

        step("vector", lambda v: v.memset(s_ceps, EPS_LN))
        step("scalar", lambda s: s.activation(s_alanes[:], s_alanes[:], AF.Exp))
        step("vector", lambda v: v.tensor_scalar_mul(s_alanes[:], s_alanes[:], -1.0))
        step("vector", lambda v: v.tensor_scalar_mul(s_sdm128[:], s_sdc[:], -128.0))
        step("vector", lambda v: v.reciprocal(s_rtemp[0:1, 0:1], s_temp[0:1, 0:1]))
        step("tensor", lambda t: t.matmul(PS[0][:, 0:1], s_ones_r_f[:], s_rtemp[0:1, 0:1],
                                          start=True, stop=True))
        step("vector", lambda v: v.tensor_copy(s_rtemp[:, 0:1], PS[0][:, 0:1]))

        phase("ph1_ln0")
        NT = T // P
        K1 = {}
        bd1 = barrier_deps()
        xbufs1 = [s_big0, s_big4]
        for i in range(NT):
            xb1 = xbufs1[i % 2]
            K1['l', i] = step("sync", lambda E, dma, i=i, xb1=xb1:
                 dma(xb1[:, 0:DM], dram["x_own"][i * P:(i + 1) * P, :]), ndma=1,
                 deps=(bd1 if i < 2 else [K1['t', i - 2, DMC - 1]]))

            def stats1(v, xb1=xb1):
                v.reduce_sum(s_big3[:, 0:1], xb1[:, 0:DM], AX.X)
                v.tensor_mul(s_big2[:, 0:DM], xb1[:, 0:DM], xb1[:, 0:DM])
                v.reduce_sum(s_big3[:, 1:2], s_big2[:, 0:DM], AX.X)
                v.tensor_scalar_mul(s_big3[:, 2:3], s_big3[:, 0:1], -1.0 / DM)
                v.tensor_scalar_mul(s_big3[:, 3:4], s_big3[:, 1:2], 1.0 / DM)
                v.tensor_mul(s_big3[:, 4:5], s_big3[:, 2:3], s_big3[:, 2:3])
                return v.tensor_sub(s_big3[:, 5:6], s_big3[:, 3:4], s_big3[:, 4:5])
            K1['v1', i] = step("vector", stats1,
                deps=[K1['l', i]] + ([K1['rm', i - 1]] if i else []))

            def rstd(s):
                s.activation(s_big3[:, 6:7], s_big3[:, 5:6], AF.Ln, bias=s_ceps)
                return s.activation(s_big3[:, 7:8], s_big3[:, 6:7], AF.Exp, scale=-0.5)
            K1['s1', i] = step("scalar", rstd, deps=[K1['v1', i]])
            K1['v2', i] = step("vector", lambda v, xb1=xb1:
                 v.tensor_scalar(s_big1[:, 0:DM], xb1[:, 0:DM],
                                 s_big3[:, 2:3], s_big3[:, 7:8],
                                 ALU.add, ALU.mult),
                 deps=[K1['s1', i]] + ([K1['t', i - 1, DMC - 1]] if i else []))
            for dc in range(DMC):
                par = dc % 2

                def tr(t, dc=dc, par=par, xb1=xb1):
                    t.matmul(PS[par][:, 0:P], xb1[:, dc * P:(dc + 1) * P], s_idf[:],
                             is_transpose=True, start=True, stop=True)
                    return t.matmul(PS[2 + par][:, 0:P], s_big1[:, dc * P:(dc + 1) * P],
                                    s_idf[:], is_transpose=True, start=True, stop=True)
                K1['t', i, dc] = step("tensor", tr,
                    deps=[K1['v2', i]] + ([K1['e', i, dc - 2]] if dc >= 2 else
                                          (bd1 if i == 0 else [])))

                def ev(v, dc=dc, par=par, i=i):
                    v.tensor_copy(s_xacc[:, dc * T + i * P: dc * T + (i + 1) * P], PS[par][:, 0:P])
                    v.tensor_copy(xn3[:, dc, i * P:(i + 1) * P], PS[2 + par][:, 0:P])
                    return v.tensor_copy(s_big2[:, dc * P:(dc + 1) * P], PS[2 + par][:, 0:P])
                K1['e', i, dc] = step("vector", ev, deps=[K1['t', i, dc]])

            def router_mm(t, i=i):
                for dc in range(DMC):
                    last = t.matmul(PS[4][0:E_EXP, 0:P],
                                    s_wr.rearrange("p (dc e) -> p dc e", e=E_EXP)[:, dc, :],
                                    s_big2[:, dc * P:(dc + 1) * P],
                                    start=(dc == 0), stop=(dc == DMC - 1))
                return last
            K1['rm', i] = step("tensor", router_mm,
                deps=[K1['e', i, DMC - 1]] + ([K1['lc', i - 1]] if i else []))
            K1['lc', i] = step("vector", lambda v, i=i:
                 v.tensor_copy(s_logits[:, i * P:(i + 1) * P], PS[4][0:E_EXP, 0:P]),
                 deps=[K1['rm', i]])

        phase("ph2_inproj")
        K2 = {}
        bd2 = barrier_deps()
        for fc in range(64):
            fb2 = fc % 2
            wb2 = s_wbuf0 if fb2 == 0 else s_wbuf1
            eb2 = s_big0.bitcast(BF16)[:, fb2 * 2048:fb2 * 2048 + 2048]

            def wload(E, dma, fc=fc, wb2=wb2):
                dma(wb2.rearrange("p (dc j) -> p dc j", j=P),
                    dram["w_inT"][:, fc * P:(fc + 1) * P].rearrange("(dc p) j -> p dc j", p=P))
            K2['w', fc] = step("gpsimd_dma", wload, ndma=1,
                deps=(bd2 if fc < 2 else [K2['t', fc - 2]]))

            def mm(t, fc=fc, wb2=wb2, fb2=fb2):
                b3 = wb2.rearrange("p (dc j) -> p dc j", j=P)
                for half in range(2):
                    for dc in range(DMC):
                        last = t.matmul(PS[2 * fb2 + half][:, :], b3[:, dc, :],
                                        xn3[:, dc, half * 512:(half + 1) * 512],
                                        start=(dc == 0), stop=(dc == DMC - 1))
                return last
            K2['t', fc] = step("tensor", mm,
                deps=[K2['w', fc]] + ([K2['e', fc - 2]] if fc >= 2 else bd2))

            if fc < 32:
                def evx(v, eb2=eb2, fb2=fb2):
                    v.tensor_copy(eb2[:, 0:512], PS[2 * fb2][:, :])
                    return v.tensor_copy(eb2[:, 512:1024], PS[2 * fb2 + 1][:, :])
                K2['e', fc] = step("vector", evx,
                    deps=[K2['t', fc]] + ([K2['o', fc - 2]] if fc >= 2 else []))
            else:
                def evz(s, eb2=eb2, fb2=fb2):
                    s.activation(eb2[:, 0:512], PS[2 * fb2][:, :], AF.Silu)
                    return s.activation(eb2[:, 512:1024], PS[2 * fb2 + 1][:, :], AF.Silu)
                K2['e', fc] = step("scalar", evz,
                    deps=[K2['t', fc]] + ([K2['o', fc - 2]] if fc >= 2 else
                                          [K2['o', 30 + fc - 32]] if fc < 34 else []))
            dst2 = (dram["xc_dram"][fc * P:(fc + 1) * P, :] if fc < 32 else
                    dram["sz_dram"][(fc - 32) * P:(fc - 31) * P, :])
            K2['o', fc] = step("sync", lambda E, dma, dst2=dst2, eb2=eb2:
                 dma(dst2, eb2[:, 0:T]), ndma=1, deps=[K2['e', fc]])

        def exchange(tag, k, body_fn):
            pre_a, post_a = body_fn("a")
            pre_b, post_b = body_fn("b")
            assert len(pre_a) == len(pre_b) and len(post_a) == len(post_b)
            n_total = len(pre_a) + 1 + len(post_a)
            prevs = barrier_deps()
            cqs0 = C["qs"]

            def body(sy):
                for pn, pv in prevs:
                    sy.wait_ge(SEMS[pn], pv)
                role = sy.value_load(s_role[0:1, 0:1])
                mreg = sy.value_load(s_mcmp[0:1, k:k + 1])
                with sy.register(f"ca_{tag}") as ca, sy.register(f"cb_{tag}") as cb:
                    sy.reg_alu(ca, role, 0, ALU.is_equal)
                    sy.reg_alu(cb, role, 1, ALU.is_equal)
                    for ab, creg, pre, post in (("a", ca, pre_a, post_a),
                                                ("b", cb, pre_b, post_b)):
                        other = "b" if ab == "a" else "a"
                        with sy.While(creg):
                            nd = 0
                            for dst, src in pre:
                                sy.dma_start(out=dst, in_=src).then_inc(qs, 16)
                                nd += 16
                            sy.wait_ge(qs, cqs0 + nd)
                            sy.dma_start(out=dram[f"fl{k}_{ab}"],
                                         in_=s_magic[0:1, 8 * k:8 * k + 8]).then_inc(qs, 16)
                            nd += 16
                            sy.wait_ge(qs, cqs0 + nd)
                            if not SIM_RACECHECK:
                                with sy.register(f"cw_{tag}_{ab}") as cond:
                                    sy.sem_clear(spin)
                                    sy.dma_start(out=s_flag[:], in_=dram[f"fl{k}_{other}"]).then_inc(spin, 16)
                                    sy.wait_ge(spin, 16)
                                    fl = sy.value_load(s_flag[0:1, 0:1])
                                    sy.reg_alu(cond, fl, mreg, ALU.not_equal)
                                    with sy.While(cond):
                                        sy.sem_clear(spin)
                                        sy.dma_start(out=s_flag[:], in_=dram[f"fl{k}_{other}"]).then_inc(spin, 16)
                                        sy.wait_ge(spin, 16)
                                        fl2 = sy.value_load(s_flag[0:1, 0:1])
                                        sy.reg_alu(cond, fl2, mreg, ALU.not_equal)
                            for dst, src in post:
                                sy.dma_start(out=dst, in_=src).then_inc(qs, 16)
                                nd += 16
                            sy.wait_ge(qs, cqs0 + nd)
                            sy.reg_alu(creg, creg, 0, ALU.mult)
            block.sync(body)
            C["qs"] += 16 * n_total
            chain[0] = ("qs", C["qs"])

        def e1_body(ab):
            me0 = (ab == "a")
            own_lo, own_hi = (0, T) if me0 else (T, TT)
            send = (DIH, DI) if me0 else (0, DIH)
            keep = (0, DIH) if me0 else (DIH, DI)
            other = "b" if ab == "a" else "a"
            po_lo, po_hi = (T, TT) if me0 else (0, T)
            pre = [
                (dram[f"sh_e1_{ab}"][0:DIH, :], dram["xc_dram"][send[0]:send[1], :]),
                (dram[f"sh_e1_{ab}"][DIH:DI, :], dram["sz_dram"][send[0]:send[1], :]),
                (dram["xc_scan"][:, own_lo:own_hi], dram["xc_dram"][keep[0]:keep[1], :]),
                (dram["sz_scan"][:, own_lo:own_hi], dram["sz_dram"][keep[0]:keep[1], :]),
            ]
            post = [
                (dram["xc_scan"][:, po_lo:po_hi], dram[f"sh_e1_{other}"][0:DIH, :]),
                (dram["sz_scan"][:, po_lo:po_hi], dram[f"sh_e1_{other}"][DIH:DI, :]),
            ]
            return pre, post
        phase("exch_e1")
        exchange("e1", 0, e1_body)

        phase("ph3_conv_xproj")
        def wxp_load(E, dma):
            dma(s_wbig[:, 0:DIHC * 160].rearrange("p (dc o) -> p dc o", o=160),
                dram["w_xpT_loc"].rearrange("(dc p) o -> p dc o", p=P))
        step("gpsimd_dma", wxp_load, ndma=1)
        wxp3 = s_wbig[:, 0:DIHC * 160].rearrange("p (dc o) -> p dc o", o=160)
        cw3 = s_convw.rearrange("p (dc j) -> p dc j", j=4)
        K3 = {}
        bd3 = barrier_deps()
        xcbufs = [s_big2.bitcast(BF16), s_big4.bitcast(BF16)]
        accbufs = [s_big1, s_big0]

        def pad0(v):
            v.memset(xcbufs[0][:, 0:3], 0.0)
            return v.memset(xcbufs[1][:, 0:3], 0.0)
        K3['pad'] = step("vector", pad0, deps=bd3)

        for dc in range(DIHC):
            p3 = dc % 2
            xc_in = xcbufs[p3]
            acc = accbufs[p3]
            xts = s_big3.bitcast(BF16)[:, p3 * 2048:p3 * 2048 + 2048]

            K3['l', dc] = step("sync", lambda E, dma, dc=dc, xc_in=xc_in:
                 dma(xc_in[:, 3:3 + TT], dram["xc_scan"][dc * P:(dc + 1) * P, :]), ndma=1,
                 deps=(bd3 if dc < 2 else [K3['c', dc - 2]]))

            def conv(v, dc=dc, xc_in=xc_in, acc=acc):
                v.tensor_scalar_mul(acc[:, 0:TT], xc_in[:, 0:TT], cw3[:, dc, 0:1])
                v.scalar_tensor_tensor(acc[:, 0:TT], xc_in[:, 1:1 + TT], cw3[:, dc, 1:2],
                                       acc[:, 0:TT], ALU.mult, ALU.add)
                v.scalar_tensor_tensor(acc[:, 0:TT], xc_in[:, 2:2 + TT], cw3[:, dc, 2:3],
                                       acc[:, 0:TT], ALU.mult, ALU.add)
                return v.scalar_tensor_tensor(acc[:, 0:TT], xc_in[:, 3:3 + TT], cw3[:, dc, 3:4],
                                              acc[:, 0:TT], ALU.mult, ALU.add)
            K3['c', dc] = step("vector", conv,
                deps=[K3['l', dc]] + ([K3['s', dc - 2]] if dc >= 2 else [K3['pad']]))

            K3['s', dc] = step("scalar", lambda s, dc=dc, acc=acc, xts=xts:
                 s.activation(xts[:, 0:TT], acc[:, 0:TT],
                              AF.Silu, bias=s_convb[:, dc:dc + 1]),
                 deps=[K3['c', dc]] +
                      ([K3['t', dc - 2], K3['o', dc - 2]] if dc >= 2 else []))
            K3['o', dc] = step("sync", lambda E, dma, dc=dc, xts=xts:
                 dma(dram["xt_scan"][dc * P:(dc + 1) * P, :], xts[:, 0:TT]),
                 ndma=1, deps=[K3['s', dc]])

            def xp(t, dc=dc, xts=xts):
                for tg in range(4):
                    t.matmul(PS[tg][:, :], wxp3[:, dc, 0:P],
                             xts[:, tg * 512:(tg + 1) * 512],
                             start=(dc == 0), stop=(dc == DIHC - 1))
                    last = t.matmul(PS[4 + tg][0:32, :], wxp3[:, dc, P:160],
                                    xts[:, tg * 512:(tg + 1) * 512],
                                    start=(dc == 0), stop=(dc == DIHC - 1))
                return last
            K3['t', dc] = step("tensor", xp,
                deps=[K3['s', dc]] + (bd3 if dc == 0 else []))

        def xp_evac(v):
            for tg in range(4):
                v.tensor_copy(s_big0[:, tg * 512:(tg + 1) * 512], PS[tg][:, :])
                last = v.tensor_copy(s_big1[0:32, tg * 512:(tg + 1) * 512], PS[4 + tg][0:32, :])
            return last
        step("vector", xp_evac)

        def e2_body(ab):
            other = "b" if ab == "a" else "a"
            pre = [
                (dram[f"sh_e2_{ab}"][0:P, :], s_big0[:, 0:TT]),
                (dram[f"sh_e2_{ab}"][P:160, :], s_big1[0:32, 0:TT]),
            ]
            post = [
                (s_big2[:, 0:TT], dram[f"sh_e2_{other}"][0:P, :]),
                (s_big3[0:32, 0:TT], dram[f"sh_e2_{other}"][P:160, :]),
            ]
            return pre, post
        phase("exch_e2")
        exchange("e2", 1, e2_body)

        s_dtlow = s_big4.bitcast(BF16)

        def sums(v):
            v.tensor_add(s_dtlow[:, 0:TT], s_big0[:, 0:TT], s_big2[:, 0:TT])
            return v.tensor_add(s_big3[0:32, 0:TT], s_big1[0:32, 0:TT], s_big3[0:32, 0:TT])
        step("vector", sums)
        step("sync", lambda E, dma: dma(dram["bc_dram"], s_big3[0:32, 0:TT]), ndma=1)

        phase("ph4_scan")
        def ph4_wload(E, dma):
            dma(s_wbig[:, 0:2048], dram["w_dtT_loc"])
            dma(s_wbig[:, 2048:4096], dram["selrep"])
            dma(s_wbig[:, 4096:6144], dram["selred"])
        step("gpsimd_dma", ph4_wload, ndma=3)
        s_wdt = s_wbig[:, 0:2048]
        selrep3 = s_wbig[:, 2048:4096].rearrange("p (m j) -> p m j", j=P)
        selred3 = s_wbig[:, 4096:6144].rearrange("p (m j) -> p m j", j=P)
        dA_reg = [smf[:, 0:1024], smf[:, 4096:5120]]
        v_reg = [smf[:, 1024:2048], smf[:, 3072:4096]]
        h_reg = [smf[:, 2048:3072], s_big1[:, 0:1024]]
        hc_reg = [smb[:, 14336:15360], smb[:, 15360:16384]]
        ya_stg = smb[:, 15360:16384]
        S4 = {}
        dbg_dt = any(nm == "dt_dbg" for nm, _ in DEBUG_COPY)
        dbg_scan = any(nm == "scan_dbg" for nm, _ in DEBUG_COPY)
        bds = barrier_deps()

        for th in range(2):
            tlo = th * 1024
            S4['bl', th] = step("sync", lambda E, dma, tlo=tlo:
                 dma(s_big0[0:32, 0:1024], dram["bc_dram"][:, tlo:tlo + 1024]), ndma=1,
                 deps=(bds if th == 0 else [S4['bt', 0]]))

            def bc_build(t):
                for half in range(2):
                    t.matmul(PS[4 + half][:, :], s_sB[:, :],
                             s_big0[0:32, half * 512:(half + 1) * 512],
                             start=True, stop=True)
                    last = t.matmul(PS[6 + half][:, :], s_sC[:, :],
                                    s_big0[0:32, half * 512:(half + 1) * 512],
                                    start=True, stop=True)
                return last
            S4['bt', th] = step("tensor", bc_build,
                deps=[S4['bl', th]] +
                     ([S4['v1a', th - 1, DIHC - 1, N - 1], S4['ya', th - 1, DIHC - 1]]
                      if th else bds))

            def bc_ev(v):
                for half in range(2):
                    v.tensor_copy(smf[:, 5120 + half * 512:5120 + (half + 1) * 512],
                                  PS[4 + half][:, :])
                    last = v.tensor_copy(smf[:, 6144 + half * 512:6144 + (half + 1) * 512],
                                         PS[6 + half][:, :])
                return last
            S4['be', th] = step("vector", bc_ev, deps=[S4['bt', th]])

            for dc in range(DIHC):
                tb = dc % 2
                wbt = s_wbuf0 if tb == 0 else s_wbuf1

                def tile_ld(E, dma, dc=dc, tlo=tlo, wbt=wbt):
                    dma(wbt[:, 0:1024], dram["xt_scan"][dc * P:(dc + 1) * P, tlo:tlo + 1024])
                    dma(wbt[:, 1024:2048], dram["sz_scan"][dc * P:(dc + 1) * P, tlo:tlo + 1024])
                tl_deps = []
                if (th, dc) == (0, 0):
                    tl_deps = bds
                elif (th, dc) == (0, 1):
                    tl_deps = bds
                else:
                    pdc = dc - 2
                    pth = th
                    if pdc < 0:
                        pdc += DIHC
                        pth -= 1
                    tl_deps = [S4['ya', pth, pdc]]
                S4['tl', th, dc] = step("sync", tile_ld, ndma=2, deps=tl_deps)

                def dt_mm(t, dc=dc, tlo=tlo):
                    for half in range(2):
                        last = t.matmul(PS[half][:, :], s_wdt[:, dc * P:(dc + 1) * P],
                                        s_dtlow[:, tlo + half * 512: tlo + (half + 1) * 512],
                                        start=True, stop=True)
                    return last
                dtm_deps = []
                if dc == 0:
                    dtm_deps = [S4['be', th]] if th else bds
                else:
                    dtm_deps = [S4['sv1', th, dc - 1]]
                S4['dt', th, dc] = step("tensor", dt_mm, deps=dtm_deps)

                def sp_v1(v, dc=dc):
                    for half in range(2):
                        v.tensor_scalar_add(smf[:, 3072 + half * 512:3072 + (half + 1) * 512],
                                            PS[half][:, :], s_dtb[:, dc:dc + 1])
                    v.tensor_scalar_mul(smf[:, 4096:5120], smf[:, 3072:4096], -1.0)
                    return v.tensor_max(smf[:, 4096:5120], smf[:, 4096:5120],
                                        smf[:, 3072:4096])
                sv1_deps = [S4['dt', th, dc]]
                if (th, dc) != (0, 0):
                    pdc, pth = (dc - 1, th) if dc else (DIHC - 1, th - 1)
                    sv1_deps += [S4['v1b', pth, pdc, N - 1]]
                S4['sv1', th, dc] = step("vector", sp_v1, deps=sv1_deps)

                def sp_s(s):
                    s.activation(smf[:, 4096:5120], smf[:, 4096:5120], AF.Exp, scale=-1.0)
                    return s.activation(smf[:, 4096:5120], smf[:, 4096:5120], AF.Ln, bias=1.0)
                S4['sp', th, dc] = step("scalar", sp_s, deps=[S4['sv1', th, dc]])

                def sp_v2(v, dc=dc, wbt=wbt):
                    v.tensor_scalar_max(smf[:, 0:1024], smf[:, 3072:4096], 0.0)
                    v.tensor_add(s_dtlow[:, 2048:3072], smf[:, 4096:5120], smf[:, 0:1024])
                    return v.tensor_mul(s_dtlow[:, 3072:4096], s_dtlow[:, 2048:3072],
                                        wbt[:, 0:1024])
                spv2_deps = [S4['sp', th, dc], S4['tl', th, dc]]
                if (th, dc) != (0, 0):
                    pdc, pth = (dc - 1, th) if dc else (DIHC - 1, th - 1)
                    spv2_deps += [S4['rp', pth, pdc, N - 1]]
                S4['dtu', th, dc] = step("vector", sp_v2, deps=spv2_deps)
                if dbg_dt:
                    step("sync", lambda E, dma, dc=dc, tlo=tlo:
                         dma(dram["dt_dbg"][dc * P:(dc + 1) * P, tlo:tlo + 1024],
                             s_dtlow[:, 2048:3072]), ndma=1)

                if (th, dc) != (0, 0):
                    pdc, pth = (dc - 1, th) if dc else (DIHC - 1, th - 1)
                else:
                    pdc = pth = None

                def mk_red(m):
                    mp = m % 2

                    def red(t, m=m, mp=mp):
                        for half in range(2):
                            last = t.matmul(PS[6 + half][:, :], selred3[:, m, :],
                                            hc_reg[mp][:, half * 512:(half + 1) * 512],
                                            start=(m == 0), stop=(m == N - 1))
                        return last
                    rd_deps = [S4['vh', th, dc, m]]
                    if m == 0 and pdc is not None:
                        rd_deps += [S4['ya', pth, pdc]]
                    S4['rd', th, dc, m] = step("tensor", red, deps=rd_deps)

                def mk_vh(m):
                    mp = m % 2
                    k = dc * N + m

                    def hCf(v, m=m, mp=mp, k=k, th=th):
                        last = v.tensor_mul(hc_reg[mp][:, 0:1024], h_reg[mp][:, 0:1024],
                                            smf[:, 6144:7168])
                        if th == 0:
                            last = v.tensor_copy(s_hcarry[:, k:k + 1],
                                                 h_reg[mp][:, 1023:1024])
                        return last
                    vh_deps = [S4['v1b', th, dc, m]]
                    if m >= 2:
                        vh_deps += [S4['rd', th, dc, m - 2]]
                    elif pdc is not None:
                        vh_deps += [S4['rd', pth, pdc, N - 1]]
                        if m == 1:
                            vh_deps += [S4['yo', pth, pdc]]
                    S4['vh', th, dc, m] = step("vector_ns", hCf, deps=vh_deps)

                for m in range(N):
                    k = dc * N + m
                    mp = m % 2

                    def rep(t, m=m):
                        for sub in range(2):
                            t.matmul(PS[2 + sub][:, :], selrep3[:, m, :],
                                     s_dtlow[:, 2048 + sub * 512:2048 + (sub + 1) * 512],
                                     start=True, stop=True)
                            last = t.matmul(PS[4 + sub][:, :], selrep3[:, m, :],
                                            s_dtlow[:, 3072 + sub * 512:3072 + (sub + 1) * 512],
                                            start=True, stop=True)
                        return last
                    if m == 0:
                        rp_deps = [S4['dtu', th, dc]]
                        if pdc is not None:
                            rp_deps += [S4['da', pth, pdc, N - 1],
                                        S4['v1a', pth, pdc, N - 1]]
                    else:
                        rp_deps = [S4['da', th, dc, m - 1], S4['v1a', th, dc, m - 1]]
                    S4['rp', th, dc, m] = step("tensor", rep, deps=rp_deps)

                    def dA(s, k=k, mp=mp):
                        s.activation(dA_reg[mp][:, 0:512], PS[2][:, :], AF.Exp,
                                     scale=s_alanes[:, k:k + 1])
                        return s.activation(dA_reg[mp][:, 512:1024], PS[3][:, :], AF.Exp,
                                            scale=s_alanes[:, k:k + 1])
                    da_deps = [S4['rp', th, dc, m]]
                    if m >= 2:
                        da_deps += [S4['v1b', th, dc, m - 2]]
                    elif pdc is not None:
                        da_deps += [S4['v1b', pth, pdc, N - 1]]
                    S4['da', th, dc, m] = step("scalar_ns", dA, deps=da_deps)

                    def v1af(v, mp=mp):
                        v.tensor_mul(v_reg[mp][:, 0:512], PS[4][:, :], smf[:, 5120:5632])
                        return v.tensor_mul(v_reg[mp][:, 512:1024], PS[5][:, :],
                                            smf[:, 5632:6144])
                    va_deps = [S4['rp', th, dc, m]]
                    if m >= 2:
                        va_deps += [S4['v1b', th, dc, m - 2]]
                    elif pdc is not None:
                        va_deps += [S4['v1b', pth, pdc, N - 1]]
                    S4['v1a', th, dc, m] = step("vector_ns", v1af, deps=va_deps)

                    if m > 0:
                        mk_vh(m - 1)
                        mk_red(m - 1)

                    def scanf(v, k=k, th=th, mp=mp):
                        init = 0.0 if th == 0 else s_hcarry[:, k:k + 1]
                        return v.tensor_tensor_scan(h_reg[mp][:, 0:1024],
                                                    dA_reg[mp][:, 0:1024],
                                                    v_reg[mp][:, 0:1024], init,
                                                    ALU.mult, ALU.add)
                    sc_deps = [S4['v1a', th, dc, m], S4['da', th, dc, m]]
                    if m >= 2:
                        sc_deps += [S4['vh', th, dc, m - 2]]
                    elif pdc is not None:
                        sc_deps += [S4['vh', pth, pdc, N - 1]]
                    S4['v1b', th, dc, m] = step("vector_ns", scanf, deps=sc_deps)

                    if dbg_scan and th == 0 and dc == 0 and m == 0:
                        step("sync", lambda E, dma:
                             dma(dram["scan_dbg"], smf[:, 0:3072]), ndma=1)

                mk_vh(N - 1)
                mk_red(N - 1)

                def ya(v, dc=dc, wbt=wbt):
                    for half in range(2):
                        v.scalar_tensor_tensor(smf[:, 3072 + half * 512:3072 + (half + 1) * 512],
                                               wbt[:, half * 512:(half + 1) * 512],
                                               s_dloc[:, dc:dc + 1],
                                               PS[6 + half][:, :], ALU.mult, ALU.add)
                    return v.tensor_mul(ya_stg[:, 0:1024], smf[:, 3072:4096],
                                        wbt[:, 1024:2048])
                S4['ya', th, dc] = step("vector", ya, deps=[S4['rd', th, dc, N - 1]])
                S4['yo', th, dc] = step("scalar_dma", lambda E, dma, dc=dc, tlo=tlo:
                     dma(dram["ya_loc"][dc * P:(dc + 1) * P, tlo:tlo + 1024],
                         ya_stg[:, 0:1024]), ndma=1, deps=[S4['ya', th, dc]])

        def e3_body(ab):
            me0 = (ab == "a")
            own = (0, T) if me0 else (T, TT)
            myrows = (0, DIH) if me0 else (DIH, DI)
            other = "b" if ab == "a" else "a"
            pr = (DIH, DI) if me0 else (0, DIH)
            pre = [
                (dram[f"sh_e3_{ab}"], dram["ya_loc"][:, (T if me0 else 0):(TT if me0 else T)]),
                (dram["ya_own"][myrows[0]:myrows[1], :], dram["ya_loc"][:, own[0]:own[1]]),
            ]
            post = [(dram["ya_own"][pr[0]:pr[1], :], dram[f"sh_e3_{other}"])]
            return pre, post
        phase("exch_e3")
        exchange("e3", 2, e3_body)

        phase("ph4b_outproj")
        G = {}
        bd4 = barrier_deps()
        ob4 = s_big4.bitcast(BF16)
        for grp in range(16):
            gb_ = grp % 2
            wb4 = s_wbig[:, 0:4096] if gb_ == 0 else ob4[:, 0:4096]
            yb4 = smb[:, 0:2048] if gb_ == 0 else smb[:, 2048:4096]

            def wld(E, dma, grp=grp, wb4=wb4):
                dma(wb4.rearrange("p (j d) -> p j d", d=DM),
                    dram["w_outT"][grp * 256:(grp + 1) * 256, :].rearrange("(j p) d -> p j d", p=P))
            G['w', grp] = step("gpsimd_dma", wld, ndma=1,
                deps=(bd4 if grp < 2 else [G['t', grp - 2, DMC - 1]]))

            def yld(E, dma, grp=grp, yb4=yb4):
                dma(yb4.rearrange("p (j t) -> p j t", t=T),
                    dram["ya_own"][grp * 256:(grp + 1) * 256, :].rearrange("(j p) t -> p j t", p=P))
            G['y', grp] = step("sync", yld, ndma=1,
                deps=(bd4 if grp < 2 else [G['t', grp - 2, DMC - 1]]))

            for dm in range(DMC):
                pb4 = 2 * (dm % 2)

                def opmm(t, dm=dm, wb4=wb4, yb4=yb4, pb4=pb4):
                    w3 = wb4.rearrange("p (j d) -> p j d", d=DM)
                    y3 = yb4.rearrange("p (j t) -> p j t", t=T)
                    for half in range(2):
                        for j in range(2):
                            last = t.matmul(PS[pb4 + half][:, :], w3[:, j, dm * P:(dm + 1) * P],
                                            y3[:, j, half * 512:(half + 1) * 512],
                                            start=(j == 0), stop=(j == 1))
                    return last
                op_deps = [G['w', grp], G['y', grp]]
                if dm >= 2:
                    op_deps += [G['v', grp, dm - 2]]
                elif grp >= 1:
                    op_deps += [G['v', grp - 1, DMC - 2 + dm]]
                else:
                    op_deps += bd4
                G['t', grp, dm] = step("tensor", opmm, deps=op_deps)

                def opadd(v, dm=dm, pb4=pb4):
                    for half in range(2):
                        last = v.scalar_tensor_tensor(
                            s_xacc[:, dm * T + half * 512: dm * T + (half + 1) * 512],
                            PS[pb4 + half][:, :], 1.0,
                            s_xacc[:, dm * T + half * 512: dm * T + (half + 1) * 512],
                            ALU.mult, ALU.add)
                    return last
                G['v', grp, dm] = step("vector", opadd, deps=[G['t', grp, dm]])

        phase("ph5_moe")
        HM = {}
        bdr = barrier_deps()
        for i in range(NT):
            bk = i % 2
            HM['rt', i] = step("tensor", lambda t, i=i, bk=bk:
                 t.matmul(PS[bk][:, 0:E_EXP], s_logits[:, i * P:(i + 1) * P],
                          s_idf[64:64 + E_EXP, 64:64 + E_EXP],
                          is_transpose=True, start=True, stop=True),
                 deps=(bdr if i < 2 else [HM['rv', i - 2]]))
            HM['rv', i] = step("vector", lambda v, i=i, bk=bk:
                 v.tensor_copy(s_big0[:, i * E_EXP:(i + 1) * E_EXP], PS[bk][:, 0:E_EXP]),
                 deps=[HM['rt', i]])

        step("scalar", lambda s: s.activation(s_big2[:, 0:64], s_big0[:, 0:64],
                                              AF.Sigmoid, scale=s_rtemp[:, 0:1]))

        def rv1(v):
            v.tensor_scalar_add(s_big2[:, 64:128], s_big2[:, 0:64], -GOLDEN_CENTER)
            v.tensor_scalar(s_big2[:, 128:192], s_big2[:, 0:64], GOLDEN_LOWER, None, ALU.is_ge)
            v.tensor_scalar(s_big2[:, 192:256], s_big2[:, 0:64], GOLDEN_UPPER, None, ALU.is_le)
            return v.tensor_mul(s_big2[:, 128:192], s_big2[:, 128:192], s_big2[:, 192:256])
        step("vector", rv1)

        def rsig(s):
            s.activation(s_big2[:, 256:320], s_big2[:, 64:128], AF.Abs)
            s.activation(s_big2[:, 320:384], s_big2[:, 256:320], AF.Exp, scale=-10.0)
            return s.activation(s_big2[:, 384:448], s_big2[:, 256:320], AF.Exp, scale=-1.0 / 0.3)
        step("scalar", rsig)

        def rv2(v):
            v.tensor_mul(s_big2[:, 448:512], s_big2[:, 320:384], s_big2[:, 128:192])
            for i in range(NT):
                def a(c):
                    return s_big2[:, c + i * E_EXP: c + (i + 1) * E_EXP]
                wt = a(448)
                fb = a(384)
                v.reduce_sum(s_big3[:, i:i + 1], wt, AX.X)
                v.tensor_scalar(s_big3[:, 8 + i:9 + i], s_big3[:, i:i + 1], 1e-8, None, ALU.is_lt)
                v.reduce_max(s_big3[:, 16 + i:17 + i], fb, AX.X)
                v.tensor_scalar(a(512), fb, s_big3[:, 16 + i:17 + i], None, ALU.is_ge)
                v.tensor_scalar_mul(a(576), a(512), 1e30)
                v.tensor_sub(a(576), fb, a(576))
                v.reduce_max(s_big3[:, 24 + i:25 + i], a(576), AX.X)
                v.tensor_scalar(a(640), a(576), s_big3[:, 24 + i:25 + i], None, ALU.is_ge)
                v.tensor_add(a(640), a(640), a(512))
                v.tensor_mul(a(640), a(640), fb)
                v.reduce_sum(s_big3[:, 32 + i:33 + i], a(640), AX.X)
                v.tensor_scalar_max(s_big3[:, 32 + i:33 + i], s_big3[:, 32 + i:33 + i], 1e-8)
                v.reciprocal(s_big3[:, 40 + i:41 + i], s_big3[:, 32 + i:33 + i])
                v.tensor_scalar_mul(a(640), a(640), s_big3[:, 40 + i:41 + i])
                v.tensor_sub(a(704), a(640), wt)
                v.scalar_tensor_tensor(a(704), a(704), s_big3[:, 8 + i:9 + i], wt,
                                       ALU.mult, ALU.add)
                v.reduce_sum(s_big3[:, 48 + i:49 + i], a(704), AX.X)
                v.tensor_scalar_max(s_big3[:, 48 + i:49 + i], s_big3[:, 48 + i:49 + i], 1e-8)
                v.reciprocal(s_big3[:, 56 + i:57 + i], s_big3[:, 48 + i:49 + i])
                last = v.tensor_scalar_mul(a(704), a(704), s_big3[:, 56 + i:57 + i])
            return last
        step("vector", rv2)

        hrv2 = chain[0]
        for i in range(NT):
            bk = i % 2
            HM['wt', i] = step("tensor", lambda t, i=i, bk=bk:
                 t.matmul(PS[bk][0:E_EXP, 0:P],
                          s_big2[:, 704 + i * E_EXP:704 + (i + 1) * E_EXP],
                          s_idf[:, :], is_transpose=True, start=True, stop=True),
                 deps=([hrv2] if i < 2 else [HM['wv', i - 2], hrv2]))
            HM['wv', i] = step("vector", lambda v, i=i, bk=bk:
                 v.tensor_copy(s_wfm[:, i * P:(i + 1) * P], PS[bk][0:E_EXP, 0:P]),
                 deps=[HM['wt', i]])

        moe_wg3 = dram["moe_wg"].rearrange("(e d) f -> e d f", e=E_EXP)
        moe_wu3 = dram["moe_wu"].rearrange("(e d) f -> e d f", e=E_EXP)
        moe_wd3 = dram["moe_wd"].rearrange("(e f) d -> e f d", e=E_EXP)
        ga3 = dram["moe_ga"].rearrange("(e d) r -> e d r", e=NL)
        gb3 = dram["moe_gb"].rearrange("(e r) f -> e r f", e=NL)
        ua3 = dram["moe_ua"].rearrange("(e d) r -> e d r", e=NL)
        ub3 = dram["moe_ub"].rearrange("(e r) f -> e r f", e=NL)
        da3 = dram["moe_da"].rearrange("(e f) r -> e f r", e=NL)
        db3 = dram["moe_db"].rearrange("(e r) d -> e r d", e=NL)
        mid3 = smb.rearrange("p (q t) -> p q t", t=T)

        big4b = s_big4.bitcast(BF16)
        bd0 = barrier_deps()
        for e in range(E_EXP):
            eb = e % 2

            def wrep(E, dma, e=e, eb=eb, cqs=C["qs"]):
                k_ = [0]
                off = eb * 1024

                def dmaw(dst, src):
                    dma(dst, src)
                    k_[0] += 16
                    E.wait_ge(qs, cqs + k_[0])
                dmaw(s_big2[0:1, off:off + 1024], s_wfm[e:e + 1, :])
                n = 1
                while n < P:
                    dmaw(s_big2[n:min(2 * n, P), off:off + 1024],
                         s_big2[0:min(n, P - n), off:off + 1024])
                    n *= 2
            HM['wrv', e] = step("sync", wrep, ndma=8,
                deps=(bd0 if e == 0 else [HM['mv', e - 1, 1], HM['wv', 7]]))

            if e < NL:
                def lr_load(E, dma, e=e):
                    dma(s_wbig[0:R_LR, 0:F_MOE], gb3[e])
                    dma(s_wbig[0:R_LR, F_MOE:2 * F_MOE], ub3[e])
                    dma(s_wbig[:, 4096:5120].rearrange("p (dc r) -> p dc r", r=R_LR),
                        ga3[e].rearrange("(dc p) r -> p dc r", p=P))
                    dma(s_wbig[:, 5120:6144].rearrange("p (dc r) -> p dc r", r=R_LR),
                        ua3[e].rearrange("(dc p) r -> p dc r", p=P))
                HM['lrl', e] = step("gpsimd_dma", lr_load, ndma=4,
                    deps=(bd0 if e == 0 else [HM['dm', e - 1, 15]]))
                ga_s = s_wbig[:, 4096:5120].rearrange("p (dc r) -> p dc r", r=R_LR)
                ua_s = s_wbig[:, 5120:6144].rearrange("p (dc r) -> p dc r", r=R_LR)

                for li, (off, wsrc, half) in enumerate(
                        [(0, ga_s, 0), (0, ga_s, 1), (1024, ua_s, 0), (1024, ua_s, 1)]):
                    def lrmm(t, wsrc=wsrc, half=half):
                        for dc in range(DMC):
                            last = t.matmul(PS[7][0:R_LR, :], wsrc[:, dc, :],
                                            xn3[:, dc, half * 512:(half + 1) * 512],
                                            start=(dc == 0), stop=(dc == DMC - 1))
                        return last
                    HM['lrt', e, li] = step("tensor", lrmm,
                        deps=[HM['lrl', e]] +
                             ([HM['lrv', e, li - 1]] if li else
                              ([HM['mv', e - 1, 15], HM['dav', e - 1, 1]]
                               if e else bd0)))
                    HM['lrv', e, li] = step("vector", lambda v, off=off, half=half:
                         v.tensor_copy(s_lr[0:R_LR, off + half * 512:off + (half + 1) * 512],
                                       PS[7][0:R_LR, :]),
                        deps=[HM['lrt', e, li]])

            for fc in range(FC_MOE):
                fb = fc % 2
                wbg = s_wbuf0 if fb == 0 else big4b[:, 0:2048]
                wbu = s_wbuf1 if fb == 0 else big4b[:, 2048:4096]
                sb0 = s_big0[:, fb * 1024:(fb + 1) * 1024]
                sb1 = s_big1[:, fb * 1024:(fb + 1) * 1024]

                def gw(E, dma, e=e, fc=fc, wbg=wbg, wbu=wbu):
                    dma(wbg.rearrange("p (dc j) -> p dc j", j=P),
                        moe_wg3[e][:, fc * P:(fc + 1) * P].rearrange("(dc p) j -> p dc j", p=P))
                    dma(wbu.rearrange("p (dc j) -> p dc j", j=P),
                        moe_wu3[e][:, fc * P:(fc + 1) * P].rearrange("(dc p) j -> p dc j", p=P))
                HM['gw', e, fc] = step("gpsimd_dma", gw, ndma=2,
                    deps=(bd0 if (e == 0 and fc < 2) else
                          [HM['gu', e, fc - 2]] if fc >= 2 else [HM['dm', e - 1, 15]]))

                pb = 0 if fb == 0 else 4

                def gumm(t, e=e, fc=fc, wbg=wbg, wbu=wbu, pb=pb):
                    bg = wbg.rearrange("p (dc j) -> p dc j", j=P)
                    bu = wbu.rearrange("p (dc j) -> p dc j", j=P)
                    last = None
                    for half in range(2):
                        for dc in range(DMC):
                            last = t.matmul(PS[pb + half][:, :], bg[:, dc, :],
                                            xn3[:, dc, half * 512:(half + 1) * 512],
                                            start=(dc == 0),
                                            stop=(dc == DMC - 1 and e >= NL))
                        if e < NL:
                            last = t.matmul(PS[pb + half][:, :],
                                            s_wbig[0:R_LR, fc * P:(fc + 1) * P],
                                            s_lr[0:R_LR, half * 512:(half + 1) * 512],
                                            start=False, stop=True)
                        for dc in range(DMC):
                            last = t.matmul(PS[pb + 2 + half][:, :], bu[:, dc, :],
                                            xn3[:, dc, half * 512:(half + 1) * 512],
                                            start=(dc == 0),
                                            stop=(dc == DMC - 1 and e >= NL))
                        if e < NL:
                            last = t.matmul(PS[pb + 2 + half][:, :],
                                            s_wbig[0:R_LR, F_MOE + fc * P:F_MOE + (fc + 1) * P],
                                            s_lr[0:R_LR, 1024 + half * 512:1024 + (half + 1) * 512],
                                            start=False, stop=True)
                    return last
                gu_deps = [HM['gw', e, fc]]
                if fc >= 2:
                    gu_deps += [HM['sl', e, fc - 2], HM['mv', e, fc - 2]]
                elif e == 0:
                    gu_deps += bd0 + ([HM['lrv', e, 3]] if e < NL else [])
                elif fc == 0:
                    gu_deps += [HM['da', e - 1, 15]] + \
                        ([HM['lrv', e, 3]] if e < NL else [])
                else:
                    gu_deps += [HM['wrv', e], HM['mv', e - 1, 15], HM['sl', e - 1, 15]]
                    if e < NL:
                        gu_deps += [HM['lrv', e, 3]]
                    if e - 1 < NL:
                        gu_deps += [HM['dav', e - 1, 1]]
                HM['gu', e, fc] = step("tensor", gumm, deps=gu_deps)

                def silu_g(s, sb0=sb0, pb=pb):
                    for half in range(2):
                        last = s.activation(sb0[:, half * 512:(half + 1) * 512],
                                            PS[pb + half][:, :], AF.Silu)
                    return last
                HM['sl', e, fc] = step("scalar", silu_g,
                    deps=[HM['gu', e, fc]] +
                         ([HM['mv', e, fc - 2]] if fc >= 2 else []))

                def midv(v, fc=fc, eb=eb, sb0=sb0, sb1=sb1, pb=pb):
                    for half in range(2):
                        v.tensor_mul(sb1[:, half * 512:(half + 1) * 512],
                                     sb0[:, half * 512:(half + 1) * 512],
                                     PS[pb + 2 + half][:, :])
                        last = v.tensor_mul(mid3[:, fc, half * 512:(half + 1) * 512],
                                            sb1[:, half * 512:(half + 1) * 512],
                                            s_big2[:, eb * 1024 + half * 512:
                                                   eb * 1024 + (half + 1) * 512])
                    return last
                HM['mv', e, fc] = step("vector", midv,
                    deps=[HM['sl', e, fc]] + ([HM['wrv', e]] if fc < 2 else []))

            if e < NL:
                def da_load(E, dma, e=e):
                    dma(s_wbig[:, 4096:5120].rearrange("p (q r) -> p q r", r=R_LR),
                        da3[e].rearrange("(q p) r -> p q r", p=P))
                    dma(s_wbig[0:R_LR, 0:DM], db3[e])
                HM['dal', e] = step("gpsimd_dma", da_load, ndma=2,
                    deps=[HM['gu', e, 15]])
                da_s = s_wbig[:, 4096:5120].rearrange("p (q r) -> p q r", r=R_LR)

                for half in range(2):
                    def damm(t, half=half):
                        for q in range(FC_MOE):
                            last = t.matmul(PS[7][0:R_LR, :], da_s[:, q, :],
                                            mid3[:, q, half * 512:(half + 1) * 512],
                                            start=(q == 0), stop=(q == FC_MOE - 1))
                        return last
                    HM['dat', e, half] = step("tensor", damm,
                        deps=[HM['dal', e], HM['mv', e, 15]] +
                             ([HM['dav', e, 0]] if half else []))
                    HM['dav', e, half] = step("vector", lambda v, half=half:
                         v.tensor_copy(s_lr[0:R_LR, half * 512:(half + 1) * 512],
                                       PS[7][0:R_LR, :]),
                        deps=[HM['dat', e, half]])

            for dm in range(DMC):
                db_ = dm % 2
                wbd = s_wbuf0 if db_ == 0 else s_wbuf1

                def dw(E, dma, e=e, dm=dm, wbd=wbd):
                    dma(wbd.rearrange("p (q j) -> p q j", j=P),
                        moe_wd3[e][:, dm * P:(dm + 1) * P].rearrange("(q p) j -> p q j", p=P))
                HM['dw', e, dm] = step("gpsimd_dma", dw, ndma=1,
                    deps=([HM['dm', e, dm - 2]] if dm >= 2 else
                          [HM['gu', e, 14 + dm]]))

                def dmm(t, e=e, dm=dm, wbd=wbd, db_=db_):
                    b3 = wbd.rearrange("p (q j) -> p q j", j=P)
                    last = None
                    for half in range(2):
                        for q in range(FC_MOE):
                            last = t.matmul(PS[2 * db_ + half][:, :], b3[:, q, :],
                                            mid3[:, q, half * 512:(half + 1) * 512],
                                            start=(q == 0),
                                            stop=(q == FC_MOE - 1 and e >= NL))
                        if e < NL:
                            last = t.matmul(PS[2 * db_ + half][:, :],
                                            s_wbig[0:R_LR, dm * P:(dm + 1) * P],
                                            s_lr[0:R_LR, half * 512:(half + 1) * 512],
                                            start=False, stop=True)
                    return last
                dm_deps = [HM['dw', e, dm], HM['mv', e, 15]]
                if dm >= 2:
                    dm_deps += [HM['da', e, dm - 2]]
                else:
                    dm_deps += [HM['sl', e, 15]]
                    if e < NL:
                        dm_deps += [HM['dav', e, 1]]
                HM['dm', e, dm] = step("tensor", dmm, deps=dm_deps)

                def dadd(v, dm=dm, db_=db_):
                    for half in range(2):
                        last = v.scalar_tensor_tensor(
                            s_xacc[:, dm * T + half * 512: dm * T + (half + 1) * 512],
                            PS[2 * db_ + half][:, :], 1.0,
                            s_xacc[:, dm * T + half * 512: dm * T + (half + 1) * 512],
                            ALU.mult, ALU.add)
                    return last
                HM['da', e, dm] = step("vector", dadd, deps=[HM['dm', e, dm]])

        phase("ph6_ln1")
        for dc in range(DMC):
            step("scalar", lambda s, dc=dc:
                 s.activation(s_big0[:, 0:1024], s_xacc[:, dc * T:(dc + 1) * T], AF.Square))

            def st6(t, dc=dc):
                t.matmul(PS[6][0:1, :], s_ones_c_f[:, :], s_xacc[:, dc * T: dc * T + 512],
                         start=(dc == 0), stop=(dc == DMC - 1))
                t.matmul(PS[5][0:1, :], s_ones_c_f[:, :], s_xacc[:, dc * T + 512: dc * T + 1024],
                         start=(dc == 0), stop=(dc == DMC - 1))
                t.matmul(PS[7][0:1, :], s_ones_c_f[:, :], s_big0[:, 0:512],
                         start=(dc == 0), stop=(dc == DMC - 1))
                return t.matmul(PS[4][0:1, :], s_ones_c_f[:, :], s_big0[:, 512:1024],
                                start=(dc == 0), stop=(dc == DMC - 1))
            step("tensor", st6)

        def rows6(v):
            v.tensor_copy(row0[0:1, 0:512], PS[6][0:1, :])
            v.tensor_copy(row0[0:1, 512:1024], PS[5][0:1, :])
            v.tensor_copy(row0[0:1, 1024:1536], PS[7][0:1, :])
            v.tensor_copy(row0[0:1, 1536:2048], PS[4][0:1, :])
            v.tensor_scalar_mul(row0[0:1, 0:1024], row0[0:1, 0:1024], -1.0 / DM)
            v.tensor_scalar_mul(row0[0:1, 1024:2048], row0[0:1, 1024:2048], 1.0 / DM)
            v.tensor_mul(s_big1[0:1, 1024:2048], row0[0:1, 0:1024], row0[0:1, 0:1024])
            return v.tensor_sub(row0[0:1, 1024:2048], row0[0:1, 1024:2048],
                                s_big1[0:1, 1024:2048])
        step("vector", rows6)

        def rstd6(s):
            s.activation(row0[0:1, 1024:2048], row0[0:1, 1024:2048], AF.Ln,
                         bias=s_rtemp[0:1, 1:2])
            return s.activation(row0[0:1, 1024:2048], row0[0:1, 1024:2048],
                                AF.Exp, scale=-0.5)
        step("scalar", rstd6)

        def rep6(t):
            t.matmul(PS[6][:, :], s_ones_r_f[:, :], row0[0:1, 0:512], start=True, stop=True)
            t.matmul(PS[5][:, :], s_ones_r_f[:, :], row0[0:1, 512:1024], start=True, stop=True)
            t.matmul(PS[7][:, :], s_ones_r_f[:, :], row0[0:1, 1024:1536], start=True, stop=True)
            return t.matmul(PS[4][:, :], s_ones_r_f[:, :], row0[0:1, 1536:2048],
                            start=True, stop=True)
        step("tensor", rep6)

        def xn1f(v):
            v.tensor_copy(s_big1[:, 0:512], PS[6][:, :])
            v.tensor_copy(s_big1[:, 512:1024], PS[5][:, :])
            v.tensor_copy(s_big2[:, 0:512], PS[7][:, :])
            v.tensor_copy(s_big2[:, 512:1024], PS[4][:, :])
            for dc in range(DMC):
                v.tensor_add(s_big0[:, 0:1024], s_xacc[:, dc * T:(dc + 1) * T], s_big1[:, 0:1024])
                last = v.tensor_mul(xn3[:, dc, :], s_big0[:, 0:1024], s_big2[:, 0:1024])
            return last
        step("vector", xn1f)

        phase("ph7a_proj")
        l1w = [("wq1", 0, 8), ("wk1", 1024, 8), ("ws1", 2048, 8),
               ("wv1", 3072, 16), ("wg1", 5120, 16)]
        K7 = {}
        bd7a = barrier_deps()
        gidx = 0
        for wname, rbase, nfc in l1w:
            for fc in range(nfc):
                g = gidx
                gidx += 1
                gp = g % 2
                wb7a = s_wbuf0 if gp == 0 else s_wbuf1
                eb7a = s_big0.bitcast(BF16)[:, gp * 2048:gp * 2048 + 2048]

                K7['w', g] = step("gpsimd_dma", lambda E, dma, wname=wname, fc=fc, wb7a=wb7a:
                     dma(wb7a.rearrange("p (dc j) -> p dc j", j=P),
                         dram[wname][:, fc * P:(fc + 1) * P].rearrange("(dc p) j -> p dc j", p=P)),
                     ndma=1, deps=(bd7a if g < 2 else [K7['t', g - 2]]))

                def pmm(t, wb7a=wb7a, gp=gp):
                    b3 = wb7a.rearrange("p (dc j) -> p dc j", j=P)
                    for half in range(2):
                        for dc in range(DMC):
                            last = t.matmul(PS[2 * gp + half][:, :], b3[:, dc, :],
                                            xn3[:, dc, half * 512:(half + 1) * 512],
                                            start=(dc == 0), stop=(dc == DMC - 1))
                    return last
                K7['t', g] = step("tensor", pmm,
                    deps=[K7['w', g]] + ([K7['e', g - 2]] if g >= 2 else bd7a))

                if wname == "ws1":
                    def pev(v, eb7a=eb7a, gp=gp):
                        for half in range(2):
                            last = v.tensor_scalar(
                                eb7a[:, half * 512:(half + 1) * 512],
                                PS[2 * gp + half][:, :], 32.0, -32.0, ALU.min, ALU.max)
                        return last
                else:
                    def pev(v, eb7a=eb7a, gp=gp):
                        for half in range(2):
                            last = v.tensor_copy(
                                eb7a[:, half * 512:(half + 1) * 512],
                                PS[2 * gp + half][:, :])
                        return last
                K7['e', g] = step("vector", pev,
                    deps=[K7['t', g]] + ([K7['o', g - 2]] if g >= 2 else []))
                K7['o', g] = step("sync", lambda E, dma, rbase=rbase, fc=fc, eb7a=eb7a:
                     dma(dram["attn_loc"][rbase + fc * P: rbase + (fc + 1) * P, :],
                         eb7a[:, 0:T]), ndma=1, deps=[K7['e', g]])

        segs = [("attn_q", 0, 512, 0), ("attn_k", 1024, 512, 512),
                ("attn_s", 2048, 512, 1024), ("attn_v", 3072, 1024, 1536),
                ("attn_g", 5120, 1024, 2560)]

        def e4_body(ab):
            me0 = (ab == "a")
            own_lo, own_hi = (0, T) if me0 else (T, TT)
            other = "b" if ab == "a" else "a"
            po_lo, po_hi = (T, TT) if me0 else (0, T)
            pre, post = [], []
            for can, lbase, sz, shb in segs:
                mine_lo = lbase + (0 if me0 else sz)
                send_lo = lbase + (sz if me0 else 0)
                pre.append((dram[can][:, own_lo:own_hi],
                            dram["attn_loc"][mine_lo:mine_lo + sz, :]))
                pre.append((dram[f"sh_e4_{ab}"][shb:shb + sz, :],
                            dram["attn_loc"][send_lo:send_lo + sz, :]))
                post.append((dram[can][:, po_lo:po_hi],
                             dram[f"sh_e4_{other}"][shb:shb + sz, :]))
            return pre, post
        phase("exch_e4")
        exchange("e4", 3, e4_body)

        phase("ph7b_abc")
        b0b = s_big0.bitcast(BF16)
        b1b = s_big1.bitcast(BF16)
        b2b = s_big2.bitcast(BF16)
        b3b = s_big3.bitcast(BF16)
        p2v = s_big4.bitcast(BF16)[:, 2048:4096]
        o3 = smb.rearrange("p (hh t) -> p hh t", t=TT)

        for h in range(HH):
            def hld(E, dma, h=h):
                dma(b0b[0:DK, 0:TT], dram["attn_q"][h * DK:(h + 1) * DK, :])
                dma(b0b[0:DK, 2048:4096], dram["attn_k"][h * DK:(h + 1) * DK, :])
                dma(b1b[0:M_ABC, 2048:4096], dram["attn_s"][h * M_ABC:(h + 1) * M_ABC, :])
                dma(b3b[:, 2048:4096], dram["attn_v"][h * DV:(h + 1) * DV, :])
                dma(s_wbuf0[:, 0:TT], dram["attn_g"][h * DV:(h + 1) * DV, :])
            step("sync", hld, ndma=5)

            step("scalar", lambda s:
                 s.activation(b1b[0:M_ABC, 0:TT], b1b[0:M_ABC, 2048:4096], AF.Exp))
            step("scalar", lambda s:
                 s.activation(s_wbuf1[:, 0:TT], s_wbuf0[:, 0:TT], AF.Silu))

            ktm = b2b[:, 0:1024].rearrange("p (c x) -> p c x", x=DK)
            ektm = b2b[:, 1024:2048].rearrange("p (c x) -> p c x", x=DK)
            vtm = b3b[:, 0:2048].rearrange("p (c x) -> p c x", x=DV)
            sinv = s_big4[:, 0:1024].rearrange("p (c x) -> p c x", x=DK)
            rfm = b2b[:, 2048:4096]
            hk_sb = b1b[0:DK, 4096:4160]
            hv_sb = b1b[0:M_ABC, 4160:4288]
            rtm_sb = b1b[:, 4288:4352]
            s1m = b1b[:, 4352:4480]

            A7 = {}
            hbar = chain[0]
            for cc in range(16):
                tb3 = 0 if cc % 2 == 0 else 3

                def trp(t, cc=cc, tb3=tb3):
                    t.matmul(PSB[tb3][:, 0:DK], b0b[0:DK, 2048 + cc * P:2048 + (cc + 1) * P],
                             s_idb[0:DK, 0:DK], is_transpose=True, start=True, stop=True)
                    t.matmul(PSB[tb3 + 1][:, 0:DK], b1b[0:M_ABC, cc * P:(cc + 1) * P],
                             s_idb[0:M_ABC, 0:M_ABC], is_transpose=True, start=True, stop=True)
                    return t.matmul(PSB[tb3 + 2][:, 0:DV], b3b[:, 2048 + cc * P:2048 + (cc + 1) * P],
                                    s_idb[:, :], is_transpose=True, start=True, stop=True)
                A7['tp', cc] = step("tensor", trp,
                    deps=([hbar] if cc < 2 else [A7['te', cc - 2]]))

                def trev(v, cc=cc, tb3=tb3):
                    v.tensor_copy(ktm[:, cc, :], PSB[tb3][:, 0:DK])
                    v.tensor_copy(ektm[:, cc, :], PSB[tb3 + 1][:, 0:DK])
                    return v.tensor_copy(vtm[:, cc, :], PSB[tb3 + 2][:, 0:DV])
                A7['te', cc] = step("vector", trev, deps=[A7['tp', cc]])

            for cc in range(16):
                def p1a(t, cc=cc):
                    t.matmul(PS[6][:, 0:DK], s_tri[:, :], ektm[:, cc, :],
                             start=True, stop=(cc == 0))
                    if cc > 0:
                        t.matmul(PS[6][:, 0:DK], s_ones_r_bf[:, :], s_crow[0:1, :],
                                 start=False, stop=True)
                    t.matmul(PS[7][0:1, 0:DK], s_ones_c_bf[:, :], ektm[:, cc, :],
                             start=(cc == 0), stop=(cc == 15))
                    return t.matmul(PS[5][:, 0:P], b0b[0:DK, 2048 + cc * P:2048 + (cc + 1) * P],
                                    b0b[0:DK, cc * P:(cc + 1) * P], start=True, stop=True)
                A7['a', cc] = step("tensor", p1a,
                    deps=([A7['te', 15]] if cc == 0 else [A7['b', cc - 1]]))

                def p1b(v, cc=cc):
                    v.reciprocal(sinv[:, cc, :], PS[6][:, 0:DK])
                    v.tensor_copy(s_crow[0:1, :], PS[7][0:1, 0:DK])
                    return v.scalar_tensor_tensor(s1m, PS[5][:, 0:P], 0.125,
                                                  s_tri[:, :], ALU.mult, ALU.mult)
                A7['b', cc] = step("vector", p1b,
                    deps=[A7['a', cc]] + ([A7['c', cc - 1]] if cc else []))

                def p1c(t, cc=cc):
                    t.matmul(PS[4][:, 0:DK], s1m, ektm[:, cc, :],
                             start=True, stop=(cc == 0))
                    if cc > 0:
                        t.matmul(PS[4][:, 0:DK], b0b[0:DK, cc * P:(cc + 1) * P],
                                 hk_sb, start=False, stop=True)
                    return t.matmul(PS[3][0:DK, 0:DK], ktm[:, cc, :], ektm[:, cc, :],
                                    start=(cc == 0), stop=(cc == 15))
                A7['c', cc] = step("tensor", p1c,
                    deps=[A7['b', cc]] +
                         ([A7['hk', cc - 1], A7['mu', cc - 1]] if cc else []))

                A7['hk', cc] = step("scalar", lambda s:
                     s.activation(hk_sb, PS[3][0:DK, 0:DK], AF.Identity, scale=0.125),
                     deps=[A7['c', cc]])

                A7['mu', cc] = step("vector", lambda v, cc=cc:
                     v.tensor_mul(s_big4[:, 1024:1088], PS[4][:, 0:DK], sinv[:, cc, :]),
                     deps=[A7['c', cc]] + ([A7['ex', cc - 1]] if cc else []))
                A7['ex', cc] = step("scalar", lambda s:
                     s.activation(s_big4[:, 1088:1152], s_big4[:, 1024:1088], AF.Exp),
                     deps=[A7['mu', cc]] + ([A7['sm', cc - 1]] if cc else []))

                def smax2(v, cc=cc):
                    v.reduce_sum(s_big4[:, 1152:1153], s_big4[:, 1088:1152], AX.X)
                    v.reciprocal(s_big4[:, 1153:1154], s_big4[:, 1152:1153])
                    v.tensor_mul(s_big4[:, 1216:1280], s_big4[:, 1088:1152], sinv[:, cc, :])
                    return v.tensor_scalar_mul(rtm_sb, s_big4[:, 1216:1280],
                                               s_big4[:, 1153:1154])
                A7['sm', cc] = step("vector", smax2,
                    deps=[A7['ex', cc]] + ([A7['rt', cc - 1]] if cc else []))

                rb7 = 1 if cc % 2 == 0 else 2
                A7['rt', cc] = step("tensor", lambda t, rb7=rb7:
                     t.matmul(PSB[rb7][0:DK, 0:P], rtm_sb[:, 0:DK], s_idb[:, :],
                              is_transpose=True, start=True, stop=True),
                     deps=[A7['sm', cc]] + ([A7['rc', cc - 2]] if cc >= 2 else []))
                A7['rc', cc] = step("vector", lambda v, cc=cc, rb7=rb7:
                     v.tensor_copy(rfm[0:DK, cc * P:(cc + 1) * P], PSB[rb7][0:DK, 0:P]),
                     deps=[A7['rt', cc]])

            step("vector", lambda v: v.tensor_copy(s_big4[:, 1280:1281],
                                                   s_big4[:, 1152:1153]),
                 deps=[A7['rc', 15], A7['hk', 15], A7['ex', 15]])

            for j in range(16):
                ncols = TT - j * P
                nmm = (ncols + 511) // 512

                for qbase in range(0, nmm, 3):
                    qgrp = list(range(qbase, min(qbase + 3, nmm)))

                    def p2a(t, j=j, qgrp=qgrp):
                        for idx, q in enumerate(qgrp):
                            cl = j * P + q * 512
                            cw = min(512, TT - cl)
                            last = t.matmul(PS[4 + idx][:, 0:cw],
                                            b1b[0:M_ABC, j * P:(j + 1) * P],
                                            rfm[0:M_ABC, cl:cl + cw],
                                            start=True, stop=True)
                        return last
                    step("tensor", p2a)

                    def p2b(v, j=j, qgrp=qgrp):
                        for idx, q in enumerate(qgrp):
                            cl = j * P + q * 512
                            cw = min(512, TT - cl)
                            if q == 0:
                                last = v.tensor_mul(p2v[:, 0:P],
                                                    PS[4 + idx][:, 0:P], s_tri[:, :])
                                if cw > P:
                                    last = v.tensor_copy(p2v[:, P:cw],
                                                         PS[4 + idx][:, P:cw])
                            else:
                                last = v.tensor_copy(p2v[:, cl - j * P:cl - j * P + cw],
                                                     PS[4 + idx][:, 0:cw])
                        return last
                    step("vector", p2b)

                def p2c(t, j=j):
                    last = None
                    if j > 0:
                        bank, off = divmod(j * P, 512)
                        last = t.matmul(PS[bank][:, off:off + P], hv_sb,
                                        rfm[0:M_ABC, j * P:(j + 1) * P], start=False, stop=False)
                    for q in range(4):
                        blo, bhi = q * 512, (q + 1) * 512
                        lo = max(j * P, blo)
                        if lo >= bhi:
                            continue
                        last = t.matmul(PS[q][:, lo - blo:bhi - blo], vtm[:, j, :],
                                        p2v[:, lo - j * P:bhi - j * P],
                                        start=(j == 0), stop=(j == min(4 * q + 3, 15)))
                    last = t.matmul(PS[7][0:M_ABC, 0:DV], ektm[:, j, :], vtm[:, j, :],
                                    start=(j == 0), stop=(j == 15))
                    return last
                step("tensor", p2c)

                step("scalar", lambda s:
                     s.activation(hv_sb, PS[7][0:M_ABC, 0:DV], AF.Identity))

            def rms_s(s):
                for q in range(4):
                    last = s.activation(b0b[:, q * 512:(q + 1) * 512], PS[q][:, :], AF.Square)
                return last
            step("scalar", rms_s)

            for q in range(4):
                step("tensor", lambda t, q=q:
                     t.matmul(PS[6][0:1, 0:512], s_ones_c_bf[:, :],
                              b0b[:, q * 512:(q + 1) * 512], start=True, stop=True))
                step("vector", lambda v, q=q:
                     v.tensor_copy(row0[0:1, q * 512:(q + 1) * 512], PS[6][0:1, 0:512]))

            step("vector", lambda v:
                 v.tensor_scalar_mul(row0, row0, 1.0 / DV))

            def rmsact(s):
                s.activation(row0, row0, AF.Ln, bias=s_rtemp[0:1, 1:2])
                return s.activation(row0, row0, AF.Exp, scale=-0.5)
            step("scalar", rmsact)

            for q in range(4):
                step("tensor", lambda t, q=q:
                     t.matmul(PS[5][:, :], s_ones_r_f[:, :], row0[0:1, q * 512:(q + 1) * 512],
                              start=True, stop=True))

                def gate(v, h=h, q=q):
                    v.tensor_copy(s_big4[:, 1024:1536], PS[5][:, :])
                    v.tensor_mul(s_big4[:, 1536:2048], PS[q][:, :], s_big4[:, 1024:1536])
                    v.tensor_scalar_mul(s_big4[:, 1536:2048], s_big4[:, 1536:2048], s_gnw[:, 0:1])
                    return v.tensor_mul(o3[:, h, q * 512:(q + 1) * 512], s_big4[:, 1536:2048],
                                        s_wbuf1[:, q * 512:(q + 1) * 512])
                step("vector", gate)

        def e5_body(ab):
            me0 = (ab == "a")
            own_lo = 0 if me0 else T
            myrows = 0 if me0 else 1024
            other = "b" if ab == "a" else "a"
            prr = 1024 if me0 else 0
            pre = []
            for hh_ in range(HH):
                pre.append((dram["o_own"][myrows + hh_ * DV: myrows + (hh_ + 1) * DV, :],
                            o3[:, hh_, own_lo:own_lo + T]))
                pre.append((dram[f"sh_e5_{ab}"][hh_ * DV:(hh_ + 1) * DV, :],
                            o3[:, hh_, (T if me0 else 0):(TT if me0 else T)]))
            post = [(dram["o_own"][prr:prr + 1024, :], dram[f"sh_e5_{other}"])]
            return pre, post
        phase("exch_e5")
        exchange("e5", 4, e5_body)

        phase("ph7c_wo")
        def oo_ld(E, dma):
            dma(smb.rearrange("p (j t) -> p j t", t=T)[:, 0:16, :],
                dram["o_own"].rearrange("(j p) t -> p j t", p=P))
        step("sync", oo_ld, ndma=1)

        W7 = {}
        bd7 = barrier_deps()
        wo4b = s_big4.bitcast(BF16)
        for grp in range(8):
            gb7 = grp % 2
            wb7 = s_wbig[:, 0:4096] if gb7 == 0 else wo4b[:, 0:4096]

            def wld7(E, dma, grp=grp, wb7=wb7):
                dma(wb7.rearrange("p (j d) -> p j d", d=DM),
                    dram["wo1"][grp * 256:(grp + 1) * 256, :].rearrange("(j p) d -> p j d", p=P))
            W7['w', grp] = step("gpsimd_dma", wld7, ndma=1,
                deps=(bd7 if grp < 2 else [W7['t', grp - 2, DMC - 1]]))

            for dm in range(DMC):
                pb7 = 2 * (dm % 2)

                def womm(t, grp=grp, dm=dm, wb7=wb7, pb7=pb7):
                    w3 = wb7.rearrange("p (j d) -> p j d", d=DM)
                    om = smb.rearrange("p (j t) -> p j t", t=T)
                    for half in range(2):
                        for j in range(2):
                            last = t.matmul(PS[pb7 + half][:, :], w3[:, j, dm * P:(dm + 1) * P],
                                            om[:, grp * 2 + j, half * 512:(half + 1) * 512],
                                            start=(j == 0), stop=(j == 1))
                    return last
                wo_deps = [W7['w', grp]]
                if dm >= 2:
                    wo_deps += [W7['v', grp, dm - 2]]
                elif grp >= 1:
                    wo_deps += [W7['v', grp - 1, DMC - 2 + dm]]
                else:
                    wo_deps += bd7
                W7['t', grp, dm] = step("tensor", womm, deps=wo_deps)

                def woadd(v, dm=dm, pb7=pb7):
                    for half in range(2):
                        last = v.scalar_tensor_tensor(
                            s_xacc[:, dm * T + half * 512: dm * T + (half + 1) * 512],
                            PS[pb7 + half][:, :], 1.0,
                            s_xacc[:, dm * T + half * 512: dm * T + (half + 1) * 512],
                            ALU.mult, ALU.add)
                    return last
                W7['v', grp, dm] = step("vector", woadd, deps=[W7['t', grp, dm]])

        phase("ph8_qffn")
        def rs_t(t):
            for dc in range(DMC):
                t.matmul(PS[6][0:1, :], s_ones_c_bf[:, :], xn3[:, dc, 0:512],
                         start=(dc == 0), stop=(dc == DMC - 1))
                last = t.matmul(PS[5][0:1, :], s_ones_c_bf[:, :], xn3[:, dc, 512:1024],
                                start=(dc == 0), stop=(dc == DMC - 1))
            return last
        step("tensor", rs_t, deps=barrier_deps())

        def rs_v(v):
            v.tensor_copy(row0[0:1, 0:512], PS[6][0:1, :])
            v.tensor_copy(row0[0:1, 512:1024], PS[5][0:1, :])
            v.tensor_copy(rowb0, row0[0:1, 0:1024])
            v.tensor_sub(s_big1[0:1, 0:1024], row0[0:1, 0:1024], rowb0)
            return v.tensor_copy(rowb1, s_big1[0:1, 0:1024])
        step("vector", rs_v)
        bd8 = barrier_deps()

        mid11 = smb.rearrange("p (q t) -> p q t", t=T)
        K8 = {}
        q4b = s_big4.bitcast(BF16)
        for sup in range(4):
            for hc_i in range(11):
                hc = sup * 11 + hc_i
                par8 = hc % 2
                pb8 = 0 if par8 == 0 else 4
                wg8 = s_wbuf0 if par8 == 0 else q4b[:, 0:2048]
                wu8 = s_wbuf1 if par8 == 0 else q4b[:, 2048:4096]
                sb8 = s_big0[:, par8 * 1024:(par8 + 1) * 1024]

                def qwl(E, dma, hc=hc, wg8=wg8, wu8=wu8):
                    dma(wg8.rearrange("p (dc j) -> p dc j", j=P),
                        dram["qg_T"][:, hc * P:(hc + 1) * P].rearrange("(dc p) j -> p dc j", p=P))
                    dma(wu8.rearrange("p (dc j) -> p dc j", j=P),
                        dram["qu_T"][:, hc * P:(hc + 1) * P].rearrange("(dc p) j -> p dc j", p=P))
                K8['w', sup, hc_i] = step("gpsimd_dma", qwl, ndma=2,
                    deps=(bd8 if (sup == 0 and hc_i < 2) else
                          [K8['t', sup, hc_i - 2]] if hc_i >= 2 else
                          [K8['dt', sup - 1, DMC - 1]]))

                def qmm(t, hc=hc, wg8=wg8, wu8=wu8, pb8=pb8):
                    bg = wg8.rearrange("p (dc j) -> p dc j", j=P)
                    bu = wu8.rearrange("p (dc j) -> p dc j", j=P)
                    for half in range(2):
                        for dc in range(DMC):
                            t.matmul(PS[pb8 + half][:, :], bg[:, dc, :],
                                     xn3[:, dc, half * 512:(half + 1) * 512],
                                     start=(dc == 0), stop=False)
                        t.matmul(PS[pb8 + half][:, :], s_m128[:, :],
                                 rowb0[0:1, half * 512:(half + 1) * 512], start=False, stop=False)
                        t.matmul(PS[pb8 + half][:, :], s_m128[:, :],
                                 rowb1[0:1, half * 512:(half + 1) * 512], start=False, stop=True)
                        for dc in range(DMC):
                            t.matmul(PS[pb8 + 2 + half][:, :], bu[:, dc, :],
                                     xn3[:, dc, half * 512:(half + 1) * 512],
                                     start=(dc == 0), stop=False)
                        t.matmul(PS[pb8 + 2 + half][:, :], s_m128[:, :],
                                 rowb0[0:1, half * 512:(half + 1) * 512], start=False, stop=False)
                        last = t.matmul(PS[pb8 + 2 + half][:, :], s_m128[:, :],
                                        rowb1[0:1, half * 512:(half + 1) * 512],
                                        start=False, stop=True)
                    return last
                qm_deps = [K8['w', sup, hc_i]]
                if hc_i >= 2:
                    qm_deps += [K8['sl', sup, hc_i - 2], K8['mv', sup, hc_i - 2]]
                elif sup == 0:
                    qm_deps += bd8
                elif par8 == 0:
                    qm_deps += [K8['dv', sup - 1, DMC - 1]]
                else:
                    qm_deps += [K8['r2v', sup - 1], K8['sl', sup - 1, 10],
                                K8['mv', sup - 1, 10]]
                K8['t', sup, hc_i] = step("tensor", qmm, deps=qm_deps)

                def qsil(s, hc=hc, sb8=sb8, pb8=pb8):
                    for half in range(2):
                        last = s.activation(sb8[:, half * 512:(half + 1) * 512],
                                            PS[pb8 + half][:, :], AF.Silu,
                                            scale=s_sgc[:, hc:hc + 1])
                    return last
                K8['sl', sup, hc_i] = step("scalar", qsil,
                    deps=[K8['t', sup, hc_i]] +
                         ([K8['mv', sup, hc_i - 2]] if hc_i >= 2 else []))

                def qmid(v, hc=hc, hc_i=hc_i, sb8=sb8, pb8=pb8):
                    for half in range(2):
                        last = v.scalar_tensor_tensor(
                            mid11[:, hc_i, half * 512:(half + 1) * 512],
                            PS[pb8 + 2 + half][:, :], s_suc[:, hc:hc + 1],
                            sb8[:, half * 512:(half + 1) * 512], ALU.mult, ALU.mult)
                    return last
                K8['mv', sup, hc_i] = step("vector", qmid, deps=[K8['sl', sup, hc_i]])

            def rs2t(t, sup=sup):
                for hc_i in range(11):
                    t.matmul(PS[6][0:1, :], s_ones_c_bf[:, :], mid11[:, hc_i, 0:512],
                             start=(hc_i == 0), stop=(hc_i == 10))
                    last = t.matmul(PS[7][0:1, :], s_ones_c_bf[:, :], mid11[:, hc_i, 512:1024],
                                    start=(hc_i == 0), stop=(hc_i == 10))
                return last
            K8['r2t', sup] = step("tensor", rs2t, deps=[K8['mv', sup, 10]])

            def rs2v(v, sup=sup):
                if sup == 0:
                    v.tensor_copy(row0[0:1, 0:512], PS[6][0:1, :])
                    return v.tensor_copy(row0[0:1, 512:1024], PS[7][0:1, :])
                v.tensor_add(row0[0:1, 0:512], row0[0:1, 0:512], PS[6][0:1, :])
                return v.tensor_add(row0[0:1, 512:1024], row0[0:1, 512:1024],
                                    PS[7][0:1, :])
            K8['r2v', sup] = step("vector", rs2v, deps=[K8['r2t', sup]])

            for dm in range(DMC):
                pbD = 2 * (dm % 2)
                wbD = s_wbuf0 if dm % 2 == 0 else s_wbuf1

                def qdw(E, dma, sup=sup, dm=dm, wbD=wbD):
                    dma(wbD[:, 0:11 * P].rearrange("p (q j) -> p q j", j=P),
                        dram["qd_T"][sup * 11 * P:(sup + 1) * 11 * P, dm * P:(dm + 1) * P]
                        .rearrange("(q p) j -> p q j", p=P))
                K8['dw', sup, dm] = step("gpsimd_dma", qdw, ndma=1,
                    deps=([K8['dt', sup, dm - 2]] if dm >= 2 else [K8['t', sup, 10]]))

                def qdmm(t, dm=dm, wbD=wbD, pbD=pbD):
                    b3 = wbD[:, 0:11 * P].rearrange("p (q j) -> p q j", j=P)
                    for half in range(2):
                        for q in range(11):
                            last = t.matmul(PS[pbD + half][:, :], b3[:, q, :],
                                            mid11[:, q, half * 512:(half + 1) * 512],
                                            start=(q == 0), stop=(q == 10))
                    return last
                K8['dt', sup, dm] = step("tensor", qdmm,
                    deps=[K8['dw', sup, dm], K8['mv', sup, 10]] +
                         ([K8['dv', sup, dm - 2]] if dm >= 2 else
                          [K8['sl', sup, 10]]))

                def qdadd(v, dm=dm, pbD=pbD):
                    for half in range(2):
                        last = v.scalar_tensor_tensor(
                            s_xacc[:, dm * T + half * 512: dm * T + (half + 1) * 512],
                            PS[pbD + half][:, :], s_sdc[:, dm:dm + 1],
                            s_xacc[:, dm * T + half * 512: dm * T + (half + 1) * 512],
                            ALU.mult, ALU.add)
                    return last
                K8['dv', sup, dm] = step("vector", qdadd, deps=[K8['dt', sup, dm]])

        def rs2rep(t):
            t.matmul(PS[6][:, :], s_ones_r_f[:, :], row0[0:1, 0:512], start=True, stop=True)
            return t.matmul(PS[5][:, :], s_ones_r_f[:, :], row0[0:1, 512:1024],
                            start=True, stop=True)
        step("tensor", rs2rep,
             deps=[K8['r2v', 3], K8['sl', 3, 10], K8['dv', 3, DMC - 1]])

        def rs2corr(v):
            for dm in range(DMC):
                for half in range(2):
                    bank = PS[6] if half == 0 else PS[5]
                    last = v.scalar_tensor_tensor(
                        s_xacc[:, dm * T + half * 512: dm * T + (half + 1) * 512],
                        bank[:, :], s_sdm128[:, dm:dm + 1],
                        s_xacc[:, dm * T + half * 512: dm * T + (half + 1) * 512],
                        ALU.mult, ALU.add)
            return last
        step("vector", rs2corr)

        phase("ph9_out")
        K9 = {}
        bd9 = barrier_deps()
        for i in range(NT):
            ob9 = s_big0 if i % 2 == 0 else s_big2
            for dc in range(DMC):
                k9 = i * DMC + dc
                bk9 = k9 % 4
                K9['t', k9] = step("tensor", lambda t, i=i, dc=dc, bk9=bk9:
                     t.matmul(PS[bk9][:, 0:P], s_xacc[:, dc * T + i * P: dc * T + (i + 1) * P],
                              s_idf[:], is_transpose=True, start=True, stop=True),
                     deps=(bd9 if k9 < 4 else [K9['v', k9 - 4]]))
                ev9 = [K9['t', k9]]
                if dc == 0 and i >= 2:
                    ev9 += [K9['o', i - 2]]
                K9['v', k9] = step("vector", lambda v, dc=dc, ob9=ob9, bk9=bk9:
                     v.tensor_copy(ob9[:, dc * P:(dc + 1) * P], PS[bk9][:, 0:P]),
                     deps=ev9)
            K9['o', i] = step("scalar_dma", lambda E, dma, i=i, ob9=ob9:
                 dma(dram["y"][i * P:(i + 1) * P, :], ob9[:, 0:DM]), ndma=1,
                 deps=[K9['v', i * DMC + DMC - 1]])

        for nm, outn in DEBUG_COPY:
            step("sync", lambda E, dma, nm=nm, outn=outn:
                 dma(dram[outn], dram[nm]), ndma=1)

        phase(None)
        cfin = dict(C)

        def fin(sy):
            if chain[0] is not None:
                sy.wait_ge(SEMS[chain[0][0]], chain[0][1])
            for nm in ("qc", "qs", "qg"):
                if cfin[nm] > 0:
                    sy.wait_ge(SEMS[nm], cfin[nm])
        block.sync(fin)

    return nc



def _prep_consts():
    c = {}
    c["id_f32"] = np.eye(P, dtype=np.float32)
    c["id_bf16"] = np.eye(P, dtype=np.float32)
    tri = (np.arange(P)[:, None] <= np.arange(P)[None, :]).astype(np.float32)
    c["tri_bf16"] = tri
    c["tri_scaled"] = tri * 0.125
    selrep = np.zeros((P, N * P), np.float32)
    selred = np.zeros((P, N * P), np.float32)
    for m in range(N):
        for j in range(P):
            selrep[8 * m + (j % 8), m * P + j] = 1.0
        for p_ in range(P):
            selred[p_, m * P + 8 * m + (p_ % 8)] = 1.0
    c["selrep"] = selrep
    c["selred"] = selred
    sB = np.zeros((32, P), np.float32)
    sC = np.zeros((32, P), np.float32)
    for j in range(P):
        sB[j // 8, j] = 1.0
        sC[16 + j // 8, j] = 1.0
    c["sB"] = sB
    c["sC"] = sC
    c["ones_c_bf"] = np.ones((P, 1), np.float32)
    c["ones_c_f32"] = np.ones((P, 1), np.float32)
    c["ones_r_bf"] = np.ones((1, P), np.float32)
    c["ones_r_f32"] = np.ones((1, P), np.float32)
    c["m128_r"] = np.full((1, P), -128.0, np.float32)
    return c


_BF16_NAMES = {"id_bf16", "tri_bf16", "tri_scaled", "selrep", "selred",
               "ones_c_bf", "ones_r_bf", "m128_r"}


def kernel(**inputs):
    import ml_dtypes
    inp = {k: np.asarray(v) for k, v in inputs.items()}

    x = np.asarray(inp["L0_attn_hidden_states"], np.float32)
    w_inT = np.ascontiguousarray(np.asarray(inp["L0_attn_in_proj_w"], np.float32).T)
    conv_w = np.asarray(inp["L0_attn_conv_w"], np.float32)
    conv_b = np.asarray(inp["L0_attn_conv_b"], np.float32)
    w_xpT = np.ascontiguousarray(np.asarray(inp["L0_attn_x_proj_w"], np.float32).T)
    w_dtT = np.ascontiguousarray(np.asarray(inp["L0_attn_dt_proj_w"], np.float32).T)
    dt_b = np.asarray(inp["L0_attn_dt_proj_b"], np.float32)
    A_log = np.asarray(inp["L0_attn_A_log"], np.float32)
    D_ = np.asarray(inp["L0_attn_D"], np.float32)
    w_outT = np.ascontiguousarray(np.asarray(inp["L0_attn_out_proj_w"], np.float32).T)

    consts = _prep_consts()
    rng = np.random.default_rng()
    magic_base = int(rng.integers(1 << 20, 1 << 30))
    magic = np.zeros((1, 40), np.int32)
    for k in range(5):
        magic[0, 8 * k:8 * k + 8] = magic_base + k
    magic_cmp = np.concatenate([np.arange(5, dtype=np.int32) + magic_base,
                                np.zeros(3, np.int32)])[None, :]

    def colsify(vec, nch):
        return np.ascontiguousarray(np.asarray(vec, np.float32).reshape(nch, P).T)

    nc = build_program()

    shared_f32 = {
        "w_inT": w_inT, "w_outT": w_outT,
        "moe_wg": np.asarray(inp["L0_ffn_Wg"], np.float32).reshape(E_EXP * DM, F_MOE),
        "moe_wu": np.asarray(inp["L0_ffn_Wu"], np.float32).reshape(E_EXP * DM, F_MOE),
        "moe_wd": np.asarray(inp["L0_ffn_Wd"], np.float32).reshape(E_EXP * F_MOE, DM),
        "moe_ga": np.asarray(inp["L0_ffn_Ga"], np.float32).reshape(NL * DM, R_LR),
        "moe_gb": np.asarray(inp["L0_ffn_Gb"], np.float32).reshape(NL * R_LR, F_MOE),
        "moe_ua": np.asarray(inp["L0_ffn_Ua"], np.float32).reshape(NL * DM, R_LR),
        "moe_ub": np.asarray(inp["L0_ffn_Ub"], np.float32).reshape(NL * R_LR, F_MOE),
        "moe_da": np.asarray(inp["L0_ffn_Da"], np.float32).reshape(NL * F_MOE, R_LR),
        "moe_db": np.asarray(inp["L0_ffn_Db"], np.float32).reshape(NL * R_LR, DM),
        "wr": np.asarray(inp["L0_ffn_Wr"], np.float32),
        "wq1": np.asarray(inp["L1_attn_Wq"], np.float32),
        "wk1": np.asarray(inp["L1_attn_Wk"], np.float32),
        "wv1": np.asarray(inp["L1_attn_Wv"], np.float32),
        "wg1": np.asarray(inp["L1_attn_Wg"], np.float32),
        "ws1": np.asarray(inp["L1_attn_Ws"], np.float32),
        "wo1": np.asarray(inp["L1_attn_Wo"], np.float32),
    }
    qg_T = np.ascontiguousarray(np.asarray(inp["L1_ffn_wq_gate"], np.int32).T)
    qu_T = np.ascontiguousarray(np.asarray(inp["L1_ffn_wq_up"], np.int32).T)
    qd_T = np.ascontiguousarray(np.asarray(inp["L1_ffn_wq_down"], np.int32).T)

    in_maps = []
    for c in range(8):
        b, role = c // 2, c % 2
        dlo = role * DIH
        m = dict(shared_f32)
        m["qg_T"] = qg_T
        m["qu_T"] = qu_T
        m["qd_T"] = qd_T
        m["x_own"] = np.ascontiguousarray(x[b, role * T:(role + 1) * T, :])
        cw = conv_w[dlo:dlo + DIH]
        m["conv_w_loc"] = np.ascontiguousarray(
            cw.reshape(DIHC, P, 4).transpose(1, 0, 2).reshape(P, DIHC * 4))
        m["conv_b_loc"] = colsify(conv_b[dlo:dlo + DIH], DIHC)
        m["w_xpT_loc"] = np.ascontiguousarray(w_xpT[dlo:dlo + DIH])
        m["w_dtT_loc"] = np.ascontiguousarray(w_dtT[:, dlo:dlo + DIH])
        m["dtb_loc"] = colsify(dt_b[dlo:dlo + DIH], DIHC)
        kk = np.arange(DIHC * N)
        pp = np.arange(P)
        m["alog_lanes"] = np.ascontiguousarray(
            A_log[dlo + 8 * kk[None, :] + (pp[:, None] % 8), pp[:, None] // 8]).astype(np.float32)
        m["d_loc"] = colsify(D_[dlo:dlo + DIH], DIHC)
        m["temp"] = np.asarray(inp["L0_ffn_temperature"], np.float32).reshape(1, 1)
        m["gnw"] = np.asarray(inp["L1_attn_g_norm_weight"], np.float32).reshape(P, 1)
        m["sg_cols"] = colsify(inp["L1_ffn_s_gate"], HIDC)
        m["su_cols"] = colsify(inp["L1_ffn_s_up"], HIDC)
        m["sd_cols"] = colsify(inp["L1_ffn_s_down"], DMC)
        for k_, v_ in consts.items():
            m[k_] = v_.astype(ml_dtypes.bfloat16) if k_ in _BF16_NAMES else v_
        m["role_in"] = np.array([[role]], np.int32)
        m["magic"] = magic
        m["magic_cmp"] = magic_cmp
        in_maps.append(m)

    res = run_bass_kernel_spmd(nc, in_maps, core_ids=list(range(8)))
    global LAST_RESULTS
    LAST_RESULTS = res
    out = np.zeros((4, TT, DM), np.float32)
    for c in range(8):
        b, role = c // 2, c % 2
        out[b, role * T:(role + 1) * T, :] = res.results[c]["y"]
    return out
